# version 01 (final)
# baseline (speedup 1.0000x reference)
import numpy as np
import ml_dtypes
import concourse.bass as bass
import concourse.mybir as mybir
from concourse.bass_utils import run_bass_kernel_spmd
from contextlib import ExitStack

F32 = mybir.dt.float32
BF16 = mybir.dt.bfloat16
I32 = mybir.dt.int32
U8 = mybir.dt.uint8
AF = mybir.ActivationFunctionType
ALU = mybir.AluOpType
AX = mybir.AxisListType

T = 2048
TT = 512
NTT = T // TT
D = 1024
DC = D // 128

LN_EPS = 1e-5
GELU_BYPASS = False
DETECT_RACES = False
L0A_STAGE = 99
L0A_TRUNC = 0
L1F_SKIP_LN2 = False


class Ctx:

    def __init__(self, nc, block, stack):
        self.nc = nc
        self.block = block
        self.stack = stack
        self.sem = {}
        self.cnt = {}
        names = ["pt", "pv", "ps", "pg", "di"] + [f"d{i}" for i in range(10)]
        for name in names:
            self.sem[name] = stack.enter_context(nc.semaphore(name))
            self.cnt[name] = 0
        self.dram = {}
        self.sb = {}
        self.vb_off = {}

    def inc(self, name, n=1):
        self.cnt[name] += n
        return self.cnt[name]

    def now(self, name):
        return self.cnt[name]

    def _sec(self, key):
        return {"t": self.block.tensor, "v": self.block.vector,
                "s": self.block.scalar, "g": self.block.gpsimd,
                "y": self.block.sync}[key]

    def on(self, engkey, fn, waits=(), inc=None, chain=None):
        sem = self.sem
        pkey = {"t": "pt", "v": "pv", "s": "ps", "g": "pg", "y": None}[engkey]
        if chain is None:
            chain = engkey in ("v", "s", "g")
        holder = {}

        def wrapped(e):
            for s_, v_ in waits:
                if v_ > 0:
                    e.wait_ge(sem[s_], v_)
            if chain:
                e = _Chained(self, e, pkey)
            last = fn(e)
            if chain:
                holder["val"] = e.prev
            elif inc is not None:
                assert last is not None, "fn must return last instruction"
                last.then_inc(sem[inc], 1)
                holder["val"] = self.inc(inc)
        self._sec(engkey)(wrapped)
        return holder.get("val")

    def dma(self, q, out, in_, waits=(), sem="d0"):
        semh = self.sem[sem]

        def f(e):
            for s_, v_ in waits:
                if v_ > 0:
                    e.wait_ge(self.sem[s_], v_)
            e.dma_start(out=out, in_=in_).then_inc(semh, 16)
        self._sec(q)(f)
        return self.inc(sem, 16)

    def selfsync(self, engkey):
        pkey = {"t": "pt", "v": "pv", "s": "ps", "g": "pg"}[engkey]
        val = self.inc(pkey)

        def f(e):
            e.wait_ge(self.sem[pkey], val)
        return pkey, val

    def join(self, engines=("pt", "pv", "ps", "pg")):
        tgt = [(e, self.cnt[e]) for e in engines]
        for key, pkey in (("t", "pt"), ("v", "pv"), ("s", "ps"), ("g", "pg"),
                          ("y", None)):
            def f(e, pkey=pkey):
                for k, v in tgt:
                    if k != pkey and v > 0:
                        e.wait_ge(self.sem[k], v)
            self._sec(key)(f)


class _Chained:
    _PASS = {"wait_ge", "engine_nop", "nop", "then_inc"}

    def __init__(self, ctx, eng, pkey):
        self._ctx, self._eng, self._pkey = ctx, eng, pkey
        self.prev = None

    def __getattr__(self, name):
        fn = getattr(self._eng, name)
        if name in self._PASS or name.startswith("reg") or not callable(fn):
            return fn
        ctx, eng, pkey = self._ctx, self._eng, self._pkey

        def call(*a, **k):
            assert name != "dma_start", "don't issue DMAs via chained engines"
            if self.prev is not None:
                eng.wait_ge(ctx.sem[pkey], self.prev)
            ins = fn(*a, **k)
            ins.then_inc(ctx.sem[pkey], 1)
            self.prev = ctx.inc(pkey)
            return ins
        return call


class PsumPool:

    def __init__(self, c, psum_ap, nbanks, bank_w=512, evac_sem="pv"):
        self.c, self.ap, self.n, self.w = c, psum_ap, nbanks, bank_w
        self.evac_sem = evac_sem
        self.issued = 0
        self.evacs = []

    def bank(self):
        i = self.issued % self.n
        return self.ap[:, i * self.w:(i + 1) * self.w]

    def war_wait(self):
        j = self.issued - self.n
        if j < 0:
            return None
        e = self.evacs[j]
        return e if isinstance(e, tuple) else (self.evac_sem, e)

    def push(self, evac_val):
        self.issued += 1
        self.evacs.append(evac_val)


def xh_rhs(c, kc, tt):
    return c.sb["XH"][:, kc * T + tt * TT: kc * T + (tt + 1) * TT]


def res_slice(c, oc, tt):
    return c.sb["RES"][:, oc * T + tt * TT: oc * T + (tt + 1) * TT]


def linear_T(c, pp, lhsT_fn, rhs_fn, evac_fn, kc_n, oc_n, ntt=NTT,
             pre_wait=None, extra_mm=None, evac_eng="v", loop_order="tt"):
    eng_prog = {"v": "pv", "s": "ps"}[evac_eng]
    evac_vals = [[None] * oc_n for _ in range(ntt)]
    if loop_order == "tt":
        jobs = [(tt, oc) for tt in range(ntt) for oc in range(oc_n)]
    else:
        jobs = [(tt, oc) for oc in range(oc_n) for tt in range(ntt)]
    for tt, oc in jobs:
        if True:
            ps = pp.bank()
            war = pp.war_wait()
            waits = list(pre_wait(oc, tt)) if pre_wait else []
            if war is not None:
                waits.append(war)

            def mm(t, oc=oc, tt=tt, ps=ps):
                last = None
                for kc in range(kc_n):
                    stop = (kc == kc_n - 1) and extra_mm is None
                    last = t.matmul(ps, lhsT_fn(kc, oc), rhs_fn(kc, tt),
                                    start=(kc == 0), stop=stop)
                if extra_mm is not None:
                    last = extra_mm(oc, tt, ps, t)
                return last
            mm_done = c.on("t", mm, waits=waits, inc="pt")

            def ev(e, oc=oc, tt=tt, ps=ps):
                return evac_fn(oc, tt, ps, e)
            val = c.on(evac_eng, ev, waits=[("pt", mm_done)], inc=eng_prog)
            evac_vals[tt][oc] = val
            pp.push(val)
    return evac_vals


def emit_ln_g(c, src_fn, dst_fn, sq_fn, srcb_fn, tok0, ntok, stat_off,
              w_off=None, b_off=None, eps=LN_EPS, pre_waits=(), f32_dst=False,
              f32_sums=False):
    VEC, PSA, PSB = c.sb["VEC"], c.sb["PSA"], c.sb["PSB"]
    onecol = c.sb["ONECOL"]
    sizes = [TT] * (ntok // TT)
    if ntok % TT:
        sizes.append(ntok % TT)
    offs = [sum(sizes[:i]) for i in range(len(sizes))]
    ntt = len(sizes)
    assert ntt <= 4

    def sl_of(tt):
        return slice(tok0 + offs[tt], tok0 + offs[tt] + sizes[tt])

    def prep(se):
        last = None
        for tt in range(ntt):
            for kc in range(DC):
                sl = sl_of(tt)
                last = se.activation(sq_fn(kc, sl), src_fn(kc, sl), AF.Square)
        return last
    sq = c.on("s", prep, waits=list(pre_waits))
    sumcol = c.sb["ONECOLF"] if f32_sums else c.sb["ONECOL"]

    cp = None
    if srcb_fn is not None:
        def cpf(v):
            last = None
            for tt in range(ntt):
                for kc in range(DC):
                    sl = sl_of(tt)
                    last = v.tensor_copy(srcb_fn(kc, sl), src_fn(kc, sl))
            return last
        cp = c.on("v", cpf, waits=list(pre_waits))
        sb_fn = srcb_fn
    else:
        sb_fn = src_fn

    waits = [("ps", sq)] + ([("pv", cp)] if cp else [])
    sums_done = []
    for tt in range(ntt):
        ps = PSA[:, tt * TT:tt * TT + sizes[tt]]

        def mm(t, tt=tt, ps=ps):
            sl = sl_of(tt)
            last = None
            for kc in range(DC):
                t.matmul(ps[0:1, :], sumcol, sb_fn(kc, sl),
                         start=(kc == 0), stop=(kc == DC - 1))
            for kc in range(DC):
                last = t.matmul(ps[32:33, :], sumcol, sq_fn(kc, sl),
                                start=(kc == 0), stop=(kc == DC - 1))
            return last
        sums_done.append(c.on("t", mm, waits=waits, inc="pt", chain=False))

    def stats(v):
        last = None
        for tt in range(ntt):
            pslc = PSA[0:1, tt * TT:tt * TT + sizes[tt]]
            pslc2 = PSA[32:33, tt * TT:tt * TT + sizes[tt]]
            nsl = VEC[0:1, stat_off + offs[tt]:stat_off + offs[tt] + sizes[tt]]
            rsl = VEC[0:1, stat_off + ntok + offs[tt]:stat_off + ntok + offs[tt] + sizes[tt]]
            v.tensor_scalar(nsl, pslc, -1.0 / D, None, ALU.mult)
            v.tensor_scalar(rsl, pslc2, 1.0 / D, None, ALU.mult)
            v.tensor_tensor(pslc, nsl, nsl, ALU.mult)
            v.tensor_tensor(rsl, rsl, pslc, ALU.subtract)
            last = v.tensor_scalar(rsl, rsl, eps, None, ALU.add)
        return last
    st = c.on("v", stats, waits=[("pt", sums_done[-1])])

    rstd_all = VEC[0:1, stat_off + ntok:stat_off + 2 * ntok]

    def rsf(s):
        s.activation(rstd_all, rstd_all, AF.Ln)
        return s.activation(rstd_all, rstd_all, AF.Exp, scale=-0.5)
    rs = c.on("s", rsf, waits=[("pv", st)])

    prev_norm = None
    for tt in range(ntt):
        sl = sl_of(tt)
        n = sizes[tt]
        so = slice(stat_off + offs[tt], stat_off + offs[tt] + n)
        so2 = slice(stat_off + ntok + offs[tt], stat_off + ntok + offs[tt] + n)
        waits2 = [("ps", rs)]
        if prev_norm is not None:
            waits2.append(("pv", prev_norm))

        def bc(t, so=so, so2=so2, n=n):
            t.matmul(PSB[:, 0:n], c.sb["ONESF"][0:1, :], VEC[0:1, so],
                     start=True, stop=True)
            return t.matmul(PSB[:, TT:TT + n], c.sb["ONESF"][0:1, :],
                            VEC[0:1, so2], start=True, stop=True)
        bc_done = c.on("t", bc, waits=waits2, inc="pt", chain=False)

        def norm(v, sl=sl, n=n):
            last = None
            for kc in range(DC):
                if f32_dst:
                    d = dst_fn(kc, sl)
                    v.tensor_tensor(d, src_fn(kc, sl), PSB[:, 0:n], ALU.add)
                    last = v.tensor_tensor(d, d, PSB[:, TT:TT + n], ALU.mult)
                    continue
                tmp = sq_fn(kc, sl)
                v.tensor_tensor(tmp, src_fn(kc, sl), PSB[:, 0:n], ALU.add)
                if w_off is None:
                    last = v.tensor_tensor(dst_fn(kc, sl), tmp,
                                           PSB[:, TT:TT + n], ALU.mult)
                else:
                    v.tensor_tensor(tmp, tmp, PSB[:, TT:TT + n], ALU.mult)
                    last = v.tensor_scalar(
                        dst_fn(kc, sl), tmp,
                        c.sb["VB"][:, w_off + kc:w_off + kc + 1],
                        c.sb["VB"][:, b_off + kc:b_off + kc + 1],
                        ALU.mult, ALU.add)
            return last
        prev_norm = c.on("v", norm, waits=[("pt", bc_done)])
    return prev_norm


def emit_ln(c):
    RES, XH, SB2 = c.sb["RES"], c.sb["XH"], c.sb["SB2"]
    last = None
    for hf in range(2):
        last = emit_ln_g(
            c,
            src_fn=lambda kc, sl: RES[:, kc * T + sl.start: kc * T + sl.stop],
            dst_fn=lambda kc, sl: XH[:, kc * T + sl.start: kc * T + sl.stop],
            sq_fn=lambda kc, sl: SB2[:, kc * T + sl.start: kc * T + sl.stop],
            srcb_fn=lambda kc, sl: XH[:, kc * T + sl.start: kc * T + sl.stop],
            tok0=hf * 1024, ntok=1024, stat_off=0)
    return last


def emit_l3f(c, scale):
    XH, SCR, SB2, SB3, VEC = (c.sb["XH"], c.sb["SCR"], c.sb["SB2"],
                              c.sb["SB3"], c.sb["VEC"])
    PSB = c.sb["PSB"]

    a_sb = SB2[:, 8192:8192 + DC * 16]
    b_sb = SB2[0:16, 8320:8320 + 1024]
    u_bf = SB2[0:16, 9344:9344 + T]
    kl = ("d0", c.dma("y", SCR[:, 0:DC * D].rearrange("p (kc o) -> p kc o", kc=DC),
               c.dram["L3_ffn_kernel_T"].rearrange("(kc p) o -> p kc o", p=128), sem="d0"))
    al = ("d1", c.dma("g", a_sb.rearrange("p (kc r) -> p kc r", kc=DC),
               c.dram["L3_ffn_a_T"].rearrange("(kc p) r -> p kc r", p=128), sem="d1"))
    bl = ("d2", c.dma("g", b_sb, c.dram["L3_ffn_b_T"], sem="d2"))

    kf = SCR[:, 0:DC * D]
    wq = SB2[:, 0:DC * D]
    amax = VEC[0:128, 8:9]

    r1 = c.on("v", lambda v: v.tensor_reduce(amax, kf, AX.X, ALU.max,
                                             apply_absolute_value=True),
              waits=[kl], inc="pv")
    r2 = c.on("t", lambda t: t.transpose(PSB[0:1, 0:128], amax, c.sb["IDENTF"]),
              waits=[("pv", r1)], inc="pt")

    r3a = c.on("v", lambda v: v.tensor_reduce(VEC[0:1, 4:5], PSB[0:1, 0:128],
                                              AX.X, ALU.max),
               waits=[("pt", r2)], inc="pv")

    def q3(v):
        v.wait_ge(c.sem["pv"], r3a)
        v.tensor_scalar(VEC[0:1, 5:6], VEC[0:1, 4:5], 1.0 / 7.0, None, ALU.mult)
        return v.reciprocal(VEC[0:1, 6:7], VEC[0:1, 5:6])
    r3 = c.on("v", q3, inc="pv")

    def q4(t):
        return t.matmul(PSB[:, 128:256], c.sb["ONESF"][0:1, :], VEC[0:1, 0:128],
                        start=True, stop=True)
    r4 = c.on("t", q4, waits=[("pv", r3)], inc="pt")

    scl = VEC[0:128, 9:10]
    rscale = VEC[0:128, 10:11]

    def q5(v):
        v.tensor_copy(scl, PSB[:, 133:134])
        v.tensor_copy(rscale, PSB[:, 134:135])
        v.tensor_scalar(kf, kf, rscale, None, ALU.mult)
        v.tensor_scalar(kf, kf, 12582912.0, 12582912.0, ALU.add, ALU.subtract)
        v.tensor_scalar(kf, kf, 7.0, -8.0, ALU.min, ALU.max)
        return v.tensor_scalar(wq, kf, scl, None, ALU.mult)
    quant_done = c.on("v", q5, waits=[("pt", r4)], inc="pv")

    pp = PsumPool(c, c.sb["PSA"], 4, evac_sem="pv")
    for tt in range(NTT):
        ps = pp.bank()
        war = pp.war_wait()
        waits = [al, ("pv", quant_done)]
        if war is not None:
            waits.append(("pv", war))

        def mm(t, tt=tt, ps=ps):
            last = None
            for kc in range(DC):
                last = t.matmul(ps[0:16, :], a_sb[:, kc * 16:(kc + 1) * 16],
                                xh_rhs(c, kc, tt), start=(kc == 0),
                                stop=(kc == DC - 1))
            return last
        mm_done = c.on("t", mm, waits=waits, inc="pt")
        pp.push(c.on("v", lambda v, tt=tt, ps=ps: v.tensor_copy(
            u_bf[:, tt * TT:(tt + 1) * TT], ps[0:16, :]),
            waits=[("pt", mm_done)], inc="pv"))
    u_done = c.now("pv")

    boff = c.vb_off["L3_ffn_bias"]
    sb_vec = VEC[0:128, 16:16 + DC]
    sbv = c.on("v", lambda v: v.tensor_scalar(
        sb_vec, c.sb["VB"][:, boff:boff + DC], scale, None, ALU.mult), inc="pv")

    def lhsT(kc, oc):
        return wq[:, kc * D + oc * 128: kc * D + (oc + 1) * 128]

    def extra(oc, tt, ps, t):
        return t.matmul(ps, b_sb[:, oc * 128:(oc + 1) * 128],
                        u_bf[:, tt * TT:(tt + 1) * TT], start=False, stop=True)

    def evac(oc, tt, ps, e):
        dst = res_slice(c, oc, tt)
        e.scalar_tensor_tensor(dst, ps, scale, dst, ALU.mult, ALU.add)
        return e.tensor_scalar(dst, dst, sb_vec[:, oc:oc + 1], None, ALU.add)

    def pw(oc, tt):
        return [("pv", quant_done), ("pv", u_done), bl, ("pv", sbv)]

    linear_T(c, pp, lhsT, lambda kc, tt: xh_rhs(c, kc, tt), evac, DC, DC,
             pre_wait=pw, extra_mm=extra, evac_eng="v")


def emit_l0f(c, scale):
    XH, SB2, SB3, VEC, WT, SCR = (c.sb["XH"], c.sb["SB2"], c.sb["SB3"],
                                  c.sb["VEC"], c.sb["WT"], c.sb["SCR"])
    PSB, PSA = c.sb["PSB"], c.sb["PSA"]
    E = 8
    nm = "L0_ffn"

    gl = ("d0", c.dma("g", SB2[0:128, 0:E * DC].rearrange("p (kc e) -> p kc e", kc=DC),
                      c.dram["L0_ffn_gate_w"].rearrange("(kc p) e -> p kc e", p=128),
                      sem="d0"))
    gbl = ("d1", c.dma("y", VEC[0:8, 2048:2049], c.dram["L0_gate_b8"], sem="d1"))
    esel_bf = c.sb["AUX"].bitcast(BF16)[0:8, 0:1024]
    esl = ("d2", c.dma("g", esel_bf, c.dram["esel"], sem="d2"))
    z = VEC[0:8, 0:T]
    negmax = VEC[0:8, 2049:2050]
    rsum = VEC[0:8, 2050:2051]
    zsum = VEC[0:8, 2051:2052]

    def gmm(t):
        last = None
        for tt2 in range(NTT):
            for kc in range(DC):
                last = t.matmul(PSA[0:8, tt2 * TT:(tt2 + 1) * TT],
                                SB2[:, kc * E:(kc + 1) * E],
                                xh_rhs(c, kc, tt2), start=(kc == 0),
                                stop=(kc == DC - 1))
        return last
    gdone = c.on("t", gmm, waits=[gl], inc="pt", chain=False)

    def gevac(v):
        v.tensor_scalar(z, PSA[0:8, 0:T], VEC[0:8, 2048:2049], None, ALU.add)
        v.tensor_reduce(negmax, z, AX.X, ALU.max, negate=True)
    gz = c.on("v", gevac, waits=[("pt", gdone), gbl])

    gexp = c.on("s", lambda s: s.activation(z, z, AF.Exp, bias=negmax,
                                            accum_out=zsum),
                waits=[("pv", gz)])

    def gnf(v):
        v.reciprocal(rsum, zsum)
        return v.tensor_scalar(z, z, rsum, None, ALU.mult)
    gn = c.on("v", gnf, waits=[("ps", gexp)])

    PT = SCR[:, 0:128]
    MX1 = SCR[:, 128:144]
    MX2 = SCR[:, 144:160]
    SCR8 = SCR[:, 160:168]
    IND = SCR[:, 168:296]
    WTK = SCR[:, 296:424]

    def ptr(t):
        last = None
        for i in range(16):
            last = t.transpose(PSB[:, i * 8:(i + 1) * 8], z[:, i * 128:(i + 1) * 128],
                               c.sb["IDENTF"][0:8, 0:8])
        return last
    ptd = c.on("t", ptr, waits=[("pv", gn)], inc="pt", chain=False)

    def top2(v):
        v.tensor_copy(PT, PSB[:, 0:128])
        v.tensor_reduce(MX1, PT.rearrange("p (i e) -> p i e", e=8), AX.X, ALU.max)
        ge1 = SCR[:, 424:552]
        v.tensor_tensor(ge1.rearrange("p (i e) -> p i e", e=8),
                        PT.rearrange("p (i e) -> p i e", e=8),
                        MX1.rearrange("p (i o) -> p i o", o=1).broadcast_to((128, 16, 8)),
                        ALU.is_ge)
        v.scalar_tensor_tensor(ge1, ge1, -10.0, PT, ALU.mult, ALU.add)
        v.tensor_reduce(MX2, ge1.rearrange("p (i e) -> p i e", e=8), AX.X, ALU.max)
        v.tensor_tensor(IND.rearrange("p (i e) -> p i e", e=8),
                        PT.rearrange("p (i e) -> p i e", e=8),
                        MX2.rearrange("p (i o) -> p i o", o=1).broadcast_to((128, 16, 8)),
                        ALU.is_ge)
        return v.tensor_tensor(WTK, PT, IND, ALU.mult)
    t2d = c.on("v", top2, waits=[("pt", ptd)])

    INDT = VEC.bitcast(BF16)[0:8, 0:T]
    WTKT = VEC.bitcast(BF16)[0:8, 2048:2048 + T]
    tb_prev = t2d
    for i in range(16):
        def tb(t, i=i):
            t.transpose(PSB[0:8, 128:256], IND[:, i * 8:(i + 1) * 8], c.sb["IDENTF"])
            return t.transpose(PSB[0:8, 256:384], WTK[:, i * 8:(i + 1) * 8],
                               c.sb["IDENTF"])
        tbd = c.on("t", tb, waits=[("pv", tb_prev)], inc="pt", chain=False)

        def tbe(v, i=i):
            v.tensor_copy(INDT[:, i * 128:(i + 1) * 128], PSB[0:8, 128:256])
            return v.tensor_copy(WTKT[:, i * 128:(i + 1) * 128], PSB[0:8, 256:384])
        tb_prev = c.on("v", tbe, waits=[("pt", tbd)])
    topk_done = tb_prev

    pp = PsumPool(c, PSA, 4, evac_sem="ps")
    v_sb = c.sb["VS"]
    wbc = c.sb["TRASH2"]
    trash = c.sb["TRASH"]
    esel = esel_bf

    piece_war = []
    ttr_done_prev = None
    prev_layer_evacs = None
    for e in range(E):
        wb_prev = None
        for tt2 in range(NTT):
            def wb(t, e=e, tt2=tt2):
                return t.matmul(PSB[:, 512:1024], esel[0:8, e * 128:(e + 1) * 128],
                                WTKT[:, tt2 * TT:(tt2 + 1) * TT],
                                start=True, stop=True)
            waits = [("pv", topk_done), esl]
            if wb_prev is not None:
                waits.append(("pv", wb_prev))
            if ttr_done_prev is not None:
                waits.append(("pv", ttr_done_prev))
            wbd = c.on("t", wb, waits=waits, inc="pt", chain=False)
            wb_prev = c.on("v", lambda v, tt2=tt2: v.tensor_copy(
                wbc[:, tt2 * TT:(tt2 + 1) * TT], PSB[:, 512:1024]),
                waits=[("pt", wbd)])
        wbe = wb_prev

        for layer in (1, 2, 3):
            wkey = f"{nm}_w{layer}"
            bname = f"{nm}_b{layer}_{e}"
            src_buf = {1: XH, 2: SB3, 3: SB2}[layer]
            dst_buf = {1: SB3, 2: SB2, 3: SB3}[layer]
            ptargets = []
            for piece in range(2):
                k = len(piece_war) + len(ptargets)
                war = piece_war[k - 2] if k >= 2 else None
                tgt = c.dma("g", WT[:, piece * 4096:(piece + 1) * 4096]
                            .rearrange("p (kc o) -> p kc o", kc=DC),
                            c.dram[wkey][e].rearrange("(kc p) o -> p kc o", p=128)
                            [:, :, piece * 512:(piece + 1) * 512],
                            waits=[("pt", war)] if war is not None else [],
                            sem=f"d{3 + piece}")
                ptargets.append((f"d{3 + piece}", tgt))

            def lhsT(kc, oc):
                slot = oc // 4
                return WT[:, slot * 4096 + kc * 512 + (oc % 4) * 128:
                          slot * 4096 + kc * 512 + (oc % 4) * 128 + 128]

            def rhs(kc, tt2, src_buf=src_buf):
                if src_buf is XH:
                    return xh_rhs(c, kc, tt2)
                return src_buf[:, kc * T + tt2 * TT: kc * T + (tt2 + 1) * TT]

            boff = c.vb_off[bname]

            def evac(oc, tt2, ps, s, dst_buf=dst_buf, boff=boff):
                return s.activation(dst_buf[:, oc * T + tt2 * TT: oc * T + (tt2 + 1) * TT],
                                    ps, AF.Relu, bias=c.sb["VB"][:, boff + oc:boff + oc + 1])

            deps_prev = prev_layer_evacs if layer > 1 else None
            extra_dep = ttr_done_prev if (layer == 1 and ttr_done_prev) else None

            def pw(oc, tt2, ptargets=ptargets, deps_prev=deps_prev,
                   extra_dep=extra_dep):
                w = [ptargets[oc // 4], ("pv", topk_done)]
                if deps_prev is not None:
                    w.append(("ps", max(x for row in deps_prev for x in row)))
                if extra_dep is not None:
                    w.append(("pv", extra_dep))
                return w

            prev_layer_evacs = linear_T(c, pp, lhsT, rhs, evac, DC, DC,
                                        pre_wait=pw, evac_eng="s",
                                        loop_order="oc")
            piece_war.extend([c.now("pt"), c.now("pt")])

        y_done = max(x for row in prev_layer_evacs for x in row)

        def vred(v, e=e):
            last = None
            va = v_sb.tensor.ap()[:, 64:65] if False else c.sb["VS"][:, 64:65]
            for oc in range(DC):
                v.tensor_tensor(trash[:, 0:1024], SB3[:, oc * T:oc * T + 1024],
                                wbc[:, 0:1024], ALU.mult)
                v.tensor_reduce(va, trash[:, 0:1024], AX.X, ALU.add)
                v.tensor_tensor(trash[:, 0:1024], SB3[:, oc * T + 1024:(oc + 1) * T],
                                wbc[:, 1024:2048], ALU.mult)
                v.tensor_reduce(v_sb[:, e * 8 + oc:e * 8 + oc + 1],
                                trash[:, 0:1024], AX.X, ALU.add)
                last = v.tensor_tensor(v_sb[:, e * 8 + oc:e * 8 + oc + 1],
                                       v_sb[:, e * 8 + oc:e * 8 + oc + 1], va,
                                       ALU.add)
            return last
        ttr_done_prev = c.on("v", vred, waits=[("ps", y_done), ("pv", wbe)])

    vtp = c.on("t", lambda t: t.transpose(PSB[0:64, 128:256], v_sb[:, 0:64],
                                          c.sb["IDENTF"]),
               waits=[("pv", ttr_done_prev)], inc="pt", chain=False)
    VTF = VEC[0:64, 1792:1920]
    vte = c.on("v", lambda v: v.tensor_copy(VTF, PSB[0:64, 128:256]),
               waits=[("pt", vtp)])
    vdma = c.dma("g", c.dram["v_loc"], VTF, waits=[("pv", vte)], sem="d5")

    def coll(g):
        g.wait_ge(c.sem["d5"], vdma)
        g.collective_compute(
            "AllReduce", ALU.add, replica_groups=[list(range(8))],
            ins=[c.dram["v_loc"].opt()], outs=[c.dram["v_red"].opt()],
        ).then_inc(c.sem["d6"])
    c.block.gpsimd(coll)
    cred = c.inc("d6", 1)

    v_red_sb = VEC.bitcast(BF16)[0:8, T:T + 1024]
    vrl = c.dma("g", v_red_sb, c.dram["v_red"].rearrange("(e o) p -> e (o p)", e=8),
                waits=[("d6", cred)], sem="d7")

    pp2 = PsumPool(c, PSA, 4, evac_sem="pv")

    def olhsT(kc, oc):
        return v_red_sb[:, oc * 128:(oc + 1) * 128]

    def orhs(kc, tt2):
        return INDT[:, tt2 * TT:(tt2 + 1) * TT]

    def oevac(oc, tt2, ps, v):
        dst = res_slice(c, oc, tt2)
        return v.scalar_tensor_tensor(dst, ps, scale, dst, ALU.mult, ALU.add)

    def opw(oc, tt2):
        return [("d7", vrl), ("pv", ttr_done_prev)]

    linear_T(c, pp2, olhsT, orhs, oevac, 1, DC, pre_wait=opw, evac_eng="v")



def emit_l0a(c, scale):
    XH, SB2, SB3, VEC, WT = (c.sb["XH"], c.sb["SB2"], c.sb["SB3"],
                             c.sb["VEC"], c.sb["WT"])
    PSA, PSB = c.sb["PSA"], c.sb["PSB"]
    QS, KS, VSTG, PBUF = c.sb["QS"], c.sb["KS"], c.sb["VSTG"], c.sb["PBUF"]
    VT = c.sb["TRASH"]
    nm = "L0_attn"
    HALF = 1024
    NQT = 8
    KW = 1152
    XIN = SB2[:, 8192:16384]

    unit = 0
    wslot_war = []
    ctx_done_prev = None
    ffn_done = None

    for hf in range(2):
        base = hf * HALF
        lt0 = base - (128 if hf == 1 else 0)
        hw = HALF + (128 if hf == 1 else 0)

        ln1 = None
        nsub = 2 if hw > 1088 else 1
        for sub in range(nsub):
            st0 = lt0 + sub * (hw // 2)
            sw = hw // 2 if nsub == 2 else hw
            pw = [("pv", ffn_done)] if (ffn_done and sub == 0) else (
                [("pv", ln1)] if sub else ())
            ln1 = emit_ln_g(c,
                            src_fn=lambda kc, sl: XH[:, kc * T + sl.start: kc * T + sl.stop],
                            dst_fn=lambda kc, sl, hw=hw, lt0=lt0: SB2[:, kc * hw + (sl.start - lt0): kc * hw + (sl.stop - lt0)],
                            sq_fn=lambda kc, sl, hw=hw, lt0=lt0: SB3[:, kc * hw + (sl.start - lt0): kc * hw + (sl.stop - lt0)],
                            srcb_fn=None, tok0=st0, ntok=sw, stat_off=0,
                            w_off=c.vb_off[nm + "_ln1_w"], b_off=c.vb_off[nm + "_ln1_b"],
                            pre_waits=pw)

        ctx_evac_last = [None] * 8
        for hp in range(8 if L0A_STAGE >= 2 else 0):
            slot = unit % 2
            wof = slot * 4096
            war = wslot_war[unit - 2] if unit >= 2 else None
            wwaits = [("pt", war)] if war is not None else []
            sem = f"d{3 + slot}"
            c.dma("g", WT[:, wof + 0:wof + 1024].rearrange("p (kc m) -> p kc m", kc=DC),
                  c.dram[nm + "_Wq"].rearrange("(kc p) m -> p kc m", p=128)[:, :, hp * 128:(hp + 1) * 128],
                  waits=wwaits, sem=sem)
            c.dma("g", WT[:, wof + 1024:wof + 2048].rearrange("p (kc m) -> p kc m", kc=DC),
                  c.dram[nm + "_Wk"].rearrange("(kc p) m -> p kc m", p=128)[:, :, hp * 128:(hp + 1) * 128],
                  sem=sem)
            tv = c.dma("g", WT[:, wof + 2048:wof + 3072].rearrange("p (kc m) -> p kc m", kc=DC),
                       c.dram[nm + "_Wv"].rearrange("(kc p) m -> p kc m", p=128)[:, :, hp * 128:(hp + 1) * 128],
                       sem=sem)
            wsem = (sem, tv)

            qk_jobs = [("q", 0, TT, 0), ("q", TT, TT, TT),
                       ("k", 0, TT, 128), ("k", TT, TT, 128 + TT)]
            if hf == 1:
                qk_jobs.append(("k", -128, 128, 0))
            qkv_done = []
            j = 0
            for proj, t0, ntokj, dcol in qk_jobs:
                ps = PSA[:, (j % 2) * TT:(j % 2) * TT + ntokj]
                waits = [wsem, ("pv", ln1)]
                if ctx_done_prev is not None:
                    waits.append(("pv", ctx_done_prev))
                if j >= 2:
                    waits.append(("ps", qkv_done[j - 2]))
                wo0 = 0 if proj == "q" else 1024
                dstbuf = QS if proj == "q" else KS
                dw = 1024 if proj == "q" else KW
                bcol = c.vb_off[nm + ("_bq" if proj == "q" else "_bk")] + hp

                def mmq(t, wo0=wo0, t0=t0, ntokj=ntokj, ps=ps, hw=hw, hf=hf):
                    last = None
                    off = 128 if hf == 1 else 0
                    for kc in range(DC):
                        col = kc * hw + off + t0
                        last = t.matmul(ps, WT[:, wof + wo0 + kc * 128: wof + wo0 + (kc + 1) * 128],
                                        SB2[:, col:col + ntokj],
                                        start=(kc == 0), stop=(kc == DC - 1))
                    return last
                mmd = c.on("t", mmq, waits=waits, inc="pt", chain=False)

                def ev(s, ps=ps, dstbuf=dstbuf, bcol=bcol, dcol=dcol,
                       ntokj=ntokj, dw=dw):
                    s.activation(dstbuf[0:64, dcol:dcol + ntokj],
                                 ps[0:64, 0:ntokj], AF.Identity,
                                 bias=c.sb["VB"][0:64, bcol:bcol + 1])
                    return s.activation(dstbuf[0:64, dw + dcol:dw + dcol + ntokj],
                                        ps[64:128, 0:ntokj], AF.Identity,
                                        bias=c.sb["VB"][64:128, bcol:bcol + 1])
                qkv_done.append(c.on("s", ev, waits=[("pt", mmd)]))
                j += 1

            v_jobs = [(0, TT, 1), (TT, TT, 5)]
            if hf == 1:
                v_jobs.append((-128, 128, 0))
            vt_evacs = []
            bcolv = c.vb_off[nm + "_bv"] + hp
            for t0, ntokj, ci0 in v_jobs:
                ps = PSA[:, (j % 2) * TT:(j % 2) * TT + ntokj]
                waits = [wsem, ("pv", ln1)]
                if ctx_done_prev is not None:
                    waits.append(("pv", ctx_done_prev))
                if j >= 2:
                    waits.append(("ps", qkv_done[j - 2]))
                if vt_evacs:
                    waits.append(("pv", vt_evacs[-1]))

                def mmv(t, t0=t0, ntokj=ntokj, ps=ps, hw=hw, hf=hf):
                    last = None
                    off = 128 if hf == 1 else 0
                    for kc in range(DC):
                        col = kc * hw + off + t0
                        last = t.matmul(ps, WT[:, wof + 2048 + kc * 128: wof + 2048 + (kc + 1) * 128],
                                        SB2[:, col:col + ntokj],
                                        start=(kc == 0), stop=(kc == DC - 1))
                    return last
                mmd = c.on("t", mmv, waits=waits, inc="pt", chain=False)

                def vev(s, ps=ps, ntokj=ntokj):
                    s.activation(VSTG[0:64, 0:ntokj], ps[0:64, 0:ntokj],
                                 AF.Identity, bias=c.sb["VB"][0:64, bcolv:bcolv + 1])
                    return s.activation(VSTG[0:64, 512:512 + ntokj],
                                        ps[64:128, 0:ntokj], AF.Identity,
                                        bias=c.sb["VB"][64:128, bcolv:bcolv + 1])
                vevd = c.on("s", vev, waits=[("pt", mmd)])
                qkv_done.append(vevd)
                j += 1
                nch = ntokj // 128
                for h2 in range(2):
                    for cc in range(nch):
                        ci = ci0 + cc
                        pcol = ((h2 * nch + cc) % 2) * 1024

                        def vtr1(t, h2=h2, cc=cc, pcol=pcol):
                            return t.transpose(
                                c.sb["PST"][:, pcol:pcol + 64],
                                VSTG[0:64, h2 * 512 + cc * 128: h2 * 512 + (cc + 1) * 128],
                                c.sb["IDENTB"][0:64, 0:64])
                        w2_ = [("ps", vevd)]
                        if len(vt_evacs) >= 2:
                            w2_.append(("pv", vt_evacs[-2]))
                        vtd = c.on("t", vtr1, waits=w2_, inc="pt", chain=False)
                        vt_evacs.append(c.on(
                            "v", lambda v, h2=h2, ci=ci, pcol=pcol:
                            v.tensor_copy(VT[:, h2 * 576 + ci * 64: h2 * 576 + (ci + 1) * 64],
                                          c.sb["PST"][:, pcol:pcol + 64]),
                            waits=[("pt", vtd)]))
            vt_all = vt_evacs[-1]
            qkv_all = qkv_done[-1]

            TMPF = VEC[:, 1792:2048]
            RCP = VEC[0:1, 2048:2176]
            PTILDE = PBUF[:, 0:256]
            ctx_evacs = []
            rcp_prev = None
            for h2 in range(2 if L0A_STAGE >= 3 else 0):
                for qt in range(NQT):
                    idx = h2 * NQT + qt
                    gqt = hf * NQT + qt
                    has_left = gqt > 0
                    sps = PSA[:, (qt % 2) * TT:(qt % 2) * TT + 256]
                    cps = PSA[:, 1024 + (qt % 2) * TT:1024 + (qt % 2) * TT + 128]
                    qsl = QS[0:64, h2 * 1024 + qt * 128: h2 * 1024 + (qt + 1) * 128]

                    def smm(t, h2=h2, qt=qt, sps=sps, qsl=qsl, has_left=has_left):
                        if has_left:
                            kleft = KS[0:64, h2 * KW + qt * 128: h2 * KW + (qt + 1) * 128]
                            t.matmul(sps[:, 0:128], kleft, qsl, start=True, stop=True)
                        kdiag = KS[0:64, h2 * KW + 128 + qt * 128: h2 * KW + 128 + (qt + 1) * 128]
                        return t.matmul(sps[:, 128:256], kdiag, qsl,
                                        start=True, stop=True)
                    sw = [("pv", vt_all), ("ps", qkv_all)]
                    if idx >= 2:
                        sw.append(("pv", ctx_evacs[idx - 2]))
                    smd = c.on("t", smm, waits=sw, inc="pt", chain=False)

                    def mask(v, sps=sps, has_left=has_left):
                        if has_left:
                            v.tensor_tensor(TMPF[:, 0:128], sps[:, 0:128],
                                            c.sb["MASKT"][:, 0:128], ALU.add)
                        return v.tensor_tensor(TMPF[:, 128:256], sps[:, 128:256],
                                               c.sb["MASKT"][:, 128:256], ALU.add)
                    mkd = c.on("v", mask, waits=[("pt", smd)])

                    def pexp(s, has_left=has_left):
                        if has_left:
                            s.activation(PTILDE[:, 0:128], TMPF[:, 0:128], AF.Exp,
                                         scale=0.125)
                        return s.activation(PTILDE[:, 128:256], TMPF[:, 128:256],
                                            AF.Exp, scale=0.125)
                    expd = c.on("s", pexp, waits=[("pv", mkd)])

                    def summ(t, has_left=has_left):
                        if has_left:
                            t.matmul(PSB[0:1, 0:128], c.sb["ONECOL"], PTILDE[:, 0:128],
                                     start=True, stop=False)
                        return t.matmul(PSB[0:1, 0:128], c.sb["ONECOL"],
                                        PTILDE[:, 128:256],
                                        start=(not has_left), stop=True)
                    sumw = [("ps", expd)]
                    if rcp_prev is not None:
                        sumw.append(("pv", rcp_prev))
                    sumd = c.on("t", summ, waits=sumw, inc="pt", chain=False)

                    rcp_prev = c.on("v", lambda v: v.reciprocal(RCP, PSB[0:1, 0:128]),
                                    waits=[("pt", sumd)])

                    def bcst(t):
                        return t.matmul(PSB[0:64, 512:640], c.sb["ONESF"][0:1, 0:64],
                                        RCP, start=True, stop=True)
                    bcw = [("pv", rcp_prev)]
                    if ctx_evacs:
                        bcw.append(("pv", ctx_evacs[-1]))
                    bcd = c.on("t", bcst, waits=bcw, inc="pt", chain=False)

                    def cmm(t, h2=h2, qt=qt, cps=cps, has_left=has_left):
                        if has_left:
                            vleft = VT[:, h2 * 576 + qt * 64: h2 * 576 + (qt + 1) * 64]
                            t.matmul(cps[0:64, :], vleft, PTILDE[:, 0:128],
                                     start=True, stop=False)
                        vdiag = VT[:, h2 * 576 + (qt + 1) * 64: h2 * 576 + (qt + 2) * 64]
                        return t.matmul(cps[0:64, :], vdiag, PTILDE[:, 128:256],
                                        start=(not has_left), stop=True)
                    cmd = c.on("t", cmm, waits=[("ps", expd)], inc="pt", chain=False)

                    def cev(v, h2=h2, qt=qt, cps=cps, hp=hp):
                        CT = VEC[0:64, 1792:1920]
                        v.tensor_copy(CT, cps[0:64, :])
                        v.tensor_tensor(CT, CT, PSB[0:64, 512:640], ALU.mult)
                        return v.tensor_copy(
                            SB3[h2 * 64:h2 * 64 + 64,
                                hp * 1024 + qt * 128: hp * 1024 + (qt + 1) * 128], CT)
                    ctx_evacs.append(c.on("v", cev, waits=[("pt", cmd), ("pt", bcd)]))
            ctx_done_prev = ctx_evacs[-1] if ctx_evacs else (qkv_done[-1] if False else ctx_done_prev)
            if ctx_evacs:
                pass
            else:
                ctx_done_prev = None
            ctx_evac_last[hp] = ctx_done_prev
            wslot_war.append(c.now("pt"))
            unit += 1

        if L0A_STAGE < 4:
            c.join()
            continue
        attn_all = ctx_done_prev
        wo_l = c.dma("g", WT[:, 0:8192].rearrange("p (kc m) -> p kc m", kc=DC),
                     c.dram[nm + "_Wo"].rearrange("(kc p) m -> p kc m", p=128),
                     waits=[("pt", wslot_war[-1])], sem="d3")
        pp = PsumPool(c, PSA, 4)
        xin_done = []
        for jj in range(16):
            oc, tt2 = jj % DC, jj // DC
            ps = pp.bank()
            war2 = pp.war_wait()
            w = [("d3", wo_l), ("pv", attn_all)]
            if war2 is not None:
                w.append(war2)

            def wmm(t, oc=oc, tt2=tt2, ps=ps):
                last = None
                for kc in range(DC):
                    last = t.matmul(ps, WT[:, kc * 1024 + oc * 128: kc * 1024 + (oc + 1) * 128],
                                    SB3[:, kc * 1024 + tt2 * TT: kc * 1024 + (tt2 + 1) * TT],
                                    start=(kc == 0), stop=(kc == DC - 1))
                return last
            wod = c.on("t", wmm, waits=w, inc="pt", chain=False)

            def wev(v, oc=oc, tt2=tt2, ps=ps):
                dsl = XIN[:, oc * 1024 + tt2 * TT: oc * 1024 + (tt2 + 1) * TT]
                sl_a = slice(base + tt2 * TT, base + (tt2 + 1) * TT)
                v.tensor_tensor(dsl, ps,
                                XH[:, oc * T + sl_a.start: oc * T + sl_a.stop],
                                ALU.add)
                bcol = c.vb_off[nm + "_bo"] + oc
                return v.tensor_scalar(dsl, dsl, c.sb["VB"][:, bcol:bcol + 1],
                                       None, ALU.add)
            val = c.on("v", wev, waits=[("pt", wod)])
            xin_done.append(val)
            pp.push(("pv", val))
        xin_all = xin_done[-1]

        for tt2 in range(2 if L0A_STAGE >= 5 else 0):
            sl_a = slice(base + tt2 * TT, base + (tt2 + 1) * TT)

            ln2 = emit_ln_g(
                c,
                src_fn=lambda kc, sl, tt2=tt2: XIN[:, kc * 1024 + tt2 * TT: kc * 1024 + (tt2 + 1) * TT],
                dst_fn=lambda kc, sl: SB2[:, kc * TT:(kc + 1) * TT],
                sq_fn=lambda kc, sl: SB2[:, 4096 + kc * TT:4096 + (kc + 1) * TT],
                srcb_fn=None, tok0=base + tt2 * TT, ntok=TT, stat_off=0,
                w_off=c.vb_off[nm + "_ln2_w"], b_off=c.vb_off[nm + "_ln2_b"],
                pre_waits=[("pv", xin_all)])

            piece_t = []
            piece_mm = []
            gchain = []
            for piece in range(8 if L0A_STAGE >= 5.5 else 0):
                pslot = piece % 2
                if piece >= 2:
                    lw = [("pt", piece_mm[piece - 2])]
                else:
                    lw = [("pt", c.now("pt"))]
                tgt = c.dma("g", WT[:, pslot * 4096:(pslot + 1) * 4096]
                            .rearrange("p (kc m) -> p kc m", kc=DC),
                            c.dram[nm + "_W1"].rearrange("(kc p) m -> p kc m", p=128)
                            [:, :, piece * 512:(piece + 1) * 512],
                            waits=lw, sem=f"d{3 + pslot}")
                piece_t.append((f"d{3 + pslot}", tgt))
                for sub in range(4):
                    hc = piece * 4 + sub
                    jidx = len(gchain)
                    ps = pp.bank()
                    war2 = pp.war_wait()
                    w = [piece_t[piece], ("pv", ln2)]
                    if war2 is not None:
                        w.append(war2)

                    def hmm(t, sub=sub, pslot=pslot, ps=ps):
                        last = None
                        for kc in range(DC):
                            last = t.matmul(
                                ps, WT[:, pslot * 4096 + kc * 512 + sub * 128:
                                       pslot * 4096 + kc * 512 + sub * 128 + 128],
                                SB2[:, kc * TT:(kc + 1) * TT],
                                start=(kc == 0), stop=(kc == DC - 1))
                        return last
                    hmd = c.on("t", hmm, waits=w, inc="pt", chain=False)

                    pre = c.sb["TRASH"][:, (jidx % 2) * 512:(jidx % 2) * 512 + 512]
                    gt = c.sb["TRASH2"][:, (jidx % 2) * 512:(jidx % 2) * 512 + 512]
                    slot_w = [("ps" if L0A_STAGE < 5.8 else "pv",
                               gchain[jidx - 2])] if jidx >= 2 else []
                    if L0A_STAGE < 5.8:
                        def hevp(s, ps=ps, hc=hc):
                            bcol = c.vb_off[nm + "_b1"] + hc
                            return s.activation(SB3[:, hc * TT:(hc + 1) * TT], ps,
                                                AF.Relu,
                                                bias=c.sb["VB"][:, bcol:bcol + 1])
                        gdone = c.on("s", hevp, waits=[("pt", hmd)])
                        gchain.append(gdone)
                        pp.push(("ps", gdone))
                        continue

                    def s1(s, ps=ps, pre=pre, gt=gt, hc=hc):
                        bcol = c.vb_off[nm + "_b1"] + hc
                        s.activation(pre, ps, AF.Identity,
                                     bias=c.sb["VB"][:, bcol:bcol + 1])
                        return s.activation(gt, pre, AF.Square)
                    s1d = c.on("s", s1, waits=[("pt", hmd)] + slot_w)

                    def v1(v, pre=pre, gt=gt):
                        v.tensor_scalar(gt, gt, 0.0356774081, 0.7978845608,
                                        ALU.mult, ALU.add)
                        return v.tensor_tensor(gt, gt, pre, ALU.mult)
                    v1d = c.on("v", v1, waits=[("ps", s1d)] + slot_w)

                    s2d = c.on("s", lambda s, gt=gt: s.activation(gt, gt, AF.Tanh),
                               waits=[("pv", v1d)])

                    def v2(v, pre=pre, gt=gt, hc=hc):
                        v.tensor_scalar(gt, gt, 1.0, 0.5, ALU.add, ALU.mult)
                        return v.tensor_tensor(SB3[:, hc * TT:(hc + 1) * TT],
                                               gt, pre, ALU.mult)
                    gdone = c.on("v", v2, waits=[("ps", s2d)])
                    gchain.append(gdone)
                    pp.push(("pv", gdone))
                piece_mm.append(c.now("pt"))

            w2_t = []
            w2_mm = []
            for oc in range(DC if L0A_STAGE >= 6 else 0):
                pslot = oc % 2
                lw = [("pt", w2_mm[oc - 2] if oc >= 2 else piece_mm[-1])]
                tgt = c.dma("g", WT[:, pslot * 4096:pslot * 4096 + 4096]
                            .rearrange("p (kc m) -> p kc m", kc=32),
                            c.dram[nm + "_W2"].rearrange("(kc p) m -> p kc m", p=128)
                            [:, :, oc * 128:(oc + 1) * 128],
                            waits=lw, sem=f"d{3 + pslot}")
                w2_t.append((f"d{3 + pslot}", tgt))
                ps = pp.bank()
                war2 = pp.war_wait()
                w = [w2_t[oc], ("pv", gchain[-1])]
                if war2 is not None:
                    w.append(war2)

                def omm(t, pslot=pslot, ps=ps):
                    last = None
                    for kc in range(32):
                        last = t.matmul(ps, WT[:, pslot * 4096 + kc * 128:
                                               pslot * 4096 + (kc + 1) * 128],
                                        SB3[:, kc * TT:(kc + 1) * TT],
                                        start=(kc == 0), stop=(kc == 31))
                    return last
                omd = c.on("t", omm, waits=w, inc="pt", chain=False)
                w2_mm.append(c.now("pt"))

                def oev2(v, oc=oc, tt2=tt2, ps=ps, sl_a=sl_a):
                    xsl = XIN[:, oc * 1024 + tt2 * TT: oc * 1024 + (tt2 + 1) * TT]
                    tmp = c.sb["TRASH"][:, 0:512].bitcast(F32)[:, 0:256] \
                        if False else None
                    bcol = c.vb_off[nm + "_b2"] + oc
                    v.tensor_tensor(xsl, xsl, ps, ALU.add)
                    v.tensor_scalar(xsl, xsl, c.sb["VB"][:, bcol:bcol + 1],
                                    None, ALU.add)
                    rsl = c.sb["RES"][:, oc * T + sl_a.start: oc * T + sl_a.stop]
                    return v.scalar_tensor_tensor(rsl, xsl, scale, rsl,
                                                  ALU.mult, ALU.add)
                val = c.on("v", oev2, waits=[("pt", omd)])
                pp.push(("pv", val))
            ffn_done = c.now("pv")
            c.join()
        if L0A_STAGE < 5:
            ffn_done = None
            c.join()




def _fp4_core(c, v, src_ap, dst_ap, scrA, scrB, VECQ, nf):
    MR = 12582912.0
    nb = nf // 16
    bm, rs, m8f = VECQ
    m8 = (m8f[0][:, 0:nf], m8f[1][:, 0:nf])
    scrA = scrA[:, 0:nf]
    scrB = scrB[:, 0:nf]
    v.tensor_reduce(bm[:, 0:nb], src_ap.rearrange("p (b k) -> p b k", k=16),
                    AX.X, ALU.max, apply_absolute_value=True)
    v.tensor_scalar(bm[:, 0:nb], bm[:, 0:nb], 1.0 / 6.0, None, ALU.mult)
    v.tensor_scalar(bm[:, 0:nb], bm[:, 0:nb], 1e-12, None, ALU.max)
    v.reciprocal(rs[:, 0:nb], bm[:, 0:nb])
    xn = scrA
    v.tensor_tensor(xn.rearrange("p (b k) -> p b k", k=16),
                    src_ap.rearrange("p (b k) -> p b k", k=16),
                    rs[:, 0:nb].rearrange("p (b o) -> p b o", o=1)
                    .broadcast_to((128, nb, 16)), ALU.mult)
    q = scrB
    v.tensor_scalar(q, xn, -1.0, None, ALU.mult)
    v.tensor_tensor(q, q, xn, ALU.max)
    v.tensor_scalar(m8[0], q, 4.5, None, ALU.is_le)
    v.tensor_scalar(m8[1], q, 2.25, None, ALU.is_le)
    v.tensor_scalar(q, xn, 0.5, MR, ALU.mult, ALU.add)
    v.tensor_scalar(q, q, MR, 2.0, ALU.subtract, ALU.mult)
    v.tensor_scalar(xn, xn, MR, MR, ALU.add, ALU.subtract)
    v.select(q, m8[0], xn, q)
    r1 = xn
    v.tensor_tensor(r1.rearrange("p (b k) -> p b k", k=16),
                    src_ap.rearrange("p (b k) -> p b k", k=16),
                    rs[:, 0:nb].rearrange("p (b o) -> p b o", o=1)
                    .broadcast_to((128, nb, 16)), ALU.mult)
    v.tensor_scalar(r1, r1, 2.0, MR, ALU.mult, ALU.add)
    v.tensor_scalar(r1, r1, MR, 0.5, ALU.subtract, ALU.mult)
    v.select(q, m8[1], r1, q)
    return v.tensor_tensor(dst_ap.rearrange("p (b k) -> p b k", k=16),
                           q.rearrange("p (b k) -> p b k", k=16),
                           bm[:, 0:nb].rearrange("p (b o) -> p b o", o=1)
                           .broadcast_to((128, nb, 16)), ALU.mult)


def emit_l2f(c, scale):
    XH, SB2, SB3, VEC, WT, SCR = (c.sb["XH"], c.sb["SB2"], c.sb["SB3"],
                                  c.sb["VEC"], c.sb["WT"], c.sb["SCR"])
    PSA, PSB, PST = c.sb["PSA"], c.sb["PSB"], c.sb["PST"]
    RES = c.sb["RES"]
    I2 = 2816
    IC = 22
    XQ = SB2

    def vq_scr():
        bm = VEC[:, 1024:1088]
        rs = VEC[:, 1088:1152]
        m8 = (SB3.bitcast(U8)[:, 16384:16896], SB3.bitcast(U8)[:, 16896:17408])
        return bm, rs, m8

    if not getattr(c, "l2f_q_done", False):
        c.l2f_q_done = True
        pcs = [("L2_ffn_w_gate", "l2f_wg_scr", 1024, 22),
               ("L2_ffn_w_up", "l2f_wu_scr", 1024, 22),
               ("L2_ffn_w_down", "l2f_wd_scr", I2, 8)]
        prev_spill = None
        prev_q = None
        for wname, sname, pw, npiece in pcs:
            nchunks = pw // 128
            for pi in range(npiece):
                lw = [("pv", prev_q)] if prev_q is not None else []
                ld = c.dma("y", SCR[:, 0:pw], c.dram[wname][pi * 128:(pi + 1) * 128, :],
                           waits=lw, sem="d0")

                def qf2(v, pw=pw):
                    last = None
                    m8w = (SB3.bitcast(U8)[:, 18432:18944],
                           SB3.bitcast(U8)[:, 18944:19456])
                    for off in range(0, pw, 512):
                        w = min(512, pw - off)
                        last = _fp4_core(
                            c, v, SCR[:, off:off + w],
                            SB3[:, 2 * pw + off:2 * pw + off + w],
                            VEC[:, 0:w], VEC[:, 512:512 + w],
                            (VEC[:, 1024:1088], VEC[:, 1088:1152], m8w), w)
                    return last
                qd = c.on("v", qf2, waits=[("d0", ld)])
                prev_q = qd
                TK = SB3[:, 12288:12288 + nchunks * 128]
                tprev = None
                tevs = []
                for kk in range(nchunks):
                    def tp(t, kk=kk, pw=pw):
                        return t.transpose(
                            PST[:, (kk % 2) * 1024:(kk % 2) * 1024 + 128],
                            SB3[:, 2 * pw + kk * 128:2 * pw + (kk + 1) * 128],
                            c.sb["IDENTB2"])
                    w_ = [("pv", qd)]
                    if len(tevs) >= 2:
                        w_.append(("pv", tevs[-2]))
                    td = c.on("t", tp, waits=w_, inc="pt", chain=False)
                    evw = [("pt", td)]
                    if kk == 0 and prev_spill is not None:
                        evw.append(prev_spill)
                    tevs.append(c.on("v", lambda v, kk=kk: v.tensor_copy(
                        TK[:, kk * 128:(kk + 1) * 128],
                        PST[:, (kk % 2) * 1024:(kk % 2) * 1024 + 128]),
                        waits=evw))
                prev_q = tevs[-1]
                sp = c.dma("g", c.dram[sname]
                           .rearrange("(kc p) f -> p kc f", p=128)[:, :, pi * 128:(pi + 1) * 128],
                           TK.rearrange("p (kc f) -> p kc f", f=128),
                           waits=[("pv", prev_q)], sem="d1")
                prev_spill = ("d1", sp)
        c.l2f_spill = prev_spill
        c.join()
    spill_w = [c.l2f_spill] if getattr(c, "l2f_spill", None) else []

    wslot = [None, None]
    prev_ev = {}
    for qt4 in range(4):
        tok0 = qt4 * TT

        lnpre = list(spill_w)
        if prev_ev.get("down"):
            lnpre.append(("pv", prev_ev["down"]))
        ln = emit_ln_g(
            c,
            src_fn=lambda kc, sl: RES[:, kc * T + sl.start: kc * T + sl.stop],
            dst_fn=lambda kc, sl: SCR[:, kc * TT:(kc + 1) * TT],
            sq_fn=lambda kc, sl: SCR[:, 4096 + kc * TT:4096 + (kc + 1) * TT],
            srcb_fn=None, tok0=tok0, ntok=TT, stat_off=0, f32_dst=True,
            f32_sums=True, pre_waits=lnpre)

        xq_done = []
        for g in range(4):
            tevs = []
            for kc in range(DC):
                def tpx(t, g=g, kc=kc):
                    return t.transpose(PSB[:, (kc % 2) * 512:(kc % 2) * 512 + 128],
                                       SCR[:, kc * TT + g * 128: kc * TT + (g + 1) * 128],
                                       c.sb["IDENTF"])
                w_ = [("pv", ln)]
                if len(tevs) >= 2:
                    w_.append(("pv", tevs[-2]))
                if xq_done:
                    w_.append(("pv", xq_done[-1]))
                td = c.on("t", tpx, waits=w_, inc="pt", chain=False)
                tevs.append(c.on("v", lambda v, kc=kc: v.tensor_copy(
                    VEC[:, 1152 + kc * 128:1152 + (kc + 1) * 128],
                    PSB[:, (kc % 2) * 512:(kc % 2) * 512 + 128]),
                    waits=[("pt", td)]))

            def qx(v, g=g):
                last = None
                for off in range(0, 1024, 512):
                    last = _fp4_core(
                        c, v, VEC[:, 1152 + off:1152 + off + 512],
                        c.sb["TRASH"][:, off:off + 512],
                        VEC[:, 0:512], VEC[:, 512:1024], vq_scr(), 512)
                return last
            qxd = c.on("v", qx, waits=[("pv", tevs[-1])])

            bevs = []
            for kc in range(DC):
                def tpb(t, kc=kc):
                    return t.transpose(PST[:, (kc % 2) * 1024:(kc % 2) * 1024 + 128],
                                       c.sb["TRASH"][:, kc * 128:(kc + 1) * 128],
                                       c.sb["IDENTB2"])
                w_ = [("pv", qxd)]
                if len(bevs) >= 2:
                    w_.append(("pv", bevs[-2]))
                td = c.on("t", tpb, waits=w_, inc="pt", chain=False)
                bevs.append(c.on("v", lambda v, kc=kc, g=g: v.tensor_copy(
                    XQ[:, kc * T + tok0 + g * 128: kc * T + tok0 + (g + 1) * 128],
                    PST[:, (kc % 2) * 1024:(kc % 2) * 1024 + 128]),
                    waits=[("pt", td)]))
            xq_done.append(bevs[-1])
        xq_all = xq_done[-1]

        pp = PsumPool(c, PSA, 4)
        fchunks = [(0, 512), (512, 512), (1024, 512), (1536, 512),
                   (2048, 512), (2560, 256)]
        for wi, (wname, act) in enumerate((("l2f_wg_scr", "gate"),
                                           ("l2f_wu_scr", "up"))):
            for ci, (f0, fw) in enumerate(fchunks):
                slot = (wi * len(fchunks) + ci) % 2
                lw = list(spill_w)
                if wslot[slot] is not None:
                    lw.append(("pt", wslot[slot]))
                tgt = c.dma("g", WT[:, slot * 4096:slot * 4096 + DC * fw]
                            .rearrange("p (kc f) -> p kc f", kc=DC),
                            c.dram[wname].rearrange("(kc p) f -> p kc f", p=128)
                            [:, :, f0:f0 + fw],
                            waits=lw, sem=f"d{3 + slot}")
                for g in range(4):
                    ps = pp.bank()
                    war2 = pp.war_wait()
                    w_ = [(f"d{3 + slot}", tgt), ("pv", xq_all)]
                    if war2 is not None:
                        w_.append(war2)

                    def gmm(t, g=g, slot=slot, fw=fw, ps=ps):
                        last = None
                        for kc in range(DC):
                            last = t.matmul(
                                ps[:, 0:fw],
                                XQ[:, kc * T + tok0 + g * 128: kc * T + tok0 + (g + 1) * 128],
                                WT[:, slot * 4096 + kc * fw: slot * 4096 + (kc + 1) * fw],
                                start=(kc == 0), stop=(kc == DC - 1))
                        return last
                    gmd = c.on("t", gmm, waits=w_, inc="pt", chain=False)

                    if act == "gate":
                        def gev(s, g=g, f0=f0, fw=fw, ps=ps):
                            return s.activation(
                                SB3[:, g * I2 + f0: g * I2 + f0 + fw],
                                ps[:, 0:fw], AF.Silu)
                        pp.push(("ps", c.on("s", gev, waits=[("pt", gmd)])))
                    else:
                        def uev(v, g=g, f0=f0, fw=fw, ps=ps):
                            dsl = SB3[:, g * I2 + f0: g * I2 + f0 + fw]
                            return v.tensor_tensor(dsl, dsl, ps[:, 0:fw], ALU.mult)
                        pp.push(("pv", c.on("v", uev, waits=[("pt", gmd)])))
                wslot[slot] = c.now("pt")
        h_done = c.now("pv")

        hq_done = []
        mh8 = (SB3.bitcast(U8)[:, 22528:23040], SB3.bitcast(U8)[:, 23040:23552])
        for g in range(4):
            def qh(v, g=g):
                last = None
                for off in range(0, I2, 512):
                    w = min(512, I2 - off)
                    last = _fp4_core(
                        c, v, SB3[:, g * I2 + off: g * I2 + off + w],
                        SB3[:, g * I2 + off: g * I2 + off + w],
                        VEC[:, 0:512], VEC[:, 512:1024],
                        (VEC[:, 1024:1088], VEC[:, 1088:1152], mh8), w)
                return last
            hq_done.append(c.on("v", qh, waits=[("pv", h_done)] if g == 0 else
                                [("pv", hq_done[-1])]))

        hkt_done = []
        for g in range(4):
            tevs = []
            for kk in range(IC):
                def tph(t, g=g, kk=kk):
                    return t.transpose(PST[:, (kk % 2) * 1024:(kk % 2) * 1024 + 128],
                                       SB3[:, g * I2 + kk * 128: g * I2 + (kk + 1) * 128],
                                       c.sb["IDENTB2"])
                w_ = [("pv", hq_done[g])]
                if len(tevs) >= 2:
                    w_.append(("pv", tevs[-2]))
                td = c.on("t", tph, waits=w_, inc="pt", chain=False)
                tevs.append(c.on("v", lambda v, g=g, kk=kk: v.tensor_copy(
                    SB3[:, g * I2 + kk * 128: g * I2 + (kk + 1) * 128],
                    PST[:, (kk % 2) * 1024:(kk % 2) * 1024 + 128]),
                    waits=[("pt", td)]))
            hkt_done.append(tevs[-1])
        hkt_all = hkt_done[-1]

        down_evs = []
        for wave in range(2):
            ocs = range(wave * 4, wave * 4 + 4)
            mm_last = None
            for kk in range(IC):
                slot = kk % 2
                lw = list(spill_w)
                if wslot[slot] is not None:
                    lw.append(("pt", wslot[slot]))
                tgt = c.dma("g", WT[:, slot * 4096:slot * 4096 + 1024],
                            c.dram["l2f_wd_scr"][kk * 128:(kk + 1) * 128, :],
                            waits=lw, sem=f"d{3 + slot}")

                def dmm(t, kk=kk, slot=slot, ocs=ocs, wave=wave):
                    last = None
                    for oc in ocs:
                        ocl = oc - wave * 4
                        for g in range(4):
                            ps = PSA[:, ocl * 512 + g * 128: ocl * 512 + (g + 1) * 128]
                            last = t.matmul(
                                ps, WT[:, slot * 4096 + oc * 128: slot * 4096 + (oc + 1) * 128],
                                SB3[:, g * I2 + kk * 128: g * I2 + (kk + 1) * 128],
                                start=(kk == 0 and g == 0 and oc == ocs[0] if False
                                       else (kk == 0 and g == 0)),
                                stop=(kk == IC - 1 and g == 3))
                    return last
                w_ = [(f"d{3 + slot}", tgt), ("pv", hkt_all)]
                if down_evs:
                    w_.append(("pv", down_evs[-1]))
                mm_last = c.on("t", dmm, waits=w_, inc="pt", chain=False)
                wslot[slot] = c.now("pt")

            def dev(v, ocs=ocs, wave=wave, tok0=tok0):
                last = None
                for oc in ocs:
                    ocl = oc - wave * 4
                    ps = PSA[:, ocl * 512:(ocl + 1) * 512]
                    rsl = RES[:, oc * T + tok0: oc * T + tok0 + 512]
                    last = v.scalar_tensor_tensor(rsl, ps, scale, rsl,
                                                  ALU.mult, ALU.add)
                return last
            down_evs.append(c.on("v", dev, waits=[("pt", mm_last)]))
        prev_ev["down"] = down_evs[-1]
        c.join()


def emit_fp4_quant(c, v, s, src, dst, nf, scratch_f32, scratch2_f32, nb):
    MR = 12582912.0

    def step1(v):
        bm = scratch2_f32[:, 0:nb]
        v.tensor_reduce(bm, src.rearrange("p (b k) -> p b k", k=16), AX.X,
                        ALU.max, apply_absolute_value=True)
        v.tensor_scalar(bm, bm, 1.0 / 6.0, 1e-12, ALU.mult, ALU.max)
        rs = scratch2_f32[:, nb:2 * nb]
        v.reciprocal(rs, bm)
        xn = scratch_f32[:, 0:nf]
        v.tensor_tensor(xn.rearrange("p (b k) -> p b k", k=16),
                        src.rearrange("p (b k) -> p b k", k=16),
                        rs.rearrange("p (b o) -> p b o", o=1).broadcast_to((128, nb, 16)),
                        ALU.mult)
        aa = scratch2_f32[:, 2 * nb:2 * nb + nf]
        return v.tensor_scalar(aa, xn, 0.0, None, ALU.abs_max)
    p1 = c.on("v", step1)

    xn = scratch_f32[:, 0:nf]
    aa = scratch2_f32[:, 2 * nb:2 * nb + nf]

    def step2(v):
        r1 = scratch2_f32[:, 2 * nb + nf:2 * nb + 2 * nf]
        v.tensor_scalar(r1, xn, 2.0, MR, ALU.mult, ALU.add)
        v.tensor_scalar(r1, r1, MR, 0.5, ALU.subtract, ALU.mult)
        r3 = scratch_f32[:, nf:2 * nf] if False else None
        q = scratch2_f32[:, 2 * nb + 2 * nf:2 * nb + 3 * nf]
        v.tensor_scalar(q, xn, 0.5, MR, ALU.mult, ALU.add)
        v.tensor_scalar(q, q, MR, 2.0, ALU.subtract, ALU.mult)
        r2 = xn
        v.tensor_scalar(r2, xn, MR, None, ALU.add)
        v.tensor_scalar(r2, r2, MR, None, ALU.subtract)
        m = scratch2_f32.bitcast(U8)[:, 0:nf]
        v.tensor_scalar(m, aa, 4.5, None, ALU.is_le)
        v.select(q, m, r2, q)
        v.tensor_scalar(m, aa, 2.25, None, ALU.is_le)
        v.select(q, m, r1, q)
        return v.tensor_tensor(dst.rearrange("p (b k) -> p b k", k=16),
                               q.rearrange("p (b k) -> p b k", k=16),
                               scratch2_f32[:, 0:nb]
                               .rearrange("p (b o) -> p b o", o=1)
                               .broadcast_to((128, nb, 16)),
                               ALU.mult)
    return c.on("v", step2, waits=[("pv", p1)])



def emit_l1f(c, scale):
    XH, SB2, SB3, VEC, WT, SCR = (c.sb["XH"], c.sb["SB2"], c.sb["SB3"],
                                  c.sb["VEC"], c.sb["WT"], c.sb["SCR"])
    PSA, PSB, PST = c.sb["PSA"], c.sb["PSB"], c.sb["PST"]
    PBUF = c.sb["PBUF"]
    nm = "L1_ffn"
    NQT = 16
    QS = SB2[0:64, 0:4096]
    KS = SB2[0:64, 4096:8192]
    VT = SB2[:, 8192:10240]
    VSTG = SB2[0:64, 10240:11264]

    unit = 0
    wslot_war = []
    ctx_done_prev = None
    for hp in range(8):
        slot = unit % 2
        wof = slot * 4096
        war = wslot_war[unit - 2] if unit >= 2 else None
        wwaits = [("pt", war)] if war is not None else []
        sem = f"d{3 + slot}"
        c.dma("g", WT[:, wof + 0:wof + 1024].rearrange("p (kc m) -> p kc m", kc=DC),
              c.dram[nm + "_Wq"].rearrange("(kc p) m -> p kc m", p=128)[:, :, hp * 128:(hp + 1) * 128],
              waits=wwaits, sem=sem)
        c.dma("g", WT[:, wof + 1024:wof + 2048].rearrange("p (kc m) -> p kc m", kc=DC),
              c.dram[nm + "_Wk"].rearrange("(kc p) m -> p kc m", p=128)[:, :, hp * 128:(hp + 1) * 128],
              sem=sem)
        tv = c.dma("g", WT[:, wof + 2048:wof + 3072].rearrange("p (kc m) -> p kc m", kc=DC),
                   c.dram[nm + "_Wv"].rearrange("(kc p) m -> p kc m", p=128)[:, :, hp * 128:(hp + 1) * 128],
                   sem=sem)
        wsem = (sem, tv)

        qkv_done = []
        j = 0
        for proj, wo0, dstbuf, dw, bname in (
                ("q", 0, QS, 4096, "_bq"), ("k", 1024, KS, 4096, "_bk")):
            bcol = c.vb_off[nm + bname] + hp
            for tt2 in range(4):
                ps = PSA[:, (j % 2) * TT:(j % 2) * TT + TT]
                waits = [wsem]
                if ctx_done_prev is not None:
                    waits.append(("pv", ctx_done_prev))
                if j >= 2:
                    waits.append(("ps", qkv_done[j - 2]))

                def mmq(t, wo0=wo0, tt2=tt2, ps=ps):
                    last = None
                    for kc in range(DC):
                        last = t.matmul(ps, WT[:, wof + wo0 + kc * 128: wof + wo0 + (kc + 1) * 128],
                                        xh_rhs(c, kc, tt2),
                                        start=(kc == 0), stop=(kc == DC - 1))
                    return last
                mmd = c.on("t", mmq, waits=waits, inc="pt", chain=False)

                def ev(s, tt2=tt2, ps=ps, dstbuf=dstbuf, bcol=bcol, dw=dw):
                    s.activation(dstbuf[0:64, tt2 * TT:(tt2 + 1) * TT],
                                 ps[0:64, :], AF.Identity,
                                 bias=c.sb["VB"][0:64, bcol:bcol + 1])
                    return s.activation(dstbuf[0:64, dw // 2 * 0 + 2048 + tt2 * TT:2048 + (tt2 + 1) * TT],
                                        ps[64:128, :], AF.Identity,
                                        bias=c.sb["VB"][64:128, bcol:bcol + 1])
                qkv_done.append(c.on("s", ev, waits=[("pt", mmd)]))
                j += 1

        vt_evacs = []
        bcolv = c.vb_off[nm + "_bv"] + hp
        for tt2 in range(4):
            ps = PSA[:, (j % 2) * TT:(j % 2) * TT + TT]
            waits = [wsem]
            if ctx_done_prev is not None:
                waits.append(("pv", ctx_done_prev))
            if j >= 2:
                waits.append(("ps", qkv_done[j - 2]))
            if vt_evacs:
                waits.append(("pv", vt_evacs[-1]))

            def mmv(t, tt2=tt2, ps=ps):
                last = None
                for kc in range(DC):
                    last = t.matmul(ps, WT[:, wof + 2048 + kc * 128: wof + 2048 + (kc + 1) * 128],
                                    xh_rhs(c, kc, tt2),
                                    start=(kc == 0), stop=(kc == DC - 1))
                return last
            mmd = c.on("t", mmv, waits=waits, inc="pt", chain=False)

            def vev(s, ps=ps):
                s.activation(VSTG[0:64, 0:TT], ps[0:64, :], AF.Identity,
                             bias=c.sb["VB"][0:64, bcolv:bcolv + 1])
                return s.activation(VSTG[0:64, 512:1024], ps[64:128, :],
                                    AF.Identity,
                                    bias=c.sb["VB"][64:128, bcolv:bcolv + 1])
            vevd = c.on("s", vev, waits=[("pt", mmd)])
            qkv_done.append(vevd)
            j += 1
            for h2 in range(2):
                for cc in range(4):
                    ci = tt2 * 4 + cc
                    idx = len(vt_evacs)
                    pcol = (idx % 2) * 1024

                    def vtr1(t, h2=h2, cc=cc, pcol=pcol):
                        return t.transpose(
                            c.sb["PST"][:, pcol:pcol + 64],
                            VSTG[0:64, h2 * 512 + cc * 128: h2 * 512 + (cc + 1) * 128],
                            c.sb["IDENTB"][0:64, 0:64])
                    w2_ = [("ps", vevd)]
                    if idx >= 2:
                        w2_.append(("pv", vt_evacs[idx - 2]))
                    vtd = c.on("t", vtr1, waits=w2_, inc="pt", chain=False)
                    vt_evacs.append(c.on(
                        "v", lambda v, h2=h2, ci=ci, pcol=pcol:
                        v.tensor_copy(VT[:, h2 * 1024 + ci * 64: h2 * 1024 + (ci + 1) * 64],
                                      c.sb["PST"][:, pcol:pcol + 64]),
                        waits=[("pt", vtd)]))
        vt_all = vt_evacs[-1]
        qkv_all = qkv_done[-1]

        RCP = VEC[0:1, 2048:2176]
        PTILDE = PBUF[:, 0:128]
        ctx_evacs = []
        rcp_prev = None
        for h2 in range(2):
            for qt in range(NQT):
                idx = h2 * NQT + qt
                qsl = QS[0:64, h2 * 2048 + qt * 128: h2 * 2048 + (qt + 1) * 128]
                cps = PSA[:, 1024 + (qt % 2) * TT:1024 + (qt % 2) * TT + 128]
                exp_prev = None
                for kc16 in range(16):
                    sps = PSA[:, (kc16 % 2) * TT:(kc16 % 2) * TT + 128]

                    def smm(t, h2=h2, kc16=kc16, sps=sps, qsl=qsl):
                        return t.matmul(sps, KS[0:64, h2 * 2048 + kc16 * 128: h2 * 2048 + (kc16 + 1) * 128],
                                        qsl, start=True, stop=True)
                    sw = [("pv", vt_all), ("ps", qkv_all)]
                    if exp_prev is not None and kc16 >= 2:
                        sw.append(("ps", exp_hist[kc16 - 2]))
                    if idx >= 2 and kc16 == 0:
                        sw.append(("pv", ctx_evacs[idx - 2]))
                    smd = c.on("t", smm, waits=sw, inc="pt", chain=False)
                    if kc16 == 0:
                        exp_hist = []

                    def pexp(s, sps=sps, kc16=kc16):
                        return s.activation(PTILDE, sps, AF.Exp, scale=0.125)
                    pw_ = [("pt", smd)]
                    if exp_prev is not None:
                        pw_.append(("pt", ctxmm_hist[kc16 - 1]))
                    exp_prev = c.on("s", pexp, waits=pw_)
                    exp_hist.append(exp_prev)
                    if kc16 == 0:
                        ctxmm_hist = []

                    def summ_ctx(t, h2=h2, qt=qt, kc16=kc16, cps=cps):
                        t.matmul(PSB[0:1, 0:128], c.sb["ONECOL"], PTILDE,
                                 start=(kc16 == 0), stop=(kc16 == 15))
                        return t.matmul(cps[0:64, :],
                                        VT[:, h2 * 1024 + kc16 * 64: h2 * 1024 + (kc16 + 1) * 64],
                                        PTILDE, start=(kc16 == 0), stop=(kc16 == 15))
                    scw = [("ps", exp_prev)]
                    if rcp_prev is not None and kc16 == 0:
                        scw.append(("pv", rcp_prev))
                    ctxmm_hist.append(c.on("t", summ_ctx, waits=scw, inc="pt",
                                           chain=False))

                rcp_prev = c.on("v", lambda v: v.reciprocal(RCP, PSB[0:1, 0:128]),
                                waits=[("pt", ctxmm_hist[-1])])

                def bcst(t):
                    return t.matmul(PSB[0:64, 512:640], c.sb["ONESF"][0:1, 0:64],
                                    RCP, start=True, stop=True)
                bcw = [("pv", rcp_prev)]
                if ctx_evacs:
                    bcw.append(("pv", ctx_evacs[-1]))
                bcd = c.on("t", bcst, waits=bcw, inc="pt", chain=False)

                def cev(v, h2=h2, qt=qt, cps=cps, hp=hp):
                    CT = VEC[0:64, 1792:1920]
                    v.tensor_copy(CT, cps[0:64, :])
                    v.tensor_tensor(CT, CT, PSB[0:64, 512:640], ALU.mult)
                    return v.tensor_copy(
                        SB3[h2 * 64:h2 * 64 + 64,
                            hp * 2048 + qt * 128: hp * 2048 + (qt + 1) * 128], CT)
                ctx_evacs.append(c.on("v", cev, waits=[("pt", cmd0 := bcd),
                                                       ("pt", ctxmm_hist[-1])]))
        ctx_done_prev = ctx_evacs[-1]
        wslot_war.append(c.now("pt"))
        unit += 1

    attn_all = ctx_done_prev
    wo_l = c.dma("g", WT[:, 0:8192].rearrange("p (kc m) -> p kc m", kc=DC),
                 c.dram[nm + "_Wo"].rearrange("(kc p) m -> p kc m", p=128),
                 waits=[("pt", wslot_war[-1])], sem="d3")
    AO = SB2
    pp = PsumPool(c, PSA, 4)
    ao_done = []
    for jj in range(32):
        oc, tt2 = jj % DC, jj // DC
        ps = pp.bank()
        war2 = pp.war_wait()
        w = [("d3", wo_l), ("pv", attn_all)]
        if war2 is not None:
            w.append(war2)

        def wmm(t, oc=oc, tt2=tt2, ps=ps):
            last = None
            for kc in range(DC):
                last = t.matmul(ps, WT[:, kc * 1024 + oc * 128: kc * 1024 + (oc + 1) * 128],
                                SB3[:, kc * T + tt2 * TT: kc * T + (tt2 + 1) * TT],
                                start=(kc == 0), stop=(kc == DC - 1))
            return last
        wod = c.on("t", wmm, waits=w, inc="pt", chain=False)

        def wev(s, oc=oc, tt2=tt2, ps=ps):
            bcol = c.vb_off[nm + "_bo"] + oc
            return s.activation(AO[:, oc * T + tt2 * TT: oc * T + (tt2 + 1) * TT],
                                ps, AF.Identity,
                                bias=c.sb["VB"][:, bcol:bcol + 1])
        val = c.on("s", wev, waits=[("pt", wod)])
        ao_done.append(val)
        pp.push(("ps", val))
    ao_all = ao_done[-1]

    ln1 = emit_ln_g(
        c,
        src_fn=lambda kc, sl: AO[:, kc * T + sl.start: kc * T + sl.stop],
        dst_fn=lambda kc, sl: SB3[:, kc * T + sl.start: kc * T + sl.stop],
        sq_fn=lambda kc, sl: SB3[:, kc * T + sl.start: kc * T + sl.stop],
        srcb_fn=None, tok0=0, ntok=1024, stat_off=0,
        w_off=c.vb_off[nm + "_g1"], b_off=c.vb_off[nm + "_be1"],
        pre_waits=[("ps", ao_all)])
    ln1b = emit_ln_g(
        c,
        src_fn=lambda kc, sl: AO[:, kc * T + sl.start: kc * T + sl.stop],
        dst_fn=lambda kc, sl: SB3[:, kc * T + sl.start: kc * T + sl.stop],
        sq_fn=lambda kc, sl: SB3[:, kc * T + sl.start: kc * T + sl.stop],
        srcb_fn=None, tok0=1024, ntok=1024, stat_off=0,
        w_off=c.vb_off[nm + "_g1"], b_off=c.vb_off[nm + "_be1"],
        pre_waits=[("ps", ao_all), ("pv", ln1)])

    x1d = c.on("v", lambda v: v.tensor_tensor(XH[:, 0:DC * T], XH[:, 0:DC * T],
                                              SB3[:, 0:DC * T], ALU.add),
               waits=[("pv", ln1b)])

    gl = ("d0", c.dma("g", c.sb["TRASH"][0:128, 0:E8 * DC]
                      .rearrange("p (kc e) -> p kc e", kc=DC),
                      c.dram[nm + "_Wg"].rearrange("(kc p) e -> p kc e", p=128),
                      sem="d0"))
    gbl = ("d1", c.dma("y", VEC[0:8, 2048:2049], c.dram["L1_gb8"], sem="d1",
                       waits=[("pv", x1d)]))
    esl = ("d2", c.dma("g", c.sb["AUX"].bitcast(BF16)[0:8, 0:1024],
                       c.dram["esel"], sem="d2"))
    bexl = ("d6", c.dma("g", SB2[0:8, 15360:16384], c.dram[nm + "_bexp"],
                        sem="d6", waits=[("pv", x1d)]))

    def gmm(t):
        last = None
        for tt2 in range(NTT):
            for kc in range(DC):
                last = t.matmul(PSA[0:8, tt2 * TT:(tt2 + 1) * TT],
                                c.sb["TRASH"][:, kc * E8:(kc + 1) * E8],
                                xh_rhs(c, kc, tt2), start=(kc == 0),
                                stop=(kc == DC - 1))
        return last
    gdone = c.on("t", gmm, waits=[gl, ("pv", x1d)], inc="pt", chain=False)

    z8 = VEC[0:8, 0:T]
    gz = c.on("v", lambda v: v.tensor_scalar(z8, PSA[0:8, 0:T],
                                             VEC[0:8, 2048:2049], None, ALU.add),
              waits=[("pt", gdone), gbl])

    PT = SCR[:, 0:128]
    MX1 = SCR[:, 128:144]
    MX2 = SCR[:, 144:160]
    GE1 = SCR[:, 160:288]
    WTK = SCR[:, 288:416]
    SM = SCR[:, 416:432]

    def ptr(t):
        last = None
        for i in range(16):
            last = t.transpose(PSB[:, i * 8:(i + 1) * 8], z8[:, i * 128:(i + 1) * 128],
                               c.sb["IDENTF"][0:8, 0:8])
        return last
    ptd = c.on("t", ptr, waits=[("pv", gz)], inc="pt", chain=False)

    def sfm(v):
        v.tensor_copy(PT, PSB[:, 0:128])
        v.tensor_reduce(MX1, PT.rearrange("p (i e) -> p i e", e=8), AX.X, ALU.max,
                        negate=True)
        v.tensor_tensor(PT.rearrange("p (i e) -> p i e", e=8),
                        PT.rearrange("p (i e) -> p i e", e=8),
                        MX1.rearrange("p (i o) -> p i o", o=1).broadcast_to((128, 16, 8)),
                        ALU.add)
        return PT
    sf1 = c.on("v", sfm, waits=[("pt", ptd)])
    sf2 = c.on("s", lambda s: s.activation(PT, PT, AF.Exp), waits=[("pv", sf1)])

    def sfm2(v):
        v.tensor_reduce(SM, PT.rearrange("p (i e) -> p i e", e=8), AX.X, ALU.add)
        v.reciprocal(SM, SM)
        v.tensor_tensor(PT.rearrange("p (i e) -> p i e", e=8),
                        PT.rearrange("p (i e) -> p i e", e=8),
                        SM.rearrange("p (i o) -> p i o", o=1).broadcast_to((128, 16, 8)),
                        ALU.mult)
        v.tensor_reduce(MX1, PT.rearrange("p (i e) -> p i e", e=8), AX.X, ALU.max)
        v.tensor_tensor(GE1.rearrange("p (i e) -> p i e", e=8),
                        PT.rearrange("p (i e) -> p i e", e=8),
                        MX1.rearrange("p (i o) -> p i o", o=1).broadcast_to((128, 16, 8)),
                        ALU.is_ge)
        v.scalar_tensor_tensor(GE1, GE1, -10.0, PT, ALU.mult, ALU.add)
        v.tensor_reduce(MX2, GE1.rearrange("p (i e) -> p i e", e=8), AX.X, ALU.max)
        v.tensor_tensor(GE1.rearrange("p (i e) -> p i e", e=8),
                        PT.rearrange("p (i e) -> p i e", e=8),
                        MX2.rearrange("p (i o) -> p i o", o=1).broadcast_to((128, 16, 8)),
                        ALU.is_ge)
        return v.tensor_tensor(WTK, PT, GE1, ALU.mult)
    t2d = c.on("v", sfm2, waits=[("ps", sf2)])

    WTKT = VEC.bitcast(BF16)[0:8, 2048:2048 + T]
    tb_prev = t2d
    for i in range(16):
        def tb(t, i=i):
            return t.transpose(PSB[0:8, 128:256], WTK[:, i * 8:(i + 1) * 8],
                               c.sb["IDENTF"])
        tbd = c.on("t", tb, waits=[("pv", tb_prev)], inc="pt", chain=False)
        tb_prev = c.on("v", lambda v, i=i: v.tensor_copy(
            WTKT[:, i * 128:(i + 1) * 128], PSB[0:8, 128:256]), waits=[("pt", tbd)])
    topk_done = tb_prev
    _l1f_moe(c, scale, topk_done, esl, x1d, bexl)



E8 = 8


def _l1f_moe(c, scale, topk_done, esl, x1d, bexl):
    XH, SB2, SB3, VEC, WT = (c.sb["XH"], c.sb["SB2"], c.sb["SB3"],
                             c.sb["VEC"], c.sb["WT"])
    PSA, PSB = c.sb["PSA"], c.sb["PSB"]
    nm = "L1_ffn"
    WTKT = VEC.bitcast(BF16)[0:8, 2048:2048 + T]
    esel_bf = c.sb["AUX"].bitcast(BF16)[0:8, 0:1024]
    bexp_sb = SB2[0:8, 15360:16384]

    wslot2 = [None, None]
    prev_tile_ev = None
    for tt2 in range(NTT):
        wb_done = []
        for e in range(E8):
            def wb(t, e=e, tt2=tt2):
                return t.matmul(PSB[:, 512:1024], esel_bf[0:8, e * 128:(e + 1) * 128],
                                WTKT[:, tt2 * TT:(tt2 + 1) * TT], start=True,
                                stop=True)
            w_ = [("pv", topk_done), esl]
            if wb_done:
                w_.append(("pv", wb_done[-1]))
            elif prev_tile_ev is not None:
                w_.append(("pv", prev_tile_ev))
            wbd = c.on("t", wb, waits=w_, inc="pt", chain=False)
            wb_done.append(c.on("v", lambda v, e=e: v.tensor_copy(
                SB2[:, 11264 + e * 512:11264 + (e + 1) * 512], PSB[:, 512:1024]),
                waits=[("pt", wbd)]))
        wb_all = wb_done[-1]

        epoch_ev = []
        for half in range(2):
            mm_last = None
            for e in range(E8):
                def xwf(v, tt2=tt2, e=e):
                    return v.tensor_tensor(
                        SB2[:, 0:DC * TT].rearrange("p (kc f) -> p kc f", f=TT),
                        XH.rearrange("p (kc f) -> p kc f", f=T)[:, :, tt2 * TT:(tt2 + 1) * TT],
                        SB2[:, 11264 + e * 512:11264 + (e + 1) * 512]
                        .rearrange("p (o f) -> p o f", o=1).broadcast_to((128, DC, TT)),
                        ALU.mult)
                xww = [("pv", wb_all), ("pv", x1d)]
                if mm_last is not None:
                    xww.append(("pt", mm_last))
                elif epoch_ev:
                    xww.append(("pv", epoch_ev[-1]))
                xwd = c.on("v", xwf, waits=xww)

                slot = e % 2
                lw = [("pv", xwd)]
                if wslot2[slot] is not None:
                    lw.append(("pt", wslot2[slot]))
                tgt = c.dma("g", WT[:, slot * 4096:(slot + 1) * 4096]
                            .rearrange("p (kc m) -> p kc m", kc=DC),
                            c.dram[nm + "_We"][e].rearrange("(kc p) m -> p kc m", p=128)
                            [:, :, half * 512:(half + 1) * 512],
                            waits=lw, sem=f"d{3 + slot}")

                def emm(t, e=e, slot=slot):
                    last = None
                    for oc4 in range(4):
                        ps = PSA[:, oc4 * 512:(oc4 + 1) * 512]
                        for kc in range(DC):
                            last = t.matmul(
                                ps, WT[:, slot * 4096 + kc * 512 + oc4 * 128:
                                       slot * 4096 + kc * 512 + oc4 * 128 + 128],
                                SB2[:, kc * TT:(kc + 1) * TT],
                                start=(e == 0 and kc == 0), stop=False)
                    return last
                mm_last = c.on("t", emm, waits=[(f"d{3 + slot}", tgt), ("pv", xwd),
                                                bexl], inc="pt", chain=False)
                wslot2[slot] = c.now("pt")

            def bmm(t, tt2=tt2, half=half):
                last = None
                for oc4 in range(4):
                    oc = half * 4 + oc4
                    ps = PSA[:, oc4 * 512:(oc4 + 1) * 512]
                    last = t.matmul(ps, bexp_sb[0:8, oc * 128:(oc + 1) * 128],
                                    WTKT[:, tt2 * TT:(tt2 + 1) * TT],
                                    start=False, stop=True)
                return last
            bmd = c.on("t", bmm, waits=[("pt", mm_last)], inc="pt", chain=False)

            def mev(v, tt2=tt2, half=half):
                last = None
                for oc4 in range(4):
                    oc = half * 4 + oc4
                    ps = PSA[:, oc4 * 512:(oc4 + 1) * 512]
                    last = v.tensor_copy(
                        SB3[:, oc * T + tt2 * TT: oc * T + (tt2 + 1) * TT], ps)
                return last
            epoch_ev.append(c.on("v", mev, waits=[("pt", bmd)]))
        prev_tile_ev = epoch_ev[-1]
    moe_all = prev_tile_ev
    if L1F_SKIP_LN2:
        return

    lnm = None
    for hf in range(2):
        lnm_pre = [("pv", moe_all)] + ([("pv", lnm)] if lnm else [])
        lnm = emit_ln_g(
            c,
            src_fn=lambda kc, sl: SB3[:, kc * T + sl.start: kc * T + sl.stop],
            dst_fn=lambda kc, sl: SB2[:, kc * 1024 + (sl.start - hf * 1024):
                                      kc * 1024 + (sl.stop - hf * 1024)],
            sq_fn=lambda kc, sl: SB2[:, 8192 + kc * 1024 + (sl.start - hf * 1024):
                                     8192 + kc * 1024 + (sl.stop - hf * 1024)],
            srcb_fn=None, tok0=hf * 1024, ntok=1024, stat_off=0,
            w_off=c.vb_off[nm + "_g2"], b_off=c.vb_off[nm + "_be2"],
            pre_waits=lnm_pre)

        def fin(v, hf=hf):
            last = None
            for kc in range(DC):
                for tt2 in range(2):
                    sl = slice(hf * 1024 + tt2 * TT, hf * 1024 + (tt2 + 1) * TT)
                    lsl = SB2[:, kc * 1024 + tt2 * TT: kc * 1024 + (tt2 + 1) * TT]
                    rsl = c.sb["RES"][:, kc * T + sl.start: kc * T + sl.stop]
                    xsl = XH[:, kc * T + sl.start: kc * T + sl.stop]
                    v.scalar_tensor_tensor(rsl, xsl, scale, rsl, ALU.mult, ALU.add)
                    last = v.scalar_tensor_tensor(rsl, lsl, scale, rsl, ALU.mult,
                                                  ALU.add)
            return last
        c.on("v", fin, waits=[("pv", lnm)])


DI = 2048
NDCI = 16
MAMBA_TRUNC = 0
MA_ALOG, MA_BDT, MA_D, MA_CW, MA_CB, MA_NW, MA_NEGB = 0, 256, 272, 288, 352, 368, 376


def mamba_dir_load(c, aux_dram):
    AUX = c.sb["AUX"]
    ld = c.dma("y", AUX[:, 0:MA_NEGB], aux_dram, sem="d7")

    def prepa(s):
        return s.activation(AUX[:, 0:256], AUX[:, 0:256], AF.Exp)
    pa = c.on("s", prepa, waits=[("d7", ld)])

    def prepn(v):
        v.tensor_scalar(AUX[:, 0:256], AUX[:, 0:256], -1.0, None, ALU.mult)
        return v.tensor_scalar(AUX[:, MA_NEGB:MA_NEGB + 16],
                               AUX[:, MA_BDT:MA_BDT + 16], -1.0, None, ALU.mult)
    return c.on("v", prepn, waits=[("ps", pa)])


def emit_rms(c, w_col, dst_base, dst_key="SB3", eps=LN_EPS):
    XH, SB2, VEC, PSA, PSB = (c.sb["XH"], c.sb["SB2"], c.sb["VEC"],
                              c.sb["PSA"], c.sb["PSB"])
    DST = c.sb[dst_key]
    last = None
    for hf in range(2):
        t0 = hf * 1024

        def sq(se):
            l = None
            for kc in range(DC):
                l = se.activation(SB2[:, kc * 2048 + 0:kc * 2048 + 1024],
                                  XH[:, kc * T + t0:kc * T + t0 + 1024],
                                  AF.Square)
            return l
        sqd = c.on("s", sq, waits=([("pv", last)] if last else []))

        def mm(t):
            l = None
            for tt2 in range(2):
                ps = PSA[0:1, tt2 * TT:(tt2 + 1) * TT]
                for kc in range(DC):
                    l = t.matmul(ps, c.sb["ONECOL"],
                                 SB2[:, kc * 2048 + tt2 * TT:kc * 2048 + (tt2 + 1) * TT],
                                 start=(kc == 0), stop=(kc == DC - 1))
            return l
        mmw = [("ps", sqd)] + ([("pv", c.now("pv"))] if hf else [])
        mmd = c.on("t", mm, waits=mmw, inc="pt", chain=False)

        def stats(v):
            return v.tensor_scalar(VEC[0:1, 0:1024], PSA[0:1, 0:1024],
                                   1.0 / D, eps, ALU.mult, ALU.add)
        st = c.on("v", stats, waits=[("pt", mmd)])

        def rsf(s):
            s.activation(VEC[0:1, 0:1024], VEC[0:1, 0:1024], AF.Ln)
            return s.activation(VEC[0:1, 0:1024], VEC[0:1, 0:1024], AF.Exp,
                                scale=-0.5)
        rs = c.on("s", rsf, waits=[("pv", st)])

        def bc(t):
            t.matmul(PSB[:, 0:512], c.sb["ONESF"][0:1, :], VEC[0:1, 0:512],
                     start=True, stop=True)
            return t.matmul(PSB[:, 512:1024], c.sb["ONESF"][0:1, :],
                            VEC[0:1, 512:1024], start=True, stop=True)
        bcd = c.on("t", bc, waits=[("ps", rs)], inc="pt", chain=False)

        def norm(v, t0=t0):
            l = None
            for kc in range(DC):
                d = DST[:, dst_base + kc * T + t0:dst_base + kc * T + t0 + 1024]
                v.tensor_tensor(d, XH[:, kc * T + t0:kc * T + t0 + 1024],
                                PSB[:, 0:1024], ALU.mult)
                l = v.tensor_scalar(d, d, w_col(kc), None, ALU.mult)
            return l
        last = c.on("v", norm, waits=[("pt", bcd)])
    return last


def dmabar(c, *sems):
    tgt = [(s, c.cnt[s]) for s in sems if c.cnt[s] > 0]
    for key in ("t", "v", "s", "g", "y"):
        def f(e, tgt=tgt):
            for s_, v_ in tgt:
                e.wait_ge(c.sem[s_], v_)
        c._sec(key)(f)


def _stream_lin(c, dram_ap, piece_ids, wt_off, rhs_fn, evac_fn, pool,
                slot_w=4096, nslots=2, kc_n=8, dma_sem="d4", dma_q="g",
                evac_eng="v", pre=(), post_evac=None):
    WT = c.sb["WT"]
    slot_user = [None] * nslots
    eng_prog = {"v": "pv", "s": "ps"}[evac_eng]
    for gi, pid in enumerate(piece_ids):
        sl = gi % nslots
        base = wt_off + sl * slot_w
        w = [("pt", slot_user[sl])] if slot_user[sl] else []
        dmad = c.dma(dma_q, WT[:, base:base + slot_w], dram_ap[pid],
                     waits=w + list(pre), sem=dma_sem)
        for ob in range(slot_w // (kc_n * 128)):
            for tt in range(NTT):
                ps = pool.bank()
                war = pool.war_wait()
                waits = [(dma_sem, dmad)]
                if war is not None:
                    waits.append(war)

                def mm(t, base=base, ob=ob, tt=tt, ps=ps):
                    l = None
                    for kc in range(kc_n):
                        l = t.matmul(ps, WT[:, base + kc * 512 + ob * 128:
                                            base + kc * 512 + ob * 128 + 128],
                                     rhs_fn(kc, tt), start=(kc == 0),
                                     stop=(kc == kc_n - 1))
                    return l
                mmd = c.on("t", mm, waits=waits, inc="pt", chain=False)
                slot_user[sl] = mmd

                def ev(e, gi=gi, ob=ob, tt=tt, ps=ps):
                    return evac_fn(gi, ob, tt, ps, e)
                val = c.on(evac_eng, ev, waits=[("pt", mmd)], inc=eng_prog)
                pool.push(val)
                if post_evac is not None:
                    post_evac(gi, ob, tt, val)
    return slot_user


def _mamba_conv(c, slot_ap, acc_ap, cw_base, cb_col, waits, src_ap=None):
    AUX = c.sb["AUX"]
    if src_ap is None:
        src_ap = slot_ap

    def conv(v):
        v.scalar_tensor_tensor(acc_ap, src_ap, AUX[:, cw_base + 3:cw_base + 4],
                               acc_ap, ALU.mult, ALU.bypass)
        v.scalar_tensor_tensor(acc_ap[:, 1:T], src_ap[:, 0:T - 1],
                               AUX[:, cw_base + 2:cw_base + 3],
                               acc_ap[:, 1:T], ALU.mult, ALU.add)
        v.scalar_tensor_tensor(acc_ap[:, 2:T], src_ap[:, 0:T - 2],
                               AUX[:, cw_base + 1:cw_base + 2],
                               acc_ap[:, 2:T], ALU.mult, ALU.add)
        return v.scalar_tensor_tensor(acc_ap[:, 3:T], src_ap[:, 0:T - 3],
                                      AUX[:, cw_base + 0:cw_base + 1],
                                      acc_ap[:, 3:T], ALU.mult, ALU.add)
    cd = c.on("v", conv, waits=list(waits))

    def silu(s):
        return s.activation(slot_ap, acc_ap, AF.Silu, bias=AUX[:, cb_col:cb_col + 1])
    return c.on("s", silu, waits=[("pv", cd)])


def _mamba_bphase(c, M, dc, xs_slot, xs_wait):
    PSA, PSB, VEC, TRASH, VB, AUX, WT = (c.sb["PSA"], c.sb["PSB"], c.sb["VEC"],
                                         c.sb["TRASH"], c.sb["VB"], c.sb["AUX"],
                                         c.sb["WT"])
    par = dc % 2
    dbuf, dxbuf = M["dbuf"][par], M["dxbuf"][par]
    st = M["state"]
    for th in range(2):
        def mm(t, th=th):
            l = None
            for i in range(2):
                l = t.matmul(PSB[:, i * 512:(i + 1) * 512],
                             M["wdt"][:, dc * 128:(dc + 1) * 128],
                             M["dt"][0:64, th * 1024 + i * 512:th * 1024 + (i + 1) * 512],
                             start=True, stop=True)
            return l
        w = [("pv", st["psb_free"])] if st["psb_free"] else []
        mmd = c.on("t", mm, waits=w, inc="pt", chain=False)
        s1 = dbuf[:, th * 1024:(th + 1) * 1024]
        s2 = TRASH[:, 0:1024]
        bcol = AUX[:, MA_BDT + dc:MA_BDT + dc + 1]
        nbcol = AUX[:, MA_NEGB + dc:MA_NEGB + dc + 1]

        def sp1(s, s1=s1, bcol=bcol, nbcol=nbcol):
            s.activation(s1, PSB[:, 0:1024], AF.Exp, bias=bcol)
            return s.activation(s2, PSB[:, 0:1024], AF.Exp, bias=nbcol, scale=-1.0)
        spw = [("pt", mmd)]
        if st["s2_free"]:
            spw.append(("pv", st["s2_free"]))
        if st["dbuf_free"][par]:
            spw.append(("pv", st["dbuf_free"][par]))
        v1 = c.on("s", sp1, waits=spw)

        def spmin(v, s1=s1):
            return v.tensor_tensor(s2, s2, s1, ALU.min)
        v2 = c.on("v", spmin, waits=[("ps", v1)])

        def sp2(s, s1=s1, bcol=bcol):
            s.activation(s2, s2, AF.Ln, bias=c.sb["ONECOLF"][:, 0:1])
            return s.activation(s1, PSB[:, 0:1024], AF.Relu, bias=bcol)
        v3 = c.on("s", sp2, waits=[("pv", v2)])

        def spadd(v, s1=s1):
            return v.tensor_tensor(s1, s1, s2, ALU.add)
        v4 = c.on("v", spadd, waits=[("ps", v3)])
        st["s2_free"] = v4
        st["psb_free"] = v4

    def dxm(v):
        return v.tensor_tensor(dxbuf, dbuf, xs_slot, ALU.mult)
    dxw = list(xs_wait)
    if st["dxbuf_free"][par]:
        dxw.append(("pt", st["dxbuf_free"][par]))
    dxd = c.on("v", dxm, waits=dxw)

    ps_dx = PSA[:, 0:1024]
    ps_d = PSA[:, 1024:2048]
    y_done_pv = None
    for th in range(2):
        for j in range(16):
            jp = j % 2
            bsb, hsb = M["bsb"][jp], M["hsb"][jp]
            asb = VEC[:, 0:1024] if jp == 0 else VEC[:, 1040:2064]
            rep = WT[:, j * 128:(j + 1) * 128]
            selw = WT[:, 2048 + 128 - j * 8:2048 + 256 - j * 8]

            def mdx(t, th=th, rep=rep):
                l = None
                for i in range(2):
                    l = t.matmul(ps_dx[:, i * 512:(i + 1) * 512], rep,
                                 dxbuf[:, th * 1024 + i * 512:th * 1024 + (i + 1) * 512],
                                 start=True, stop=True)
                return l
            w = [("pv", dxd if (th == 0 and j == 0) else st["b_prev"])]
            mdxd = c.on("t", mdx, waits=w, inc="pt", chain=False)

            def bmul(v, th=th, bsb=bsb):
                return v.tensor_tensor(bsb, ps_dx,
                                       M["btil"][:, th * 1024:(th + 1) * 1024],
                                       ALU.mult)
            bw = [("pt", mdxd)]
            if st["sel_prev2"]:
                bw.append(("pt", st["sel_prev2"]))
            st["b_prev"] = c.on("v", bmul, waits=bw)

            def md(t, th=th, rep=rep):
                l = None
                for i in range(2):
                    l = t.matmul(ps_d[:, i * 512:(i + 1) * 512], rep,
                                 dbuf[:, th * 1024 + i * 512:th * 1024 + (i + 1) * 512],
                                 start=True, stop=True)
                return l
            mw = [] if st["a_prev"] is None else [("ps", st["a_prev"])]
            mdd = c.on("t", md, waits=mw, inc="pt", chain=False)
            st["rep_d_last"] = mdd

            acol = AUX[:, MA_ALOG + dc * 16 + j:MA_ALOG + dc * 16 + j + 1]

            def aexp(s, asb=asb, acol=acol):
                return s.activation(asb, ps_d, AF.Exp, scale=acol)
            aw = [("pt", mdd)]
            if st["scan_prev2"]:
                aw.append(("pv", st["scan_prev2"]))
            st["a_prev"] = c.on("s", aexp, waits=aw)

            hcol = c.sb["PBUF"][:, j:j + 1]

            def scan(v, th=th, j=j, asb=asb, bsb=bsb, hsb=hsb, hcol=hcol):
                init = 0.0 if th == 0 else hcol
                v.tensor_tensor_scan(out=hsb, data0=asb, data1=bsb,
                                     initial=init, op0=ALU.mult, op1=ALU.add)
                if th == 0:
                    v.tensor_copy(hcol, hsb[:, 1023:1024])
                return v.tensor_tensor(bsb, hsb,
                                       M["ctil"][:, th * 1024:(th + 1) * 1024],
                                       ALU.mult)
            tmpd = c.on("v", scan, waits=[("ps", st["a_prev"])])
            st["scan_prev2"] = st["scan_prev1"]
            st["scan_prev1"] = tmpd

            def msel(t, th=th, j=j, selw=selw, bsb=bsb):
                l = None
                for i in range(2):
                    l = t.matmul(PSB[:, i * 512:(i + 1) * 512], selw,
                                 bsb[:, i * 512:(i + 1) * 512],
                                 start=(j == 0), stop=(j == 15))
                return l
            sw = [("pv", tmpd)]
            if th == 0 and j == 0 and st["psb_free"]:
                sw.append(("pv", st["psb_free"]))
            if th == 1 and j == 0:
                sw.append(("pv", st["yevac_prev"]))
            seld = c.on("t", msel, waits=sw, inc="pt", chain=False)
            st["sel_prev2"] = st["sel_prev1"]
            st["sel_prev1"] = seld

        ysl = xs_slot[:, th * 1024:(th + 1) * 1024]
        dcol = AUX[:, MA_D + dc:MA_D + dc + 1]

        def yev(v, th=th):
            return v.scalar_tensor_tensor(ysl, ysl, dcol, PSB[:, 0:1024],
                                          ALU.mult, ALU.add)
        yv = c.on("v", yev, waits=[("pt", st["sel_prev1"])])
        st["yevac_prev"] = yv
        y_done_pv = M["gate"](c, dc, th, ysl, yv)
        st["psb_free"] = y_done_pv
    st["dbuf_free"][par] = y_done_pv
    st["dxbuf_free"][par] = c.now("pt")
    return y_done_pv


def _mamba_bstate():
    return {"psb_free": None, "s2_free": None, "b_prev": None, "a_prev": None,
            "scan_prev1": None, "scan_prev2": None, "sel_prev1": None,
            "sel_prev2": None, "yevac_prev": None, "rep_d_last": None,
            "dbuf_free": [None, None], "dxbuf_free": [None, None]}


def emit_l1a(c, scale):
    XH, SB2, SB3, VEC, TRASH2, AUX, QS, WT = (
        c.sb["XH"], c.sb["SB2"], c.sb["SB3"], c.sb["VEC"], c.sb["TRASH2"],
        c.sb["AUX"], c.sb["QS"], c.sb["WT"])
    vo = c.vb_off

    def resxh(v):
        l = None
        for kc in range(DC):
            l = v.tensor_tensor(c.sb["RES"][:, kc * T:(kc + 1) * T],
                                c.sb["RES"][:, kc * T:(kc + 1) * T],
                                XH[:, kc * T:(kc + 1) * T], ALU.add)
        return l
    c.on("v", resxh)

    mamba_dir_load(c, c.dram["l1a_aux"])
    wd = c.dma("g", QS[0:64, 0:2048], c.dram["l1a_wdt"], sem="d4")

    emit_rms(c, lambda kc: AUX[:, MA_NW + kc:MA_NW + kc + 1], 0, "SB3")
    c.join()
    if MAMBA_TRUNC == 1:
        return

    painA = PsumPool(c, c.sb["PSA"], 4, 512)
    xpp = PsumPool(c, c.sb["PSB"], 2, 512)

    def xn_rhs(kc, tt):
        return SB3[:, kc * T + tt * TT:kc * T + (tt + 1) * TT]

    spill_xs = None
    res_stage_dma = [None] * 4
    res_ctr = [0]
    for h in range(2):
        def xc_ev(gi, ob, tt, ps, e, h=h):
            lc = gi * 4 + ob
            return e.tensor_copy(SB2[:, lc * T + tt * TT:lc * T + (tt + 1) * TT], ps)
        pre = ([("d3", spill_xs)] if spill_xs else []) + [("pt", c.now("pt"))]
        _stream_lin(c, c.dram["l1a_win"], [h * 2, h * 2 + 1], 0, xn_rhs,
                    xc_ev, painA, pre=pre)
        silu_done = []
        for lc in range(DC):
            dcg = h * 8 + lc
            cvw = [("pv", c.now("pv")), ("ps", c.now("ps"))]
            if c.now("d3") > 0:
                cvw.append(("d3", c.now("d3")))
            sd = _mamba_conv(c, SB2[:, lc * T:(lc + 1) * T], TRASH2,
                             MA_CW + dcg * 4, MA_CB + dcg, waits=cvw)
            silu_done.append(sd)
        xw = c.dma("g", c.sb["TRASH"][:, 0:768], c.dram["l1a_wxp"][h],
                   waits=[("pt", c.now("pt"))] if h else [], sem="d4")
        for tt in range(NTT):
            ps = xpp.bank()
            war = xpp.war_wait()
            w = [("d4", xw), ("ps", silu_done[-1])] + ([war] if war else [])

            def mm(t, tt=tt, ps=ps):
                l = None
                for kcl in range(DC):
                    l = t.matmul(ps[0:96, :],
                                 c.sb["TRASH"][:, kcl * 96:(kcl + 1) * 96],
                                 SB2[:, kcl * T + tt * TT:kcl * T + (tt + 1) * TT],
                                 start=(kcl == 0), stop=(kcl == DC - 1))
                return l
            mmd = c.on("t", mm, waits=w, inc="pt", chain=False)

            def ev(v, tt=tt, ps=ps, h=h):
                if h == 0:
                    v.tensor_copy(VEC[0:64, tt * TT:(tt + 1) * TT], ps[0:64, :])
                    return v.tensor_copy(VEC[64:96, tt * TT:(tt + 1) * TT],
                                         ps[64:96, :])
                v.tensor_tensor(VEC[0:64, tt * TT:(tt + 1) * TT],
                                VEC[0:64, tt * TT:(tt + 1) * TT], ps[0:64, :], ALU.add)
                return v.tensor_tensor(VEC[64:96, tt * TT:(tt + 1) * TT],
                                       VEC[64:96, tt * TT:(tt + 1) * TT],
                                       ps[64:96, :], ALU.add)
            val = c.on("v", ev, waits=[("pt", mmd)], inc="pv")
            xpp.push(val)
        def res_ev(gi, ob, tt, ps, e, h=h):
            i = res_ctr[0] % 4
            w = res_stage_dma[i]
            if w:
                e.wait_ge(c.sem["d3"], w)
            return e.tensor_copy(TRASH2[:, i * 512:(i + 1) * 512], ps)

        def res_post(gi, ob, tt, val, h=h):
            i = res_ctr[0] % 4
            cc = (h * 8 + gi * 4 + ob) * T + tt * TT
            res_stage_dma[i] = c.dma("y", c.dram["l1a_res"][:, cc:cc + TT],
                                     TRASH2[:, i * 512:(i + 1) * 512],
                                     waits=[("pv", val)], sem="d3")
            res_ctr[0] += 1
        _stream_lin(c, c.dram["l1a_wres"], [h * 2, h * 2 + 1], 0, xn_rhs,
                    res_ev, painA, post_evac=res_post,
                    pre=[("pt", c.now("pt"))])
        if h == 0:
            spill_xs = c.dma("y", c.dram["l1a_xsA"], SB2[:, :],
                             waits=[("ps", silu_done[-1])], sem="d3")
    c.join()
    dmabar(c, "d3", "d4")
    if MAMBA_TRUNC == 2:
        return

    rsl = c.dma("y", WT[:, 0:2304], c.dram["repsel"], sem="d7")

    def dtcp(v):
        return v.tensor_copy(SB3[0:64, 0:2048], VEC[0:64, 0:2048])
    c.on("v", dtcp)
    bco = c.dma("y", c.dram["l1a_bc"], VEC[64:96, 0:2048], sem="d6")
    BT, CT = SB3[:, 2048:4096], SB3[:, 4096:6144]
    for n in range(16):
        c.dma("g", BT[n * 8:(n + 1) * 8, :],
              c.dram["l1a_bc"][n:n + 1, :].broadcast_to((8, 2048)),
              waits=[("d6", bco)] if n == 0 else [], sem="d6")
        c.dma("g", CT[n * 8:(n + 1) * 8, :],
              c.dram["l1a_bc"][16 + n:17 + n, :].broadcast_to((8, 2048)),
              sem="d6")
    c.join()
    dmabar(c, "d6", "d7")
    if MAMBA_TRUNC == 3:
        return

    M = {"dt": SB3, "wdt": QS, "btil": BT, "ctil": CT,
         "dbuf": [TRASH2, SB3[:, 12288:14336]],
         "dxbuf": [SB3[:, 6144:8192], SB3[:, 14336:16384]],
         "bsb": [SB3[:, 8192:9216], SB3[:, 9216:10240]],
         "hsb": [SB3[:, 10240:11264], SB3[:, 11264:12288]],
         "state": _mamba_bstate()}

    def gate(c2, dc, th, ysl, yv):
        par = dc % 2
        gsl = M["dbuf"][par][:, th * 1024:(th + 1) * 1024]
        gd = c2.dma("y", gsl, c.dram["l1a_res"][:, dc * T + th * 1024:
                                               dc * T + (th + 1) * 1024],
                    waits=[("pt", M["state"]["rep_d_last"]), ("pv", yv)],
                    sem="d5")

        def sg(s):
            return s.activation(gsl, gsl, AF.Silu)
        sgd = c2.on("s", sg, waits=[("d5", gd)])

        def gm(v):
            return v.tensor_tensor(ysl, ysl, gsl, ALU.mult)
        return c2.on("v", gm, waits=[("ps", sgd)])
    M["gate"] = gate

    def y_rhs(kc, tt):
        return SB2[:, kc * T + tt * TT:kc * T + (tt + 1) * TT]

    woutp = PsumPool(c, c.sb["PSA"], 4, 512)
    for halfsel in (1, 0):
        xs_ld = [None] * DC
        if halfsel == 0:
            for lc in range(DC):
                xs_ld[lc] = c.dma("y", SB2[:, lc * T:(lc + 1) * T],
                                  c.dram["l1a_xsA"][:, lc * T:(lc + 1) * T],
                                  waits=[("pt", c.now("pt"))], sem="d5")
        for lc in range(DC):
            dc = halfsel * 8 + lc
            xw = [("d5", xs_ld[lc])] if xs_ld[lc] else []
            _mamba_bphase(c, M, dc, SB2[:, lc * T:(lc + 1) * T], xw)
        c.join()
        if MAMBA_TRUNC == 4 and halfsel == 1:
            return

        def wo_ev(gi, ob, tt, ps, e, halfsel=halfsel):
            oc = gi * 4 + ob
            r = res_slice(c, oc, tt)
            return e.tensor_tensor(r, r, ps, ALU.add)
        _stream_lin(c, c.dram["l1a_wout"], [halfsel * 2, halfsel * 2 + 1],
                    2560, y_rhs, wo_ev, woutp, nslots=1)
        c.join()


def emit_l3a(c, scale):
    XH, SB2, SB3, VEC, TRASH2, AUX, QS, WT = (
        c.sb["XH"], c.sb["SB2"], c.sb["SB3"], c.sb["VEC"], c.sb["TRASH2"],
        c.sb["AUX"], c.sb["QS"], c.sb["WT"])

    def resxh(v):
        l = None
        for kc in range(DC):
            l = v.tensor_tensor(c.sb["RES"][:, kc * T:(kc + 1) * T],
                                c.sb["RES"][:, kc * T:(kc + 1) * T],
                                XH[:, kc * T:(kc + 1) * T], ALU.add)
        return l
    c.on("v", resxh)

    mamba_dir_load(c, c.dram["l3a_aux_f"])
    emit_rms(c, lambda kc: AUX[:, MA_NW + kc:MA_NW + kc + 1], 0, "SB3")
    c.join()

    painA = PsumPool(c, c.sb["PSA"], 4, 512)
    xpp = PsumPool(c, c.sb["PSB"], 2, 512)

    def xn_rhs(kc, tt):
        return SB3[:, kc * T + tt * TT:kc * T + (tt + 1) * TT]

    spill_xi = None
    z_stage_dma = [None] * 4
    z_ctr = [0]
    for h in range(2):
        def xi_ev(gi, ob, tt, ps, e):
            lc = gi * 4 + ob
            return e.tensor_copy(SB2[:, lc * T + tt * TT:lc * T + (tt + 1) * TT], ps)
        pre = ([("d3", spill_xi)] if spill_xi else []) + [("pt", c.now("pt"))]
        _stream_lin(c, c.dram["l3a_wipx"], [h * 2, h * 2 + 1], 0, xn_rhs,
                    xi_ev, painA, pre=pre)
        spill_xi = c.dma("y", c.dram["l3a_xi"][:, h * 16384:(h + 1) * 16384],
                         SB2[:, :], waits=[("pv", c.now("pv"))], sem="d3")

        def z_ev(gi, ob, tt, ps, e):
            i = z_ctr[0] % 4
            w = z_stage_dma[i]
            if w:
                e.wait_ge(c.sem["d3"], w)
            return e.tensor_copy(TRASH2[:, i * 512:(i + 1) * 512], ps)

        def z_post(gi, ob, tt, val, h=h):
            i = z_ctr[0] % 4
            cc = (h * 8 + gi * 4 + ob) * T + tt * TT
            z_stage_dma[i] = c.dma("y", c.dram["l3a_z"][:, cc:cc + TT],
                                   TRASH2[:, i * 512:(i + 1) * 512],
                                   waits=[("pv", val)], sem="d3")
            z_ctr[0] += 1
        _stream_lin(c, c.dram["l3a_wipz"], [h * 2, h * 2 + 1], 0, xn_rhs,
                    z_ev, painA, post_evac=z_post, pre=[("pt", c.now("pt"))])
    c.join()
    dmabar(c, "d3", "d4")
    rsl = c.dma("y", WT[:, 0:2304], c.dram["repsel"], sem="d7")

    for dr in ("b", "f"):
        mamba_dir_load(c, c.dram[f"l3a_aux_{dr}"])
        wd = c.dma("g", QS[0:64, 0:2048], c.dram[f"l3a_wdt_{dr}"], sem="d4")
        XTMP = SB3[:, 0:2048]
        prev_rcv = None
        for h in range(2):
            silu_done = []
            for lc in range(DC):
                dc = h * 8 + lc
                slot = SB2[:, lc * T:(lc + 1) * T]
                xiw = [("d3", c.now("d3")), ("pt", c.now("pt")),
                       ("pv", c.now("pv"))]
                if dr == "b":
                    xl = c.dma("y", XTMP, c.dram["l3a_xi"][:, dc * T:(dc + 1) * T],
                               waits=(xiw if prev_rcv is None else
                                      [("pv", prev_rcv)] + xiw[:1]), sem="d5")

                    def rcv(v, slot=slot):
                        return v.tensor_copy(slot, XTMP[:, ::-1])
                    prev_rcv = c.on("v", rcv, waits=[("d5", xl)])
                else:
                    xl = c.dma("y", slot, c.dram["l3a_xi"][:, dc * T:(dc + 1) * T],
                               waits=xiw, sem="d5")
                cvw = [("pv", c.now("pv")), ("ps", c.now("ps"))]
                if dr == "f":
                    cvw.append(("d5", xl))
                if c.now("d3") > 0:
                    cvw.append(("d3", c.now("d3")))
                sd = _mamba_conv(c, slot, TRASH2, MA_CW + dc * 4, MA_CB + dc,
                                 waits=cvw)
                silu_done.append(sd)
            xw = c.dma("g", c.sb["TRASH"][:, 0:768], c.dram[f"l3a_wxp_{dr}"][h],
                       waits=[("pt", c.now("pt"))], sem="d4")
            for tt in range(NTT):
                ps = xpp.bank()
                war = xpp.war_wait()
                w = [("d4", xw), ("ps", silu_done[-1])] + ([war] if war else [])

                def mm(t, tt=tt, ps=ps):
                    l = None
                    for kcl in range(DC):
                        l = t.matmul(ps[0:96, :],
                                     c.sb["TRASH"][:, kcl * 96:(kcl + 1) * 96],
                                     SB2[:, kcl * T + tt * TT:kcl * T + (tt + 1) * TT],
                                     start=(kcl == 0), stop=(kcl == DC - 1))
                    return l
                mmd = c.on("t", mm, waits=w, inc="pt", chain=False)

                def ev(v, tt=tt, ps=ps, h=h):
                    if h == 0:
                        v.tensor_copy(VEC[0:64, tt * TT:(tt + 1) * TT], ps[0:64, :])
                        return v.tensor_copy(VEC[64:96, tt * TT:(tt + 1) * TT],
                                             ps[64:96, :])
                    v.tensor_tensor(VEC[0:64, tt * TT:(tt + 1) * TT],
                                    VEC[0:64, tt * TT:(tt + 1) * TT], ps[0:64, :],
                                    ALU.add)
                    return v.tensor_tensor(VEC[64:96, tt * TT:(tt + 1) * TT],
                                           VEC[64:96, tt * TT:(tt + 1) * TT],
                                           ps[64:96, :], ALU.add)
                val = c.on("v", ev, waits=[("pt", mmd)], inc="pv")
                xpp.push(val)
            if h == 0:
                c.dma("y", c.dram["l1a_xsA"], SB2[:, :],
                      waits=[("ps", silu_done[-1])], sem="d3")
        c.join()
        dmabar(c, "d3", "d4", "d5", "d7")

        def dtcp(v):
            return v.tensor_copy(SB3[0:64, 0:2048], VEC[0:64, 0:2048])
        c.on("v", dtcp)
        bco = c.dma("y", c.dram["l1a_bc"], VEC[64:96, 0:2048], sem="d6")
        BT, CT = SB3[:, 2048:4096], SB3[:, 4096:6144]
        for n in range(16):
            c.dma("g", BT[n * 8:(n + 1) * 8, :],
                  c.dram["l1a_bc"][n:n + 1, :].broadcast_to((8, 2048)),
                  waits=[("d6", bco)] if n == 0 else [], sem="d6")
            c.dma("g", CT[n * 8:(n + 1) * 8, :],
                  c.dram["l1a_bc"][16 + n:17 + n, :].broadcast_to((8, 2048)),
                  sem="d6")
        c.join()
        dmabar(c, "d6")

        M = {"dt": SB3, "wdt": QS, "btil": BT, "ctil": CT,
             "dbuf": [TRASH2, SB3[:, 12288:14336]],
             "dxbuf": [SB3[:, 6144:8192], SB3[:, 14336:16384]],
             "bsb": [SB3[:, 8192:9216], SB3[:, 9216:10240]],
             "hsb": [SB3[:, 10240:11264], SB3[:, 11264:12288]],
             "state": _mamba_bstate()}

        def gate(c2, dc, th, ysl, yv, dr=dr):
            par = dc % 2
            gsl = M["dbuf"][par][:, th * 1024:(th + 1) * 1024]
            sth = (1 - th) if dr == "b" else th
            gd = c2.dma("y", gsl,
                        c.dram["l3a_z"][:, dc * T + sth * 1024:dc * T + (sth + 1) * 1024],
                        waits=[("pt", M["state"]["rep_d_last"]), ("pv", yv)],
                        sem="d5")

            def sg(s, gsl=gsl):
                return s.activation(gsl, gsl, AF.Silu)
            sgd = c2.on("s", sg, waits=[("d5", gd)])

            def gm(v, gsl=gsl, dr=dr):
                src = gsl[:, ::-1] if dr == "b" else gsl
                return v.tensor_tensor(ysl, ysl, src, ALU.mult)
            return c2.on("v", gm, waits=[("ps", sgd)])
        M["gate"] = gate

        woutp = PsumPool(c, c.sb["PSA"], 4, 512)
        for halfsel in (1, 0):
            xs_ld = [None] * DC
            if halfsel == 0:
                for lc in range(DC):
                    xs_ld[lc] = c.dma("y", SB2[:, lc * T:(lc + 1) * T],
                                      c.dram["l1a_xsA"][:, lc * T:(lc + 1) * T],
                                      waits=[("pt", c.now("pt")),
                                             ("d3", c.now("d3"))], sem="d5")
            for lc in range(DC):
                dc = halfsel * 8 + lc
                xw = [("d5", xs_ld[lc])] if xs_ld[lc] else []
                yd = _mamba_bphase(c, M, dc, SB2[:, lc * T:(lc + 1) * T], xw)
                if dr == "b":
                    c.dma("y", c.dram["l3a_yb"][:, dc * T:(dc + 1) * T],
                          SB2[:, lc * T:(lc + 1) * T], waits=[("pv", yd)],
                          sem="d3")
            c.join()
            if dr == "f":
                for lc in range(DC):
                    dc = halfsel * 8 + lc
                    par = dc % 2
                    stg = M["dbuf"][par]
                    yl = c.dma("y", stg, c.dram["l3a_yb"][:, dc * T:(dc + 1) * T],
                               waits=[("pv", c.now("pv"))], sem="d5")

                    def comb(v, lc=lc, stg=stg):
                        slot = SB2[:, lc * T:(lc + 1) * T]
                        v.tensor_tensor(slot, slot, stg[:, ::-1], ALU.add)
                        return v.tensor_scalar(slot, slot, 0.5, None, ALU.mult)
                    c.on("v", comb, waits=[("d5", yl)])
                c.join()

                def wo_ev(gi, ob, tt, ps, e):
                    oc = gi * 4 + ob
                    r = res_slice(c, oc, tt)
                    return e.tensor_tensor(r, r, ps, ALU.add)

                def y_rhs(kc, tt):
                    return SB2[:, kc * T + tt * TT:kc * T + (tt + 1) * TT]
                _stream_lin(c, c.dram["l3a_wop"], [halfsel * 2, halfsel * 2 + 1],
                            2560, y_rhs, wo_ev, woutp, nslots=1)
            c.join()
        dmabar(c, "d3", "d5")


L2EPS = 1e-6
LA_CW, LA_CB, LA_NIN, LA_AB, LA_NAB, LA_BB = 0, 32, 40, 48, 56, 64
LB_U1B, LB_U2B, LB_FFW = 0, 1, 9


def emit_l2apre(c, scale):
    XH, SB2, SB3, AUX, TRASH, TRASH2, WT, VEC = (
        c.sb["XH"], c.sb["SB2"], c.sb["SB3"], c.sb["AUX"], c.sb["TRASH"],
        c.sb["TRASH2"], c.sb["WT"], c.sb["VEC"])
    ax = c.dma("y", AUX[:, 0:72], c.dram["l2a_aux1"], sem="d7")

    def negab(v):
        return v.tensor_scalar(AUX[:, LA_NAB:LA_NAB + 8],
                               AUX[:, LA_AB:LA_AB + 8], -1.0, None, ALU.mult)
    c.on("v", negab, waits=[("d7", ax)])
    emit_rms(c, lambda kc: AUX[:, LA_NIN + kc:LA_NIN + kc + 1], 0, "SB3",
             eps=L2EPS)
    c.join()
    for lc in range(DC):
        _mamba_conv(c, SB2[:, lc * T:(lc + 1) * T], TRASH2,
                    LA_CW + lc * 4, LA_CB + lc,
                    waits=[("pv", c.now("pv")), ("ps", c.now("ps"))],
                    src_ap=SB3[:, lc * T:(lc + 1) * T])
    c.join()

    def h_rhs(kc, tt):
        return SB2[:, kc * T + tt * TT:kc * T + (tt + 1) * TT]
    pp = PsumPool(c, c.sb["PSA"], 4, 512)
    st_ctr = [0]
    st_dma = [None] * 4
    for wn, outn in (("l2a_qw", "l2a_q"), ("l2a_kw", "l2a_k"),
                     ("l2a_vw", "l2a_v")):
        for piece in range(2):
            pw = [("pt", c.now("pt"))]
            dmad = c.dma("g", WT[:, piece * 4096:(piece + 1) * 4096],
                         c.dram[wn][:, piece * 4096:(piece + 1) * 4096],
                         waits=pw, sem="d4")
            for fcl in range(4):
                fc = piece * 4 + fcl
                for tt in range(NTT):
                    ps = pp.bank()
                    war = pp.war_wait()
                    w = [("d4", dmad)] + ([war] if war else [])

                    def mm(t, piece=piece, fcl=fcl, tt=tt, ps=ps):
                        l = None
                        for kc in range(DC):
                            l = t.matmul(ps, WT[:, piece * 4096 + fcl * 1024 + kc * 128:
                                                piece * 4096 + fcl * 1024 + kc * 128 + 128],
                                         h_rhs(kc, tt), start=(kc == 0),
                                         stop=(kc == DC - 1))
                        return l
                    mmd = c.on("t", mm, waits=w, inc="pt", chain=False)
                    i = st_ctr[0] % 4
                    stg = TRASH2[:, i * 512:(i + 1) * 512]
                    sw = [("pt", mmd)]
                    if st_dma[i]:
                        sw.append(("d2", st_dma[i]))

                    def ev(v, stg=stg, ps=ps):
                        return v.tensor_copy(stg, ps)
                    evd = c.on("v", ev, waits=sw)
                    c.dma("y", c.dram[outn][:, (2 * fc) * T + tt * TT:
                                            (2 * fc) * T + (tt + 1) * TT],
                          stg[0:64, :], waits=[("pv", evd)], sem="d2")
                    st_dma[i] = c.dma("y", c.dram[outn][:, (2 * fc + 1) * T + tt * TT:
                                                        (2 * fc + 1) * T + (tt + 1) * TT],
                                      stg[64:128, :], sem="d2")
                    pp.push(evd)
                    st_ctr[0] += 1
    for piece in range(2):
        pw = [("pt", c.now("pt"))]
        dmad = c.dma("g", WT[:, piece * 4096:(piece + 1) * 4096],
                     c.dram["l2a_aw"][:, piece * 4096:(piece + 1) * 4096],
                     waits=pw, sem="d4")
        for fcl in range(4):
            fc = piece * 4 + fcl
            for tt in range(NTT):
                ps = pp.bank()
                war = pp.war_wait()
                w = [("d4", dmad)] + ([war] if war else [])

                def mm(t, piece=piece, fcl=fcl, tt=tt, ps=ps):
                    l = None
                    for kc in range(DC):
                        l = t.matmul(ps, WT[:, piece * 4096 + fcl * 1024 + kc * 128:
                                            piece * 4096 + fcl * 1024 + kc * 128 + 128],
                                     h_rhs(kc, tt), start=(kc == 0),
                                     stop=(kc == DC - 1))
                    return l
                mmd = c.on("t", mm, waits=w, inc="pt", chain=False)
                i = st_ctr[0] % 4
                stg = TRASH2[:, i * 512:(i + 1) * 512]
                s2 = TRASH[:, 0:512]
                abc = AUX[:, LA_AB + fc:LA_AB + fc + 1]
                nbc = AUX[:, LA_NAB + fc:LA_NAB + fc + 1]

                def sp1(s, stg=stg, abc=abc, nbc=nbc, ps=ps):
                    s.activation(stg, ps, AF.Exp, bias=abc)
                    return s.activation(s2, ps, AF.Exp, bias=nbc, scale=-1.0)
                sw = [("pt", mmd), ("ps", c.now("ps")), ("pv", c.now("pv"))]
                if st_dma[i]:
                    sw.append(("d2", st_dma[i]))
                v1 = c.on("s", sp1, waits=sw)

                def spv(v, stg=stg, ps=ps, nbc=nbc):
                    return v.tensor_tensor(s2, s2, stg, ALU.min)
                v2 = c.on("v", spv, waits=[("ps", v1)])

                def sp2(s, stg=stg, ps=ps, nbc=nbc):
                    s.activation(s2, s2, AF.Ln, bias=c.sb["ONECOLF"][:, 0:1])
                    return s.activation(stg, ps, AF.Relu, bias=nbc, scale=-1.0)
                v3 = c.on("s", sp2, waits=[("pv", v2)])

                def sp3(v, stg=stg):
                    return v.tensor_tensor(stg, stg, s2, ALU.add)
                v4 = c.on("v", sp3, waits=[("ps", v3)])
                st_dma[i] = c.dma("y", c.dram["l2a_sp"][:, (2 * fc) * T + tt * TT:
                                                         (2 * fc) * T + (tt + 1) * TT],
                                  stg[0:64, :], waits=[("pv", v4)], sem="d2")
                st_dma[i] = c.dma("y", c.dram["l2a_sp"][:, (2 * fc + 1) * T + tt * TT:
                                                        (2 * fc + 1) * T + (tt + 1) * TT],
                                  stg[64:128, :], sem="d2")
                pp.push(("d2", st_dma[i]))
                st_ctr[0] += 1
    bw = c.dma("g", TRASH[:, 512:640], c.dram["l2a_bw"], sem="d4")
    bev = None
    for tt in range(NTT):
        def mm(t, tt=tt):
            l = None
            for kc in range(DC):
                l = t.matmul(c.sb["PSB"][0:16, 0:512],
                             TRASH[:, 512 + kc * 16:512 + (kc + 1) * 16],
                             h_rhs(kc, tt), start=(kc == 0), stop=(kc == DC - 1))
            return l
        w = [("d4", bw)]
        if bev:
            w.append(("pv", bev))
        mmd = c.on("t", mm, waits=w, inc="pt", chain=False)

        def bsig(s, tt=tt):
            return s.activation(VEC[0:16, tt * TT:(tt + 1) * TT],
                                c.sb["PSB"][0:16, 0:512], AF.Sigmoid,
                                bias=AUX[0:16, LA_BB:LA_BB + 1])
        bsd = c.on("s", bsig, waits=[("pt", mmd)])

        def bcp(v, tt=tt):
            return v.tensor_copy(VEC[0:16, tt * TT:(tt + 1) * TT],
                                 VEC[0:16, tt * TT:(tt + 1) * TT])
        bev = c.on("v", bcp, waits=[("ps", bsd)])
    c.dma("y", c.dram["l2a_beta"], VEC[0:16, 0:2048], waits=[("pv", bev)],
          sem="d3")
    c.join()
    dmabar(c, "d2", "d3", "d4")


def build_l2a_seg():
    nc = bass.Bass("TRN2", target_bir_lowering=False,
                   detect_race_conditions=DETECT_RACES)
    dram = {}
    for nm in ("sg_q", "sg_k", "sg_v", "sg_sp"):
        dram[nm] = nc.dram_tensor(nm, [64, 2048], BF16, kind="ExternalInput").ap()
    dram["sg_beta"] = nc.dram_tensor("sg_beta", [128, 16], F32, kind="ExternalInput").ap()
    dram["sg_sin"] = nc.dram_tensor("sg_sin", [64, 1024], F32, kind="ExternalInput").ap()
    dram["sg_hnw"] = nc.dram_tensor("sg_hnw", [64, 16], F32, kind="ExternalInput").ap()
    dram["sg_masks"] = nc.dram_tensor("sg_masks", [128, 256], BF16, kind="ExternalInput").ap()
    dram["sg_ib"] = nc.dram_tensor("sg_ib", [64, 64], BF16, kind="ExternalInput").ap()
    dram["sg_ib2"] = nc.dram_tensor("sg_ib2", [128, 128], BF16, kind="ExternalInput").ap()
    dram["sg_sout"] = nc.dram_tensor("sg_sout", [64, 1024], F32, kind="ExternalOutput").ap()
    dram["sg_dx"] = nc.dram_tensor("sg_dx", [128, 128], BF16, kind="Internal").ap()
    dram["sg_dr"] = nc.dram_tensor("sg_dr", [128, 64], BF16, kind="Internal").ap()
    dram["sg_dm"] = nc.dram_tensor("sg_dm", [128, 64], BF16, kind="Internal").ap()
    dram["sg_do"] = nc.dram_tensor("sg_do", [16, 512], F32, kind="Internal").ap()
    dram["sg_o"] = nc.dram_tensor("sg_o", [128, 1024], BF16, kind="ExternalOutput").ap()
    stack = ExitStack()
    with stack:
        sb = {}

        def alloc(name, shape, dt):
            sb[name] = stack.enter_context(nc.sbuf_tensor(name, shape, dt)).ap()
        for nm in ("Qa", "Ka", "Va", "SPa"):
            alloc(nm, [64, 2048], BF16)
        alloc("BETA", [128, 16], F32)
        alloc("SST", [64, 1024], F32)
        alloc("HNW", [64, 16], F32)
        alloc("MASKS", [128, 256], BF16)
        alloc("IB", [64, 64], BF16)
        alloc("IB2", [128, 128], BF16)
        alloc("CSA", [64, 128], F32)
        alloc("CSB", [64, 128], F32)
        alloc("EP", [64, 128], BF16)
        alloc("EM", [64, 128], BF16)
        alloc("EMC", [64, 8], F32)
        alloc("ERC", [64, 8], F32)
        alloc("KK", [64, 128], BF16)
        alloc("RT", [64, 128], BF16)
        alloc("BCOL", [128, 1], F32)
        alloc("NBCOL", [128, 1], F32)
        alloc("Asb", [128, 128], BF16)
        alloc("ATsb", [128, 128], BF16)
        alloc("A2sb", [128, 128], BF16)
        alloc("A2Tsb", [128, 128], BF16)
        alloc("A4sb", [128, 128], BF16)
        alloc("A4Tsb", [128, 128], BF16)
        alloc("A8Tsb", [128, 128], BF16)
        alloc("MqTsb", [128, 128], BF16)
        alloc("Xsb", [128, 128], BF16)
        alloc("MQVsb", [128, 64], BF16)
        alloc("MQYsb", [128, 64], BF16)
        alloc("QCsb", [64, 128], BF16)
        alloc("Xfl", [16, 1024], BF16)
        alloc("RTP", [128, 64], BF16)
        alloc("RFL", [16, 512], BF16)
        alloc("MQVfl", [16, 512], BF16)
        alloc("OFL", [16, 512], F32)
        alloc("ASTsb", [64, 64], BF16)
        alloc("PCsb", [64, 64], BF16)
        alloc("S0sb", [64, 64], BF16)
        alloc("OTP", [128, 64], BF16)
        alloc("SQ", [128, 64], F32)
        alloc("RS", [128, 1], F32)
        alloc("OPAIR", [128, 1024], BF16)
        alloc("ONEF", [128, 1], F32)
        p1 = stack.enter_context(nc.psum_tensor("P1", [128, 512], F32)).ap()
        p2 = stack.enter_context(nc.psum_tensor("P2", [128, 512], F32)).ap()
        p3 = stack.enter_context(nc.psum_tensor("P3", [128, 512], F32)).ap()
        pt = stack.enter_context(nc.psum_tensor("PT", [128, 1024], BF16)).ap()

        with nc.Block() as block:
            c = Ctx(nc, block, stack)
            c.dram = dram
            c.sb = sb
            Q, K, V128, SP = sb["Qa"], sb["Ka"], sb["Va"], sb["SPa"]

            def ginit(g):
                return g.memset(sb["ONEF"][:, :], 1.0)
            gi = c.on("g", ginit, inc="pg")
            ld = 0
            for nm, dst in (("sg_q", Q), ("sg_k", K), ("sg_v", V128),
                            ("sg_sp", SP)):
                ld = c.dma("y", dst, dram[nm], sem="di")
            ld = c.dma("y", sb["BETA"], dram["sg_beta"], sem="di")
            ld = c.dma("y", sb["SST"], dram["sg_sin"], sem="di")
            ld = c.dma("y", sb["HNW"], dram["sg_hnw"], sem="di")
            ld = c.dma("y", sb["MASKS"], dram["sg_masks"], sem="di")
            ld = c.dma("y", sb["IB"], dram["sg_ib"], sem="di")
            ld = c.dma("y", sb["IB2"], dram["sg_ib2"], sem="di")
            for key in ("t", "v", "s", "g"):
                def wall(e):
                    e.wait_ge(c.sem["di"], ld)
                    e.wait_ge(c.sem["pg"], gi)
                c._sec(key)(wall)

            MSTR = sb["MASKS"][:, 0:128]
            MINCT = sb["MASKS"][:, 128:256]
            for h in range(16):
                q = Q[:, h * 128:(h + 1) * 128]
                k = K[:, h * 128:(h + 1) * 128]
                vv_ = V128[:, h * 128:(h + 1) * 128]
                sp = SP[:, h * 128:(h + 1) * 128]
                CSA, CSB = sb["CSA"], sb["CSB"]
                EP, EM, EMC, ERC = sb["EP"], sb["EM"], sb["EMC"], sb["ERC"]
                KK, RT = sb["KK"], sb["RT"]

                def cums(v, sp=sp):
                    v.tensor_copy(CSA, sp)
                    src, dst = CSA, CSB
                    for lvl in (1, 2, 4, 8):
                        s3 = src.rearrange("p (c t) -> p c t", c=8)
                        d3 = dst.rearrange("p (c t) -> p c t", c=8)
                        v.tensor_tensor(d3[:, :, lvl:16], s3[:, :, lvl:16],
                                        s3[:, :, 0:16 - lvl], ALU.add)
                        v.tensor_copy(d3[:, :, 0:lvl], s3[:, :, 0:lvl])
                        src, dst = dst, src
                    c3 = src.rearrange("p (c t) -> p c t", c=8)
                    v.tensor_tensor(dst.rearrange("p (c t) -> p c t", c=8),
                                    c3[:, :, 7:8].broadcast_to((64, 8, 16)),
                                    c3, ALU.subtract)
                    return v.tensor_tensor(ERC, src[:, 7::16],
                                           src[:, 15::16], ALU.subtract)
                cu = c.on("v", cums, waits=[("ps", c.now("ps")),
                                            ("pt", c.now("pt"))])

                def exps(s):
                    s.activation(EP, CSB, AF.Exp)
                    s.activation(EM, CSB, AF.Exp, scale=-1.0)
                    s.activation(EMC, CSA[:, 7::16], AF.Exp, scale=-1.0)
                    return s.activation(ERC, ERC, AF.Exp)
                ex = c.on("s", exps, waits=[("pv", cu)])

                def scal(v, q=q, k=k, h=h):
                    v.tensor_tensor(KK, k, EP, ALU.mult)
                    v.tensor_tensor(RT, k, EM, ALU.mult)
                    v.tensor_tensor(q, q, EP, ALU.mult)
                    v.tensor_copy(sb["BCOL"], sb["BETA"][:, h:h + 1])
                    return v.tensor_scalar(sb["NBCOL"], sb["BCOL"], -1.0, None,
                                           ALU.mult)
                sc = c.on("v", scal, waits=[("ps", ex)])

                def mms(t, q=q):
                    t.matmul(p1[:, 0:128], KK, RT, start=True, stop=True)
                    return t.matmul(p2[:, 0:128], RT, q, start=True, stop=True)
                md = c.on("t", mms, waits=[("pv", sc)], inc="pt", chain=False)

                def evA(v):
                    v.scalar_tensor_tensor(sb["Asb"], p1[:, 0:128], sb["NBCOL"],
                                           MSTR, ALU.mult, ALU.mult)
                    return v.tensor_tensor(sb["MqTsb"], p2[:, 0:128], MINCT,
                                           ALU.mult)
                ea = c.on("v", evA, waits=[("pt", md)])

                def trs(t, vv_=vv_):
                    t.transpose(pt[:, 0:128], sb["Asb"], sb["IB2"])
                    t.transpose(pt[:, 256:320], vv_, sb["IB"])
                    t.transpose(pt[:, 896:960], RT, sb["IB"])
                    return t.transpose(pt[:, 384:448], KK, sb["IB"])
                tr = c.on("t", trs, waits=[("pv", ea)], inc="pt", chain=False)

                def evT(v):
                    v.tensor_copy(sb["ATsb"], pt[:, 0:128])
                    v.tensor_copy(sb["RTP"], pt[:, 896:960])
                    v.scalar_tensor_tensor(sb["Xsb"][:, 0:64], pt[:, 256:320],
                                           sb["BCOL"], sb["Xsb"][:, 0:64],
                                           ALU.mult, ALU.bypass)
                    return v.scalar_tensor_tensor(sb["Xsb"][:, 64:128],
                                                  pt[:, 384:448], sb["BCOL"],
                                                  sb["Xsb"][:, 64:128],
                                                  ALU.mult, ALU.bypass)
                et = c.on("v", evT, waits=[("pt", tr)])

                def mp1(t):
                    t.matmul(p1[:, 0:128], sb["ATsb"], sb["Asb"], start=True, stop=True)
                    return t.matmul(p2[:, 0:128], sb["Asb"], sb["ATsb"], start=True, stop=True)
                m1 = c.on("t", mp1, waits=[("pv", et)], inc="pt", chain=False)

                def ev1(v):
                    v.tensor_copy(sb["A2sb"], p1[:, 0:128])
                    return v.tensor_copy(sb["A2Tsb"], p2[:, 0:128])
                e1 = c.on("v", ev1, waits=[("pt", m1)])

                def mp2(t):
                    t.matmul(p1[:, 0:128], sb["A2Tsb"], sb["A2sb"], start=True, stop=True)
                    return t.matmul(p2[:, 0:128], sb["A2sb"], sb["A2Tsb"], start=True, stop=True)
                m2 = c.on("t", mp2, waits=[("pv", e1)], inc="pt", chain=False)

                def ev2(v):
                    v.tensor_copy(sb["A4sb"], p1[:, 0:128])
                    return v.tensor_copy(sb["A4Tsb"], p2[:, 0:128])
                e2 = c.on("v", ev2, waits=[("pt", m2)])

                def mp3(t):
                    return t.matmul(p1[:, 0:128], sb["A4sb"], sb["A4Tsb"],
                                    start=True, stop=True)
                m3 = c.on("t", mp3, waits=[("pv", e2)], inc="pt", chain=False)

                def ev3(v):
                    return v.tensor_copy(sb["A8Tsb"], p1[:, 0:128])
                e3 = c.on("v", ev3, waits=[("pt", m3)])

                prevv = e3
                for pw_ in ("ATsb", "A2Tsb", "A4Tsb", "A8Tsb"):
                    def app_mm(t, pw_=pw_):
                        return t.matmul(p2[:, 0:128], sb[pw_], sb["Xsb"],
                                        start=True, stop=True)
                    am = c.on("t", app_mm, waits=[("pv", prevv)], inc="pt",
                              chain=False)

                    def app_add(v):
                        return v.tensor_tensor(sb["Xsb"], sb["Xsb"],
                                               p2[:, 0:128], ALU.add)
                    prevv = c.on("v", app_add, waits=[("pt", am)])

                def mqx(t):
                    return t.matmul(p1[:, 0:128], sb["MqTsb"], sb["Xsb"],
                                    start=True, stop=True)
                mq = c.on("t", mqx, waits=[("pv", prevv)], inc="pt", chain=False)

                def evq(v):
                    v.tensor_copy(sb["MQVsb"], p1[:, 0:64])
                    return v.tensor_copy(sb["MQYsb"], p1[:, 64:128])
                eq = c.on("v", evq, waits=[("pt", mq)])

                def trq(t):
                    return t.transpose(pt[0:64, 512:640], sb["MQYsb"], sb["IB2"])
                tq = c.on("t", trq, waits=[("pv", eq)], inc="pt", chain=False)

                def evqc(v, q=q):
                    return v.tensor_tensor(sb["QCsb"], q, pt[0:64, 512:640],
                                           ALU.subtract)
                qc = c.on("v", evqc, waits=[("pt", tq)])

                c.dma("y", c.dram["sg_dx"], sb["Xsb"], waits=[("pv", prevv)],
                      sem="d1")
                f1 = c.dma("y", sb["Xfl"].rearrange("t (c n) -> t c n", c=8),
                           c.dram["sg_dx"].rearrange("(c t) n -> t c n", c=8),
                           sem="d1")
                c.dma("y", c.dram["sg_dr"], sb["RTP"], waits=[("pv", et)],
                      sem="d1")
                f2_ = c.dma("y", sb["RFL"].rearrange("t (c d) -> t c d", c=8),
                            c.dram["sg_dr"].rearrange("(c t) d -> t c d", c=8),
                            sem="d1")
                c.dma("y", c.dram["sg_dm"], sb["MQVsb"], waits=[("pv", eq)],
                      sem="d1")
                f3_ = c.dma("y", sb["MQVfl"].rearrange("t (c n) -> t c n", c=8),
                            c.dram["sg_dm"].rearrange("(c t) n -> t c n", c=8),
                            sem="d1")

                for ch in range(8):
                    def pcry(t, ch=ch):
                        t.matmul(p3[0:64, 0:128], sb["RFL"][:, ch * 64:(ch + 1) * 64],
                                 sb["Xfl"][:, ch * 128:(ch + 1) * 128],
                                 start=True, stop=True)
                        return t.matmul(p3[0:64, 128:192],
                                        sb["Xfl"][:, ch * 128 + 64:(ch + 1) * 128],
                                        sb["RFL"][:, ch * 64:(ch + 1) * 64],
                                        start=True, stop=True)
                    w = [("d1", f2_), ("d1", f1), ("pv", c.now("pv"))]
                    pm = c.on("t", pcry, waits=w, inc="pt", chain=False)

                    def evpc(v, ch=ch):
                        v.tensor_copy(sb["PCsb"], p3[0:64, 0:64])
                        return v.tensor_tensor(sb["ASTsb"], sb["IB"],
                                               p3[0:64, 128:192], ALU.subtract)
                    pe = c.on("v", evpc, waits=[("pt", pm)])

                    def s0scale(s, ch=ch, h=h):
                        return s.activation(sb["S0sb"],
                                            sb["SST"][:, h * 64:(h + 1) * 64],
                                            AF.Identity,
                                            scale=EMC[:, ch:ch + 1])
                    s0 = c.on("s", s0scale, waits=[("pv", pe)])

                    def omm(t, ch=ch):
                        return t.matmul(p1[0:16, 256:320],
                                        sb["QCsb"][:, ch * 16:(ch + 1) * 16],
                                        sb["S0sb"], start=True, stop=True)
                    om = c.on("t", omm, waits=[("ps", s0), ("pv", qc)],
                              inc="pt", chain=False)

                    def oev(v, ch=ch):
                        return v.tensor_tensor(sb["OFL"][:, ch * 64:(ch + 1) * 64],
                                               p1[0:16, 256:320],
                                               sb["MQVfl"][:, ch * 64:(ch + 1) * 64],
                                               ALU.add)
                    ov = c.on("v", oev, waits=[("pt", om), ("d1", f3_)])

                    def smm(t, ch=ch):
                        t.matmul(p2[0:64, 256:320], sb["ASTsb"], sb["S0sb"],
                                 start=True, stop=False)
                        return t.matmul(p2[0:64, 256:320], sb["IB"], sb["PCsb"],
                                        start=False, stop=True)
                    sm = c.on("t", smm, waits=[("ps", s0), ("pv", pe)],
                              inc="pt", chain=False)

                    def snew(s, ch=ch, h=h):
                        return s.activation(sb["SST"][:, h * 64:(h + 1) * 64],
                                            p2[0:64, 256:320], AF.Identity,
                                            scale=ERC[:, ch:ch + 1])
                    c.on("s", snew, waits=[("pt", sm)])

                odo = c.dma("y", c.dram["sg_do"], sb["OFL"],
                            waits=[("pv", c.now("pv")), ("ps", c.now("ps"))],
                            sem="d1")
                od = None
                for ch in range(8):
                    od = c.dma("g", sb["OTP"][ch * 16:(ch + 1) * 16, :],
                               c.dram["sg_do"][:, ch * 64:(ch + 1) * 64],
                               waits=([("d1", odo)] if ch == 0 else []),
                               sem="d1")

                def hsq(s):
                    return s.activation(sb["SQ"], sb["OTP"], AF.Square)
                h1 = c.on("s", hsq, waits=[("d1", od)])

                def hred(v):
                    v.tensor_reduce(sb["RS"], sb["SQ"], AX.X, ALU.add)
                    return v.tensor_scalar(sb["RS"], sb["RS"], 1.0 / 64,
                                           L2EPS, ALU.mult, ALU.add)
                h2 = c.on("v", hred, waits=[("ps", h1)])

                def hrs(s):
                    s.activation(sb["RS"], sb["RS"], AF.Ln)
                    return s.activation(sb["RS"], sb["RS"], AF.Exp, scale=-0.5)
                h3 = c.on("s", hrs, waits=[("pv", h2)])

                def hmul(v):
                    return v.scalar_tensor_tensor(sb["OTP"], sb["OTP"],
                                                  sb["RS"], sb["OTP"],
                                                  ALU.mult, ALU.bypass)
                h4 = c.on("v", hmul, waits=[("ps", h3)])

                def otr(t):
                    return t.transpose(pt[0:64, 768:896], sb["OTP"], sb["IB2"])
                h5 = c.on("t", otr, waits=[("pv", h4)], inc="pt", chain=False)

                def ofin(v, h=h):
                    dst = sb["OPAIR"][(h % 2) * 64:(h % 2) * 64 + 64,
                                      (h // 2) * 128:(h // 2) * 128 + 128]
                    return v.scalar_tensor_tensor(dst, pt[0:64, 768:896],
                                                  sb["HNW"][:, h:h + 1],
                                                  sb["QCsb"], ALU.mult,
                                                  ALU.bypass)
                c.on("v", ofin, waits=[("pt", h5)])
                c.join()

            def yout(g):
                g.dma_start(out=dram["sg_sout"], in_=sb["SST"]).then_inc(c.sem["d9"], 16)
                g.dma_start(out=dram["sg_o"], in_=sb["OPAIR"]).then_inc(c.sem["d9"], 16)
                g.wait_ge(c.sem["d9"], 32)
            c.block.gpsimd(yout)
            c.inc("d9", 32)
    return nc


def emit_l2apost(c, scale):
    XH, SB2, SB3, AUX, TRASH, TRASH2, WT = (
        c.sb["XH"], c.sb["SB2"], c.sb["SB3"], c.sb["AUX"], c.sb["TRASH"],
        c.sb["TRASH2"], c.sb["WT"])
    ax = c.dma("y", AUX[:, 0:32], c.dram["l2a_aux2"], sem="d7")
    cast_done = 0
    for wn, scr in (("l2a_f1", "l2a_f1s"), ("l2a_f3", "l2a_f3s"),
                    ("l2a_f2", "l2a_f2s")):
        for piece in range(8):
            lv = c.dma("g", WT[:, (piece % 2) * 4096:(piece % 2) * 4096 + 4096],
                       c.dram[wn][:, piece * 4096:(piece + 1) * 4096],
                       waits=[("d6", cast_done)] if cast_done else [], sem="d5")
            cast_done = c.dma("y", c.dram[scr][:, piece * 4096:(piece + 1) * 4096],
                              WT[:, (piece % 2) * 4096:(piece % 2) * 4096 + 4096],
                              waits=[("d5", lv)], sem="d6")
    ol = c.dma("y", SB2[:, :], c.dram["l2a_o"], sem="d3")
    u1 = c.dma("g", TRASH[:, 0:512], c.dram["l2a_u1"], sem="d4")
    u2 = c.dma("g", SB3[0:64, 0:1024], c.dram["l2a_u2"], sem="d4")
    g1ev = None
    for tt in range(NTT):
        def mm(t, tt=tt):
            l = None
            for kc in range(DC):
                l = t.matmul(c.sb["PSB"][0:64, 0:512],
                             TRASH[:, kc * 64:(kc + 1) * 64],
                             xh_rhs(c, kc, tt), start=(kc == 0),
                             stop=(kc == DC - 1))
            return l
        w = [("d4", u1), ("d7", ax)]
        if g1ev:
            w.append(("ps", g1ev))
        mmd = c.on("t", mm, waits=w, inc="pt", chain=False)

        def ev(s, tt=tt):
            return s.activation(SB3[0:64, 2048 + tt * TT:2048 + (tt + 1) * TT],
                                c.sb["PSB"][0:64, 0:512], AF.Silu,
                                bias=AUX[0:64, LB_U1B:LB_U1B + 1])
        g1ev = c.on("s", ev, waits=[("pt", mmd)])
    c.join()
    pp = PsumPool(c, c.sb["PSA"], 4, 512)
    ow0 = c.dma("g", WT[:, 0:4096], c.dram["l2a_ow"][:, 0:4096],
                waits=[("pt", c.now("pt"))], sem="d4")
    ow1 = c.dma("g", WT[:, 4096:8192], c.dram["l2a_ow"][:, 4096:8192],
                sem="d4")
    for oc in range(8):
        for tt in range(NTT):
            ps = pp.bank()
            war = pp.war_wait()
            w = [("d4", ow1 if oc >= 4 else ow0), ("d3", ol)] + ([war] if war else [])

            def mm(t, oc=oc, tt=tt, ps=ps):
                l = None
                for kc in range(DC):
                    l = t.matmul(ps, WT[:, oc * 1024 + kc * 128:oc * 1024 + (kc + 1) * 128],
                                 SB2[:, kc * T + tt * TT:kc * T + (tt + 1) * TT],
                                 start=(kc == 0), stop=(kc == DC - 1))
                return l
            mmd = c.on("t", mm, waits=w, inc="pt", chain=False)

            def gmm(t, oc=oc, tt=tt):
                return t.matmul(c.sb["PSB"][:, 0:512], SB3[0:64, oc * 128:(oc + 1) * 128],
                                SB3[0:64, 2048 + tt * TT:2048 + (tt + 1) * TT],
                                start=True, stop=True)
            gw = [("d4", u2), ("pv", c.now("pv"))]
            gmd = c.on("t", gmm, waits=gw, inc="pt", chain=False)

            def gsig(s, oc=oc, tt=tt):
                return s.activation(TRASH2[:, 0:512], c.sb["PSB"][:, 0:512],
                                    AF.Sigmoid, bias=AUX[:, LB_U2B + oc:LB_U2B + oc + 1])
            gsd = c.on("s", gsig, waits=[("pt", gmd)])

            def yev(v, oc=oc, tt=tt, ps=ps):
                xsl = XH[:, oc * T + tt * TT:oc * T + (tt + 1) * TT]
                v.tensor_tensor(TRASH2[:, 512:1024], ps, TRASH2[:, 0:512], ALU.mult)
                return v.tensor_tensor(xsl, xsl, TRASH2[:, 512:1024], ALU.add)
            val = c.on("v", yev, waits=[("pt", mmd), ("ps", gsd)])
            pp.push(val)
    c.join()
    def resy(v):
        l = None
        for kc in range(DC):
            l = v.tensor_tensor(c.sb["RES"][:, kc * T:(kc + 1) * T],
                                c.sb["RES"][:, kc * T:(kc + 1) * T],
                                XH[:, kc * T:(kc + 1) * T], ALU.add)
        return l
    c.on("v", resy)
    emit_rms(c, lambda kc: AUX[:, LB_FFW + kc:LB_FFW + kc + 1], 0, "SB3",
             eps=L2EPS)
    c.join()
    dmabar(c, "d5", "d6")

    def xn_rhs(kc, tt):
        return SB3[:, kc * T + tt * TT:kc * T + (tt + 1) * TT]
    for qu in range(NTT):
        for piece in range(8):
            w = [("pt", c.now("pt"))]
            d1 = c.dma("g", WT[:, 0:4096], c.dram["l2a_f1s"][:, piece * 4096:(piece + 1) * 4096],
                       waits=w, sem="d4")
            d3 = c.dma("g", WT[:, 4096:8192], c.dram["l2a_f3s"][:, piece * 4096:(piece + 1) * 4096],
                       sem="d4")
            for hcl in range(4):
                hc = piece * 4 + hcl

                def mm1(t, hcl=hcl, qu=qu):
                    l = None
                    for kc in range(DC):
                        l = t.matmul(c.sb["PSB"][:, 0:512],
                                     WT[:, hcl * 1024 + kc * 128:hcl * 1024 + (kc + 1) * 128],
                                     xn_rhs(kc, qu), start=(kc == 0),
                                     stop=(kc == DC - 1))
                    return l
                w1 = [("d4", d1), ("ps", c.now("ps"))]
                m1 = c.on("t", mm1, waits=w1, inc="pt", chain=False)

                def ev1(s):
                    return s.activation(TRASH2[:, 0:512], c.sb["PSB"][:, 0:512],
                                        AF.Silu)
                e1 = c.on("s", ev1, waits=[("pt", m1)])

                def mm3(t, hcl=hcl, qu=qu):
                    l = None
                    for kc in range(DC):
                        l = t.matmul(c.sb["PSB"][:, 512:1024],
                                     WT[:, 4096 + hcl * 1024 + kc * 128:4096 + hcl * 1024 + (kc + 1) * 128],
                                     xn_rhs(kc, qu), start=(kc == 0),
                                     stop=(kc == DC - 1))
                    return l
                m3 = c.on("t", mm3, waits=[("d4", d3), ("ps", e1),
                                           ("pv", c.now("pv"))], inc="pt",
                          chain=False)

                def ev3(v, hc=hc):
                    return v.tensor_tensor(SB2[:, hc * 512:(hc + 1) * 512],
                                           TRASH2[:, 0:512],
                                           c.sb["PSB"][:, 512:1024], ALU.mult)
                c.on("v", ev3, waits=[("pt", m3)])
        c.join()
        for wave in range(2):
            down_mm = None
            for piece in range(8):
                w = [("pt", c.now("pt"))]
                d2 = c.dma("g", WT[:, 0:4096],
                           c.dram["l2a_f2s"][:, piece * 4096:(piece + 1) * 4096],
                           waits=w, sem="d4")
                for hcl in range(4):
                    hc = piece * 4 + hcl

                    def mmd_(t, piece=piece, hcl=hcl, hc=hc, wave=wave):
                        l = None
                        for ocl in range(4):
                            oc = wave * 4 + ocl
                            l = t.matmul(c.sb["PSA"][:, ocl * 512:(ocl + 1) * 512],
                                         WT[:, hcl * 1024 + oc * 128:hcl * 1024 + oc * 128 + 128],
                                         SB2[:, hc * 512:(hc + 1) * 512],
                                         start=(piece == 0 and hcl == 0),
                                         stop=(piece == 7 and hcl == 3))
                        return l
                    down_mm = c.on("t", mmd_, waits=[("d4", d2)], inc="pt",
                                   chain=False)

            def dev(v, qu=qu, wave=wave):
                l = None
                for ocl in range(4):
                    oc = wave * 4 + ocl
                    rsl = c.sb["RES"][:, oc * T + qu * TT:oc * T + (qu + 1) * TT]
                    l = v.tensor_tensor(rsl, rsl,
                                        c.sb["PSA"][:, ocl * 512:(ocl + 1) * 512],
                                        ALU.add)
                return l
            c.on("v", dev, waits=[("pt", down_mm)])
            c.join()


VB_VECS = ([("L3_ffn_bias", 8)] +
           [(f"L0_ffn_b{l}_{e}", 8) for e in range(8) for l in (1, 2, 3)] +
           [("L1_ffn_bq", 8), ("L1_ffn_bk", 8), ("L1_ffn_bv", 8),
            ("L1_ffn_bo", 8), ("L1_ffn_g1", 8), ("L1_ffn_be1", 8),
            ("L1_ffn_g2", 8), ("L1_ffn_be2", 8)] +
           [("L0_attn_bq", 8), ("L0_attn_bk", 8), ("L0_attn_bv", 8),
            ("L0_attn_bo", 8), ("L0_attn_b1", 32), ("L0_attn_b2", 8),
            ("L0_attn_ln1_w", 8), ("L0_attn_ln1_b", 8),
            ("L0_attn_ln2_w", 8), ("L0_attn_ln2_b", 8)])


def vb_fetch(inputs, nm):
    if nm.startswith("L0_ffn_b"):
        l, e = nm[8], nm[10:]
        return inputs[f"L0_ffn_b{l}"][int(e)]
    return inputs[nm]


def mamba_host_prep(W_in, W_xproj, W_dt, W_out, A_log, b_dt, Dp, conv_w,
                    conv_b, norm_w):
    out = {}
    arr = W_in[:DI].reshape(2, 2, 4, 128, 8, 128)
    out["win"] = np.ascontiguousarray(arr.transpose(0, 1, 5, 4, 2, 3)
                                      .reshape(4, 128, 4096))
    if W_in.shape[0] > DI:
        arr = W_in[DI:].reshape(2, 2, 4, 128, 8, 128)
        out["wres"] = np.ascontiguousarray(arr.transpose(0, 1, 5, 4, 2, 3)
                                           .reshape(4, 128, 4096))
    arr = W_xproj.reshape(96, 2, 8, 128)
    out["wxp"] = np.ascontiguousarray(arr.transpose(1, 3, 2, 0).reshape(2, 128, 768))
    out["wdt"] = np.ascontiguousarray(W_dt.T)
    arr = W_out.reshape(2, 4, 128, 2, 8, 128)
    out["wout"] = np.ascontiguousarray(arr.transpose(3, 0, 5, 4, 1, 2)
                                       .reshape(4, 128, 4096))
    aux = np.zeros((128, MA_NEGB), np.float32)
    arr = A_log.reshape(16, 16, 8, 16)
    aux[:, MA_ALOG:MA_ALOG + 256] = arr.transpose(3, 2, 0, 1).reshape(128, 256)
    aux[:, MA_BDT:MA_BDT + 16] = b_dt.reshape(16, 128).T
    aux[:, MA_D:MA_D + 16] = Dp.reshape(16, 128).T
    aux[:, MA_CW:MA_CW + 64] = conv_w.reshape(16, 128, 4).transpose(1, 0, 2).reshape(128, 64)
    aux[:, MA_CB:MA_CB + 16] = conv_b.reshape(16, 128).T
    if norm_w is not None:
        aux[:, MA_NW:MA_NW + 8] = norm_w.reshape(8, 128).T
    out["aux"] = aux
    return out


def repsel_host():
    rs = np.zeros((128, 2304), np.float32)
    for j in range(16):
        for m in range(128):
            rs[j * 8 + m % 8, j * 128 + m] = 1.0
    for p in range(128):
        rs[p, 2048 + 128 + p % 8] = 1.0
    return rs.astype(ml_dtypes.bfloat16)


def vb_offsets():
    off, o = {}, 0
    for nm, w in VB_VECS:
        off[nm] = o
        o += w
    return off, o


def prep_inputs(inputs, core):
    d = {}
    d["x_in"] = np.ascontiguousarray(inputs["L0_attn_x"][core].T)
    d["L3_ffn_kernel_T"] = np.ascontiguousarray(inputs["L3_ffn_kernel"])
    d["L3_ffn_a_T"] = np.ascontiguousarray(inputs["L3_ffn_a"])
    d["L3_ffn_b_T"] = np.ascontiguousarray(inputs["L3_ffn_b"])
    vecs = [np.ascontiguousarray(vb_fetch(inputs, nm).reshape(w, 128).T)
            for nm, w in VB_VECS]
    d["vecbank"] = np.concatenate(vecs, axis=1).astype(np.float32)
    d["ident"] = np.eye(128, dtype=np.float32)
    d["L0_ffn_gate_w"] = np.ascontiguousarray(inputs["L0_ffn_gate_w"])
    d["L0_gate_b8"] = np.ascontiguousarray(inputs["L0_ffn_gate_b"].reshape(8, 1))
    for l in (1, 2, 3):
        d[f"L0_ffn_w{l}"] = np.ascontiguousarray(inputs[f"L0_ffn_w{l}"])
    es = np.zeros((8, 8 * 128), np.float32)
    for e in range(8):
        es[e, e * 128:(e + 1) * 128] = 1.0
    d["esel"] = es.astype(ml_dtypes.bfloat16)
    for wn in ("Wq", "Wk", "Wv", "Wo", "W1", "W2"):
        d[f"L0_attn_{wn}"] = np.ascontiguousarray(inputs[f"L0_attn_{wn}"])
    jl = np.arange(128)[:, None]
    il = np.arange(128)[None, :]
    mleft = np.where(jl >= il, 0.0, -1e30).astype(np.float32)
    mdiag = np.where(jl <= il, 0.0, -1e30).astype(np.float32)
    d["maskT"] = np.concatenate([mleft, mdiag], axis=1).astype(ml_dtypes.bfloat16)
    d["identb"] = np.eye(64).astype(ml_dtypes.bfloat16)
    d["identb128"] = np.eye(128).astype(ml_dtypes.bfloat16)
    for wn in ("w_gate", "w_up", "w_down"):
        d[f"L2_ffn_{wn}"] = np.ascontiguousarray(inputs[f"L2_ffn_{wn}"])
    for wn in ("Wq", "Wk", "Wv", "Wo", "Wg", "We", "bexp"):
        d[f"L1_ffn_{wn}"] = np.ascontiguousarray(inputs[f"L1_ffn_{wn}"])
    d["L1_gb8"] = np.ascontiguousarray(inputs["L1_ffn_bg"].reshape(8, 1))
    m1 = mamba_host_prep(inputs["L1_attn_W_in"], inputs["L1_attn_W_xproj"],
                         inputs["L1_attn_W_dt"], inputs["L1_attn_W_out"],
                         inputs["L1_attn_A_log"], inputs["L1_attn_b_dt"],
                         inputs["L1_attn_D"], inputs["L1_attn_conv_w"],
                         inputs["L1_attn_conv_b"], inputs["L1_attn_norm_w"])
    for k, v in m1.items():
        d[f"l1a_{k}"] = v
    d["repsel"] = repsel_host()
    g = inputs
    mf = mamba_host_prep(g["L3_attn_in_proj_w"], g["L3_attn_xproj_w_f"],
                         g["L3_attn_dtproj_w_f"], g["L3_attn_out_proj_w"],
                         g["L3_attn_A_log_f"], g["L3_attn_dtproj_b_f"],
                         g["L3_attn_D_f"], g["L3_attn_conv_w_f"],
                         g["L3_attn_conv_b_f"], g["L3_attn_norm_w"])
    mb = mamba_host_prep(g["L3_attn_in_proj_w"], g["L3_attn_xproj_w_b"],
                         g["L3_attn_dtproj_w_b"], g["L3_attn_out_proj_w"],
                         g["L3_attn_A_log_b"], g["L3_attn_dtproj_b_b"],
                         g["L3_attn_D_b"], g["L3_attn_conv_w_b"],
                         g["L3_attn_conv_b_b"], None)
    d["l3a_wipx"], d["l3a_wipz"], d["l3a_wop"] = mf["win"], mf["wres"], mf["wout"]
    d["l3a_wxp_f"], d["l3a_wdt_f"], d["l3a_aux_f"] = mf["wxp"], mf["wdt"], mf["aux"]
    d["l3a_wxp_b"], d["l3a_wdt_b"], d["l3a_aux_b"] = mb["wxp"], mb["wdt"], mb["aux"]
    ax1 = np.zeros((128, 72), np.float32)
    ax1[:, 0:32] = g["L2_attn_conv_w"].reshape(8, 128, 4).transpose(1, 0, 2).reshape(128, 32)
    ax1[:, 32:40] = g["L2_attn_conv_b"].reshape(8, 128).T
    ax1[:, 40:48] = g["L2_attn_norm_in_w"].reshape(8, 128).T
    ax1[:, 48:56] = g["L2_attn_alpha_b"].reshape(8, 128).T
    ax1[0:16, 64] = g["L2_attn_beta_b"]
    d["l2a_aux1"] = ax1
    ax2 = np.zeros((128, 32), np.float32)
    ax2[0:64, 0] = g["L2_attn_u1_b"]
    ax2[:, 1:9] = g["L2_attn_u2_b"].reshape(8, 128).T
    ax2[:, 9:17] = g["L2_attn_ff_norm_w"].reshape(8, 128).T
    d["l2a_aux2"] = ax2

    def lh8(w):
        return np.ascontiguousarray(w.reshape(8, 128, 8, 128)
                                    .transpose(1, 2, 0, 3).reshape(128, 8192))
    d["l2a_qw"] = lh8(g["L2_attn_q_w"])
    d["l2a_kw"] = lh8(g["L2_attn_k_w"])
    d["l2a_vw"] = lh8(g["L2_attn_v_w"])
    d["l2a_aw"] = lh8(g["L2_attn_alpha_w"])
    d["l2a_ow"] = lh8(g["L2_attn_out_w"])
    d["l2a_bw"] = np.ascontiguousarray(g["L2_attn_beta_w"].reshape(8, 128, 16)
                                       .transpose(1, 0, 2).reshape(128, 128))
    d["l2a_u1"] = np.ascontiguousarray(g["L2_attn_u1_w"].reshape(8, 128, 64)
                                       .transpose(1, 0, 2).reshape(128, 512))
    d["l2a_u2"] = np.ascontiguousarray(g["L2_attn_u2_w"])
    for nm, w in (("l2a_f1", g["L2_attn_ff_w1"]), ("l2a_f3", g["L2_attn_ff_w3"])):
        d[nm] = np.ascontiguousarray(w.reshape(8, 128, 8, 4, 128)
                                     .transpose(1, 2, 3, 0, 4).reshape(128, 32768))
    d["l2a_f2"] = np.ascontiguousarray(g["L2_attn_ff_w2"].reshape(8, 4, 128, 8, 128)
                                       .transpose(2, 0, 1, 3, 4).reshape(128, 32768))
    d["l2a_o"] = np.zeros((128, 16384), ml_dtypes.bfloat16)
    return d


def seg_consts(inputs):
    mk = np.zeros((128, 256), np.float32)
    for c_ in range(8):
        for a in range(16):
            for b in range(16):
                if b < a:
                    mk[c_ * 16 + a, c_ * 16 + b] = 1.0
                if a <= b:
                    mk[c_ * 16 + a, 128 + c_ * 16 + b] = 1.0
    return {
        "sg_masks": mk.astype(ml_dtypes.bfloat16),
        "sg_ib": np.eye(64).astype(ml_dtypes.bfloat16),
        "sg_ib2": np.eye(128).astype(ml_dtypes.bfloat16),
        "sg_hnw": np.ascontiguousarray(
            inputs["L2_attn_head_norm_w"].reshape(16, 64).T.astype(np.float32)),
    }


EMITTERS = {"l3f": emit_l3f, "l0f": emit_l0f, "l0a": emit_l0a, "l2f": emit_l2f,
            "l1f": emit_l1f, "l1a": emit_l1a, "l3a": emit_l3a,
            "l2apre": emit_l2apre, "l2apost": emit_l2apost}
NO_OUTER_LN = {"l2f"}

SUB = [("l3f", 0.5)]
SUBM1 = [("l0f", 0.5), ("l0a", 1.0), ("l0f", 0.5),
         ("l1f", 0.5), ("l1a", 1.0), ("l1f", 0.5),
         ("l2f", 0.5), ("l2apre", 1.0)]
SUBM2 = [("l2apost", 1.0), ("l2f", 0.5),
         ("l3f", 0.5), ("l3a", 1.0), ("l3f", 0.5)]
DEBUG_DUMPS = []


def build_program(sublayers):
    nc = bass.Bass("TRN2", target_bir_lowering=False, detect_race_conditions=DETECT_RACES)
    stack = ExitStack()
    voff, vlen = vb_offsets()

    dram = {}
    dram["x_in"] = nc.dram_tensor("x_in", [D, T], F32, kind="ExternalInput").ap()
    dram["x_out"] = nc.dram_tensor("x_out", [D, T], F32, kind="ExternalOutput").ap()
    dram["vecbank"] = nc.dram_tensor("vecbank", [128, max(vlen, 1)], F32, kind="ExternalInput").ap()
    dram["ident"] = nc.dram_tensor("ident", [128, 128], F32, kind="ExternalInput").ap()
    dram["L3_ffn_kernel_T"] = nc.dram_tensor("L3_ffn_kernel_T", [1024, 1024], F32, kind="ExternalInput").ap()
    dram["L3_ffn_a_T"] = nc.dram_tensor("L3_ffn_a_T", [1024, 16], F32, kind="ExternalInput").ap()
    dram["L3_ffn_b_T"] = nc.dram_tensor("L3_ffn_b_T", [16, 1024], F32, kind="ExternalInput").ap()
    dram["L0_ffn_gate_w"] = nc.dram_tensor("L0_ffn_gate_w", [1024, 8], F32, kind="ExternalInput").ap()
    dram["L0_gate_b8"] = nc.dram_tensor("L0_gate_b8", [8, 1], F32, kind="ExternalInput").ap()
    for l in (1, 2, 3):
        dram[f"L0_ffn_w{l}"] = nc.dram_tensor(f"L0_ffn_w{l}", [8, 1024, 1024], F32, kind="ExternalInput").ap()
    dram["esel"] = nc.dram_tensor("esel", [8, 8 * 128], BF16, kind="ExternalInput").ap()
    for wn, shp in (("Wq", [1024, 1024]), ("Wk", [1024, 1024]), ("Wv", [1024, 1024]),
                    ("Wo", [1024, 1024]), ("W1", [1024, 4096]), ("W2", [4096, 1024])):
        dram[f"L0_attn_{wn}"] = nc.dram_tensor(f"L0_attn_{wn}", shp, F32, kind="ExternalInput").ap()
    dram["maskT"] = nc.dram_tensor("maskT", [128, 256], BF16, kind="ExternalInput").ap()
    dram["identb"] = nc.dram_tensor("identb", [64, 64], BF16, kind="ExternalInput").ap()
    dram["identb128"] = nc.dram_tensor("identb128", [128, 128], BF16, kind="ExternalInput").ap()
    for wn, shp in (("w_gate", [2816, 1024]), ("w_up", [2816, 1024]), ("w_down", [1024, 2816])):
        dram[f"L2_ffn_{wn}"] = nc.dram_tensor(f"L2_ffn_{wn}", shp, F32, kind="ExternalInput").ap()
    dram["l2f_wg_scr"] = nc.dram_tensor("l2f_wg_scr", [1024, 2816], BF16, kind="Internal").ap()
    dram["l2f_wu_scr"] = nc.dram_tensor("l2f_wu_scr", [1024, 2816], BF16, kind="Internal").ap()
    dram["l2f_wd_scr"] = nc.dram_tensor("l2f_wd_scr", [2816, 1024], BF16, kind="Internal").ap()
    for wn, shp in (("Wq", [1024, 1024]), ("Wk", [1024, 1024]), ("Wv", [1024, 1024]),
                    ("Wo", [1024, 1024]), ("Wg", [1024, 8]), ("We", [8, 1024, 1024]),
                    ("bexp", [8, 1024])):
        dram[f"L1_ffn_{wn}"] = nc.dram_tensor(f"L1_ffn_{wn}", shp, F32, kind="ExternalInput").ap()
    dram["L1_gb8"] = nc.dram_tensor("L1_gb8", [8, 1], F32, kind="ExternalInput").ap()
    dram["v_loc"] = nc.dram_tensor("v_loc", [64, 128], F32, kind="Internal").ap()
    dram["v_red"] = nc.dram_tensor("v_red", [64, 128], F32, kind="Internal").ap()
    for pre, shp in (("l1a_win", [4, 128, 4096]), ("l1a_wres", [4, 128, 4096]),
                     ("l1a_wxp", [2, 128, 768]), ("l1a_wdt", [64, 2048]),
                     ("l1a_wout", [4, 128, 4096]), ("l1a_aux", [128, MA_NEGB])):
        dram[pre] = nc.dram_tensor(pre, shp, F32, kind="ExternalInput").ap()
    dram["repsel"] = nc.dram_tensor("repsel", [128, 2304], BF16, kind="ExternalInput").ap()
    dram["l1a_xsA"] = nc.dram_tensor("l1a_xsA", [128, 16384], BF16, kind="Internal").ap()
    dram["l1a_res"] = nc.dram_tensor("l1a_res", [128, 32768], BF16, kind="Internal").ap()
    dram["l1a_bc"] = nc.dram_tensor("l1a_bc", [32, 2048], F32, kind="Internal").ap()
    for pre, shp in (("l3a_wipx", [4, 128, 4096]), ("l3a_wipz", [4, 128, 4096]),
                     ("l3a_wop", [4, 128, 4096]),
                     ("l3a_wxp_f", [2, 128, 768]), ("l3a_wxp_b", [2, 128, 768]),
                     ("l3a_wdt_f", [64, 2048]), ("l3a_wdt_b", [64, 2048]),
                     ("l3a_aux_f", [128, MA_NEGB]), ("l3a_aux_b", [128, MA_NEGB])):
        dram[pre] = nc.dram_tensor(pre, shp, F32, kind="ExternalInput").ap()
    for pre in ("l3a_xi", "l3a_z", "l3a_yb"):
        dram[pre] = nc.dram_tensor(pre, [128, 32768], BF16, kind="Internal").ap()
    for pre, shp in (("l2a_aux1", [128, 72]), ("l2a_aux2", [128, 32]),
                     ("l2a_qw", [128, 8192]), ("l2a_kw", [128, 8192]),
                     ("l2a_vw", [128, 8192]), ("l2a_aw", [128, 8192]),
                     ("l2a_bw", [128, 128]), ("l2a_ow", [128, 8192]),
                     ("l2a_u1", [128, 512]), ("l2a_u2", [64, 1024]),
                     ("l2a_f1", [128, 32768]), ("l2a_f3", [128, 32768]),
                     ("l2a_f2", [128, 32768])):
        dram[pre] = nc.dram_tensor(pre, shp, F32, kind="ExternalInput").ap()
    dram["l2a_o"] = nc.dram_tensor("l2a_o", [128, 16384], BF16, kind="ExternalInput").ap()
    for pre in ("l2a_q", "l2a_k", "l2a_v", "l2a_sp"):
        dram[pre] = nc.dram_tensor(pre, [64, 32768], BF16, kind="ExternalOutput").ap()
    dram["l2a_beta"] = nc.dram_tensor("l2a_beta", [16, 2048], F32, kind="ExternalOutput").ap()
    for pre in ("l2a_f1s", "l2a_f3s", "l2a_f2s"):
        dram[pre] = nc.dram_tensor(pre, [128, 32768], BF16, kind="Internal").ap()

    with stack:
        sb = {}
        def alloc(name, shape, dt):
            sb[name] = stack.enter_context(nc.sbuf_tensor(name, shape, dt)).ap()
        alloc("RES", [128, DC * T], F32)
        alloc("XH", [128, DC * T], BF16)
        alloc("SB2", [128, 16384], BF16)
        alloc("SB3", [128, 16384], BF16)
        alloc("WT", [128, 8192], BF16)
        alloc("VEC", [128, 2176], F32)
        sb["SCR"] = sb["SB3"].bitcast(F32)
        alloc("VB", [128, max(vlen, 1)], F32)
        alloc("ONECOL", [128, 1], BF16)
        alloc("ONECOLF", [128, 1], F32)
        alloc("TRASH", [128, 1152], BF16)
        alloc("TRASH2", [128, 2048], BF16)
        alloc("AUX", [128, 512], F32)
        alloc("VS", [128, 68], F32)
        alloc("QS", [64, 2048], BF16)
        alloc("KS", [64, 2304], BF16)
        alloc("VSTG", [64, 1024], BF16)
        alloc("PBUF", [128, 512], BF16)
        sb["MASKT"] = sb["PBUF"][:, 256:512]
        alloc("IDENTB", [64, 64], BF16)
        alloc("IDENTB2", [128, 128], BF16)
        alloc("ONESF", [64, 128], F32)
        alloc("IDENTF", [128, 128], F32)
        psum_al = nc.psum_tensor("PSA", [128, 2048], F32)
        psb_al = nc.psum_tensor("PSB", [128, 1024], F32)
        pst_al = nc.psum_tensor("PST", [128, 2048], BF16)
        sb["PSA"] = stack.enter_context(psum_al).ap()
        sb["PSB"] = stack.enter_context(psb_al).ap()
        sb["PST"] = stack.enter_context(pst_al).ap()

        with nc.Block() as block:
            c = Ctx(nc, block, stack)
            c.dram = dram
            c.sb = sb
            c.vb_off = voff

            def ginit(g):
                g.memset(sb["ONECOL"][:, :], 1.0)
                g.memset(sb["ONECOLF"][:, :], 1.0)
                g.memset(sb["VEC"][:, :], 0.0)
                g.memset(sb["TRASH"][:, :], 0.0)
                g.memset(sb["TRASH2"][:, :], 0.0)
                return g.memset(sb["ONESF"][:, :], 1.0)
            ginit_done = c.on("g", ginit, inc="pg")

            def yinit(y):
                y.dma_start(out=sb["RES"][:, :].rearrange("p (c t) -> p c t", c=DC),
                            in_=dram["x_in"].rearrange("(c p) t -> p c t", p=128)).then_inc(c.sem["di"], 16)
                y.dma_start(out=sb["VB"][:, :], in_=dram["vecbank"]).then_inc(c.sem["di"], 16)
                y.dma_start(out=sb["IDENTF"][:, :], in_=dram["ident"]).then_inc(c.sem["di"], 16)
                y.dma_start(out=sb["MASKT"][:, :], in_=dram["maskT"]).then_inc(c.sem["di"], 16)
                y.dma_start(out=sb["IDENTB"][:, :], in_=dram["identb"]).then_inc(c.sem["di"], 16)
                y.dma_start(out=sb["IDENTB2"][:, :], in_=dram["identb128"]).then_inc(c.sem["di"], 16)
            c.block.sync(yinit)
            x_loaded = c.inc("di", 96)

            for key, pkey in (("t", "pt"), ("v", "pv"), ("s", "ps"), ("g", "pg")):
                def wall(e, pkey=pkey):
                    e.wait_ge(c.sem["di"], x_loaded)
                    if pkey != "pg":
                        e.wait_ge(c.sem["pg"], ginit_done)
                c._sec(key)(wall)

            for name, scale in sublayers:
                if name not in NO_OUTER_LN:
                    emit_ln(c)
                    c.join()
                EMITTERS[name](c, scale)
                c.join()

            for dname, skey, cols, ddt in DEBUG_DUMPS:
                if skey.startswith("dram:"):
                    srckey = skey[5:]
                    shp = list(dram[srckey].shape)
                    dram[dname] = nc.dram_tensor(dname, shp, ddt, kind="ExternalOutput").ap()
                    def dumpf(g, dname=dname, srckey=srckey):
                        g.dma_start(out=dram[dname], in_=dram[srckey]).then_inc(c.sem["d8"], 16)
                else:
                    dram[dname] = nc.dram_tensor(dname, [128, cols], ddt, kind="ExternalOutput").ap()
                    def dumpf(g, dname=dname, skey=skey, cols=cols):
                        g.dma_start(out=dram[dname], in_=sb[skey][:, 0:cols]).then_inc(c.sem["d8"], 16)
                c.block.gpsimd(dumpf)
                c.inc("d8", 16)
            if DEBUG_DUMPS:
                c.block.gpsimd(lambda g: g.wait_ge(c.sem["d8"], c.now("d8")))

            def yout(g):
                g.dma_start(out=dram["x_out"].rearrange("(c p) t -> p c t", p=128),
                            in_=sb["RES"][:, :].rearrange("p (c t) -> p c t", c=DC)).then_inc(c.sem["d9"], 16)
                g.wait_ge(c.sem["d9"], c.now("d9") + 16)
            c.block.gpsimd(yout)
            c.inc("d9", 16)
    return nc


def kernel(**inputs):
    nc1 = build_program(SUBM1)
    nc2 = build_program(SUBM2)
    ncs = build_l2a_seg()
    base = [prep_inputs(inputs, core) for core in range(8)]
    cons = seg_consts(inputs)
    res1 = run_bass_kernel_spmd(nc1, base, core_ids=list(range(8)))
    S = [np.zeros((64, 1024), np.float32) for _ in range(8)]
    segs_o = [[] for _ in range(8)]
    for i in range(16):
        in_maps = []
        for core in range(8):
            r = res1.results[core]
            m = dict(cons)
            for nm, key in (("sg_q", "l2a_q"), ("sg_k", "l2a_k"),
                            ("sg_v", "l2a_v"), ("sg_sp", "l2a_sp")):
                m[nm] = np.ascontiguousarray(
                    r[key].reshape(64, 16, 2048)[:, :, i * 128:(i + 1) * 128]
                    .reshape(64, 2048))
            m["sg_beta"] = np.ascontiguousarray(
                r["l2a_beta"][:, i * 128:(i + 1) * 128].T)
            m["sg_sin"] = S[core]
            in_maps.append(m)
        rs = run_bass_kernel_spmd(ncs, in_maps, core_ids=list(range(8)))
        for core in range(8):
            S[core] = np.ascontiguousarray(rs.results[core]["sg_sout"])
            segs_o[core].append(rs.results[core]["sg_o"])
    in2 = []
    for core in range(8):
        O = np.stack(segs_o[core], 0).reshape(16, 128, 8, 128)
        O = np.ascontiguousarray(O.transpose(1, 2, 0, 3).reshape(128, 16384))
        d2 = dict(base[core])
        d2["x_in"] = np.ascontiguousarray(res1.results[core]["x_out"])
        d2["l2a_o"] = O
        in2.append(d2)
    res2 = run_bass_kernel_spmd(nc2, in2, core_ids=list(range(8)))
    outs = [np.ascontiguousarray(r["x_out"].T) for r in res2.results]
    return np.stack(outs, 0).astype(np.float32)


# version 02
# speedup vs baseline: 1.0007x; 1.0007x over previous
import numpy as np
import ml_dtypes
import concourse.bass as bass
import concourse.mybir as mybir
from concourse.bass_utils import run_bass_kernel_spmd
from contextlib import ExitStack

F32 = mybir.dt.float32
BF16 = mybir.dt.bfloat16
I32 = mybir.dt.int32
U8 = mybir.dt.uint8
AF = mybir.ActivationFunctionType
ALU = mybir.AluOpType
AX = mybir.AxisListType

T = 2048
TT = 512
NTT = T // TT
D = 1024
DC = D // 128

LN_EPS = 1e-5
GELU_BYPASS = False
DETECT_RACES = False
L0A_STAGE = 99
L0A_TRUNC = 0
L1F_SKIP_LN2 = False


class Ctx:

    def __init__(self, nc, block, stack):
        self.nc = nc
        self.block = block
        self.stack = stack
        self.sem = {}
        self.cnt = {}
        names = ["pt", "pv", "ps", "pg", "di"] + [f"d{i}" for i in range(10)]
        for name in names:
            self.sem[name] = stack.enter_context(nc.semaphore(name))
            self.cnt[name] = 0
        self.dram = {}
        self.sb = {}
        self.vb_off = {}

    def inc(self, name, n=1):
        self.cnt[name] += n
        return self.cnt[name]

    def now(self, name):
        return self.cnt[name]

    def _sec(self, key):
        return {"t": self.block.tensor, "v": self.block.vector,
                "s": self.block.scalar, "g": self.block.gpsimd,
                "y": self.block.sync}[key]

    def on(self, engkey, fn, waits=(), inc=None, chain=None):
        sem = self.sem
        pkey = {"t": "pt", "v": "pv", "s": "ps", "g": "pg", "y": None}[engkey]
        if chain is None:
            chain = engkey in ("v", "s", "g")
        holder = {}

        def wrapped(e):
            for s_, v_ in waits:
                if v_ > 0:
                    e.wait_ge(sem[s_], v_)
            if chain:
                e = _Chained(self, e, pkey)
            last = fn(e)
            if chain:
                holder["val"] = e.prev
            elif inc is not None:
                assert last is not None, "fn must return last instruction"
                last.then_inc(sem[inc], 1)
                holder["val"] = self.inc(inc)
        self._sec(engkey)(wrapped)
        return holder.get("val")

    def dma(self, q, out, in_, waits=(), sem="d0"):
        semh = self.sem[sem]

        def f(e):
            for s_, v_ in waits:
                if v_ > 0:
                    e.wait_ge(self.sem[s_], v_)
            e.dma_start(out=out, in_=in_).then_inc(semh, 16)
        self._sec(q)(f)
        return self.inc(sem, 16)

    def selfsync(self, engkey):
        pkey = {"t": "pt", "v": "pv", "s": "ps", "g": "pg"}[engkey]
        val = self.inc(pkey)

        def f(e):
            e.wait_ge(self.sem[pkey], val)
        return pkey, val

    def join(self, engines=("pt", "pv", "ps", "pg")):
        tgt = [(e, self.cnt[e]) for e in engines]
        for key, pkey in (("t", "pt"), ("v", "pv"), ("s", "ps"), ("g", "pg"),
                          ("y", None)):
            def f(e, pkey=pkey):
                for k, v in tgt:
                    if k != pkey and v > 0:
                        e.wait_ge(self.sem[k], v)
            self._sec(key)(f)


class _Chained:
    _PASS = {"wait_ge", "engine_nop", "nop", "then_inc"}

    def __init__(self, ctx, eng, pkey):
        self._ctx, self._eng, self._pkey = ctx, eng, pkey
        self.prev = None

    def __getattr__(self, name):
        fn = getattr(self._eng, name)
        if name in self._PASS or name.startswith("reg") or not callable(fn):
            return fn
        ctx, eng, pkey = self._ctx, self._eng, self._pkey

        def call(*a, **k):
            assert name != "dma_start", "don't issue DMAs via chained engines"
            if self.prev is not None:
                eng.wait_ge(ctx.sem[pkey], self.prev)
            ins = fn(*a, **k)
            ins.then_inc(ctx.sem[pkey], 1)
            self.prev = ctx.inc(pkey)
            return ins
        return call


class PsumPool:

    def __init__(self, c, psum_ap, nbanks, bank_w=512, evac_sem="pv"):
        self.c, self.ap, self.n, self.w = c, psum_ap, nbanks, bank_w
        self.evac_sem = evac_sem
        self.issued = 0
        self.evacs = []

    def bank(self):
        i = self.issued % self.n
        return self.ap[:, i * self.w:(i + 1) * self.w]

    def war_wait(self):
        j = self.issued - self.n
        if j < 0:
            return None
        e = self.evacs[j]
        return e if isinstance(e, tuple) else (self.evac_sem, e)

    def push(self, evac_val):
        self.issued += 1
        self.evacs.append(evac_val)


def xh_rhs(c, kc, tt):
    return c.sb["XH"][:, kc * T + tt * TT: kc * T + (tt + 1) * TT]


def res_slice(c, oc, tt):
    return c.sb["RES"][:, oc * T + tt * TT: oc * T + (tt + 1) * TT]


def linear_T(c, pp, lhsT_fn, rhs_fn, evac_fn, kc_n, oc_n, ntt=NTT,
             pre_wait=None, extra_mm=None, evac_eng="v", loop_order="tt"):
    eng_prog = {"v": "pv", "s": "ps"}[evac_eng]
    evac_vals = [[None] * oc_n for _ in range(ntt)]
    if loop_order == "tt":
        jobs = [(tt, oc) for tt in range(ntt) for oc in range(oc_n)]
    else:
        jobs = [(tt, oc) for oc in range(oc_n) for tt in range(ntt)]
    for tt, oc in jobs:
        if True:
            ps = pp.bank()
            war = pp.war_wait()
            waits = list(pre_wait(oc, tt)) if pre_wait else []
            if war is not None:
                waits.append(war)

            def mm(t, oc=oc, tt=tt, ps=ps):
                last = None
                for kc in range(kc_n):
                    stop = (kc == kc_n - 1) and extra_mm is None
                    last = t.matmul(ps, lhsT_fn(kc, oc), rhs_fn(kc, tt),
                                    start=(kc == 0), stop=stop)
                if extra_mm is not None:
                    last = extra_mm(oc, tt, ps, t)
                return last
            mm_done = c.on("t", mm, waits=waits, inc="pt")

            def ev(e, oc=oc, tt=tt, ps=ps):
                return evac_fn(oc, tt, ps, e)
            val = c.on(evac_eng, ev, waits=[("pt", mm_done)], inc=eng_prog)
            evac_vals[tt][oc] = val
            pp.push(val)
    return evac_vals


def emit_ln_g(c, src_fn, dst_fn, sq_fn, srcb_fn, tok0, ntok, stat_off,
              w_off=None, b_off=None, eps=LN_EPS, pre_waits=(), f32_dst=False,
              f32_sums=False):
    VEC, PSA, PSB = c.sb["VEC"], c.sb["PSA"], c.sb["PSB"]
    onecol = c.sb["ONECOL"]
    sizes = [TT] * (ntok // TT)
    if ntok % TT:
        sizes.append(ntok % TT)
    offs = [sum(sizes[:i]) for i in range(len(sizes))]
    ntt = len(sizes)
    assert ntt <= 4

    def sl_of(tt):
        return slice(tok0 + offs[tt], tok0 + offs[tt] + sizes[tt])

    def prep(se):
        last = None
        for tt in range(ntt):
            for kc in range(DC):
                sl = sl_of(tt)
                last = se.activation(sq_fn(kc, sl), src_fn(kc, sl), AF.Square)
        return last
    sq = c.on("s", prep, waits=list(pre_waits))
    sumcol = c.sb["ONECOLF"] if f32_sums else c.sb["ONECOL"]

    cp = None
    if srcb_fn is not None:
        def cpf(v):
            last = None
            for tt in range(ntt):
                for kc in range(DC):
                    sl = sl_of(tt)
                    last = v.tensor_copy(srcb_fn(kc, sl), src_fn(kc, sl))
            return last
        cp = c.on("v", cpf, waits=list(pre_waits))
        sb_fn = srcb_fn
    else:
        sb_fn = src_fn

    waits = [("ps", sq)] + ([("pv", cp)] if cp else [])
    sums_done = []
    for tt in range(ntt):
        ps = PSA[:, tt * TT:tt * TT + sizes[tt]]

        def mm(t, tt=tt, ps=ps):
            sl = sl_of(tt)
            last = None
            for kc in range(DC):
                t.matmul(ps[0:1, :], sumcol, sb_fn(kc, sl),
                         start=(kc == 0), stop=(kc == DC - 1))
            for kc in range(DC):
                last = t.matmul(ps[32:33, :], sumcol, sq_fn(kc, sl),
                                start=(kc == 0), stop=(kc == DC - 1))
            return last
        sums_done.append(c.on("t", mm, waits=waits, inc="pt", chain=False))

    def stats(v):
        last = None
        for tt in range(ntt):
            pslc = PSA[0:1, tt * TT:tt * TT + sizes[tt]]
            pslc2 = PSA[32:33, tt * TT:tt * TT + sizes[tt]]
            nsl = VEC[0:1, stat_off + offs[tt]:stat_off + offs[tt] + sizes[tt]]
            rsl = VEC[0:1, stat_off + ntok + offs[tt]:stat_off + ntok + offs[tt] + sizes[tt]]
            v.tensor_scalar(nsl, pslc, -1.0 / D, None, ALU.mult)
            v.tensor_scalar(rsl, pslc2, 1.0 / D, None, ALU.mult)
            v.tensor_tensor(pslc, nsl, nsl, ALU.mult)
            v.tensor_tensor(rsl, rsl, pslc, ALU.subtract)
            last = v.tensor_scalar(rsl, rsl, eps, None, ALU.add)
        return last
    st = c.on("v", stats, waits=[("pt", sums_done[-1])])

    rstd_all = VEC[0:1, stat_off + ntok:stat_off + 2 * ntok]

    def rsf(s):
        s.activation(rstd_all, rstd_all, AF.Ln)
        return s.activation(rstd_all, rstd_all, AF.Exp, scale=-0.5)
    rs = c.on("s", rsf, waits=[("pv", st)])

    prev_norm = None
    for tt in range(ntt):
        sl = sl_of(tt)
        n = sizes[tt]
        so = slice(stat_off + offs[tt], stat_off + offs[tt] + n)
        so2 = slice(stat_off + ntok + offs[tt], stat_off + ntok + offs[tt] + n)
        waits2 = [("ps", rs)]
        if prev_norm is not None:
            waits2.append(("pv", prev_norm))

        def bc(t, so=so, so2=so2, n=n):
            t.matmul(PSB[:, 0:n], c.sb["ONESF"][0:1, :], VEC[0:1, so],
                     start=True, stop=True)
            return t.matmul(PSB[:, TT:TT + n], c.sb["ONESF"][0:1, :],
                            VEC[0:1, so2], start=True, stop=True)
        bc_done = c.on("t", bc, waits=waits2, inc="pt", chain=False)

        def norm(v, sl=sl, n=n):
            last = None
            for kc in range(DC):
                if f32_dst:
                    d = dst_fn(kc, sl)
                    v.tensor_tensor(d, src_fn(kc, sl), PSB[:, 0:n], ALU.add)
                    last = v.tensor_tensor(d, d, PSB[:, TT:TT + n], ALU.mult)
                    continue
                tmp = sq_fn(kc, sl)
                v.tensor_tensor(tmp, src_fn(kc, sl), PSB[:, 0:n], ALU.add)
                if w_off is None:
                    last = v.tensor_tensor(dst_fn(kc, sl), tmp,
                                           PSB[:, TT:TT + n], ALU.mult)
                else:
                    v.tensor_tensor(tmp, tmp, PSB[:, TT:TT + n], ALU.mult)
                    last = v.tensor_scalar(
                        dst_fn(kc, sl), tmp,
                        c.sb["VB"][:, w_off + kc:w_off + kc + 1],
                        c.sb["VB"][:, b_off + kc:b_off + kc + 1],
                        ALU.mult, ALU.add)
            return last
        prev_norm = c.on("v", norm, waits=[("pt", bc_done)])
    return prev_norm


def emit_ln(c):
    RES, XH, SB2 = c.sb["RES"], c.sb["XH"], c.sb["SB2"]
    last = None
    for hf in range(2):
        last = emit_ln_g(
            c,
            src_fn=lambda kc, sl: RES[:, kc * T + sl.start: kc * T + sl.stop],
            dst_fn=lambda kc, sl: XH[:, kc * T + sl.start: kc * T + sl.stop],
            sq_fn=lambda kc, sl: SB2[:, kc * T + sl.start: kc * T + sl.stop],
            srcb_fn=lambda kc, sl: XH[:, kc * T + sl.start: kc * T + sl.stop],
            tok0=hf * 1024, ntok=1024, stat_off=0)
    return last


def emit_l3f(c, scale):
    XH, SCR, SB2, SB3, VEC = (c.sb["XH"], c.sb["SCR"], c.sb["SB2"],
                              c.sb["SB3"], c.sb["VEC"])
    PSB = c.sb["PSB"]

    a_sb = SB2[:, 8192:8192 + DC * 16]
    b_sb = SB2[0:16, 8320:8320 + 1024]
    u_bf = SB2[0:16, 9344:9344 + T]
    kl = ("d0", c.dma("y", SCR[:, 0:DC * D].rearrange("p (kc o) -> p kc o", kc=DC),
               c.dram["L3_ffn_kernel_T"].rearrange("(kc p) o -> p kc o", p=128), sem="d0"))
    al = ("d1", c.dma("g", a_sb.rearrange("p (kc r) -> p kc r", kc=DC),
               c.dram["L3_ffn_a_T"].rearrange("(kc p) r -> p kc r", p=128), sem="d1"))
    bl = ("d2", c.dma("g", b_sb, c.dram["L3_ffn_b_T"], sem="d2"))

    kf = SCR[:, 0:DC * D]
    wq = SB2[:, 0:DC * D]
    amax = VEC[0:128, 8:9]

    r1 = c.on("v", lambda v: v.tensor_reduce(amax, kf, AX.X, ALU.max,
                                             apply_absolute_value=True),
              waits=[kl], inc="pv")
    r2 = c.on("t", lambda t: t.transpose(PSB[0:1, 0:128], amax, c.sb["IDENTF"]),
              waits=[("pv", r1)], inc="pt")

    r3a = c.on("v", lambda v: v.tensor_reduce(VEC[0:1, 4:5], PSB[0:1, 0:128],
                                              AX.X, ALU.max),
               waits=[("pt", r2)], inc="pv")

    def q3(v):
        v.wait_ge(c.sem["pv"], r3a)
        v.tensor_scalar(VEC[0:1, 5:6], VEC[0:1, 4:5], 1.0 / 7.0, None, ALU.mult)
        return v.reciprocal(VEC[0:1, 6:7], VEC[0:1, 5:6])
    r3 = c.on("v", q3, inc="pv")

    def q4(t):
        return t.matmul(PSB[:, 128:256], c.sb["ONESF"][0:1, :], VEC[0:1, 0:128],
                        start=True, stop=True)
    r4 = c.on("t", q4, waits=[("pv", r3)], inc="pt")

    scl = VEC[0:128, 9:10]
    rscale = VEC[0:128, 10:11]

    def q5(v):
        v.tensor_copy(scl, PSB[:, 133:134])
        v.tensor_copy(rscale, PSB[:, 134:135])
        v.tensor_scalar(kf, kf, rscale, None, ALU.mult)
        v.tensor_scalar(kf, kf, 12582912.0, 12582912.0, ALU.add, ALU.subtract)
        v.tensor_scalar(kf, kf, 7.0, -8.0, ALU.min, ALU.max)
        return v.tensor_scalar(wq, kf, scl, None, ALU.mult)
    quant_done = c.on("v", q5, waits=[("pt", r4)], inc="pv")

    pp = PsumPool(c, c.sb["PSA"], 4, evac_sem="pv")
    for tt in range(NTT):
        ps = pp.bank()
        war = pp.war_wait()
        waits = [al, ("pv", quant_done)]
        if war is not None:
            waits.append(("pv", war))

        def mm(t, tt=tt, ps=ps):
            last = None
            for kc in range(DC):
                last = t.matmul(ps[0:16, :], a_sb[:, kc * 16:(kc + 1) * 16],
                                xh_rhs(c, kc, tt), start=(kc == 0),
                                stop=(kc == DC - 1))
            return last
        mm_done = c.on("t", mm, waits=waits, inc="pt")
        pp.push(c.on("v", lambda v, tt=tt, ps=ps: v.tensor_copy(
            u_bf[:, tt * TT:(tt + 1) * TT], ps[0:16, :]),
            waits=[("pt", mm_done)], inc="pv"))
    u_done = c.now("pv")

    boff = c.vb_off["L3_ffn_bias"]
    sb_vec = VEC[0:128, 16:16 + DC]
    sbv = c.on("v", lambda v: v.tensor_scalar(
        sb_vec, c.sb["VB"][:, boff:boff + DC], scale, None, ALU.mult), inc="pv")

    def lhsT(kc, oc):
        return wq[:, kc * D + oc * 128: kc * D + (oc + 1) * 128]

    def extra(oc, tt, ps, t):
        return t.matmul(ps, b_sb[:, oc * 128:(oc + 1) * 128],
                        u_bf[:, tt * TT:(tt + 1) * TT], start=False, stop=True)

    def evac(oc, tt, ps, e):
        dst = res_slice(c, oc, tt)
        e.scalar_tensor_tensor(dst, ps, scale, dst, ALU.mult, ALU.add)
        return e.tensor_scalar(dst, dst, sb_vec[:, oc:oc + 1], None, ALU.add)

    def pw(oc, tt):
        return [("pv", quant_done), ("pv", u_done), bl, ("pv", sbv)]

    linear_T(c, pp, lhsT, lambda kc, tt: xh_rhs(c, kc, tt), evac, DC, DC,
             pre_wait=pw, extra_mm=extra, evac_eng="v")


def emit_l0f(c, scale):
    XH, SB2, SB3, VEC, WT, SCR = (c.sb["XH"], c.sb["SB2"], c.sb["SB3"],
                                  c.sb["VEC"], c.sb["WT"], c.sb["SCR"])
    PSB, PSA = c.sb["PSB"], c.sb["PSA"]
    E = 8
    nm = "L0_ffn"

    gl = ("d0", c.dma("g", SB2[0:128, 0:E * DC].rearrange("p (kc e) -> p kc e", kc=DC),
                      c.dram["L0_ffn_gate_w"].rearrange("(kc p) e -> p kc e", p=128),
                      sem="d0"))
    gbl = ("d1", c.dma("y", VEC[0:8, 2048:2049], c.dram["L0_gate_b8"], sem="d1"))
    esel_bf = c.sb["AUX"].bitcast(BF16)[0:8, 0:1024]
    esl = ("d2", c.dma("g", esel_bf, c.dram["esel"], sem="d2"))
    z = VEC[0:8, 0:T]
    negmax = VEC[0:8, 2049:2050]
    rsum = VEC[0:8, 2050:2051]
    zsum = VEC[0:8, 2051:2052]

    def gmm(t):
        last = None
        for tt2 in range(NTT):
            for kc in range(DC):
                last = t.matmul(PSA[0:8, tt2 * TT:(tt2 + 1) * TT],
                                SB2[:, kc * E:(kc + 1) * E],
                                xh_rhs(c, kc, tt2), start=(kc == 0),
                                stop=(kc == DC - 1))
        return last
    gdone = c.on("t", gmm, waits=[gl], inc="pt", chain=False)

    def gevac(v):
        v.tensor_scalar(z, PSA[0:8, 0:T], VEC[0:8, 2048:2049], None, ALU.add)
        v.tensor_reduce(negmax, z, AX.X, ALU.max, negate=True)
    gz = c.on("v", gevac, waits=[("pt", gdone), gbl])

    gexp = c.on("s", lambda s: s.activation(z, z, AF.Exp, bias=negmax,
                                            accum_out=zsum),
                waits=[("pv", gz)])

    def gnf(v):
        v.reciprocal(rsum, zsum)
        return v.tensor_scalar(z, z, rsum, None, ALU.mult)
    gn = c.on("v", gnf, waits=[("ps", gexp)])

    PT = SCR[:, 0:128]
    MX1 = SCR[:, 128:144]
    MX2 = SCR[:, 144:160]
    SCR8 = SCR[:, 160:168]
    IND = SCR[:, 168:296]
    WTK = SCR[:, 296:424]

    def ptr(t):
        last = None
        for i in range(16):
            last = t.transpose(PSB[:, i * 8:(i + 1) * 8], z[:, i * 128:(i + 1) * 128],
                               c.sb["IDENTF"][0:8, 0:8])
        return last
    ptd = c.on("t", ptr, waits=[("pv", gn)], inc="pt", chain=False)

    def top2(v):
        v.tensor_copy(PT, PSB[:, 0:128])
        v.tensor_reduce(MX1, PT.rearrange("p (i e) -> p i e", e=8), AX.X, ALU.max)
        ge1 = SCR[:, 424:552]
        v.tensor_tensor(ge1.rearrange("p (i e) -> p i e", e=8),
                        PT.rearrange("p (i e) -> p i e", e=8),
                        MX1.rearrange("p (i o) -> p i o", o=1).broadcast_to((128, 16, 8)),
                        ALU.is_ge)
        v.scalar_tensor_tensor(ge1, ge1, -10.0, PT, ALU.mult, ALU.add)
        v.tensor_reduce(MX2, ge1.rearrange("p (i e) -> p i e", e=8), AX.X, ALU.max)
        v.tensor_tensor(IND.rearrange("p (i e) -> p i e", e=8),
                        PT.rearrange("p (i e) -> p i e", e=8),
                        MX2.rearrange("p (i o) -> p i o", o=1).broadcast_to((128, 16, 8)),
                        ALU.is_ge)
        return v.tensor_tensor(WTK, PT, IND, ALU.mult)
    t2d = c.on("v", top2, waits=[("pt", ptd)])

    INDT = VEC.bitcast(BF16)[0:8, 0:T]
    WTKT = VEC.bitcast(BF16)[0:8, 2048:2048 + T]
    tb_prev = t2d
    for i in range(16):
        def tb(t, i=i):
            t.transpose(PSB[0:8, 128:256], IND[:, i * 8:(i + 1) * 8], c.sb["IDENTF"])
            return t.transpose(PSB[0:8, 256:384], WTK[:, i * 8:(i + 1) * 8],
                               c.sb["IDENTF"])
        tbd = c.on("t", tb, waits=[("pv", tb_prev)], inc="pt", chain=False)

        def tbe(v, i=i):
            v.tensor_copy(INDT[:, i * 128:(i + 1) * 128], PSB[0:8, 128:256])
            return v.tensor_copy(WTKT[:, i * 128:(i + 1) * 128], PSB[0:8, 256:384])
        tb_prev = c.on("v", tbe, waits=[("pt", tbd)])
    topk_done = tb_prev

    pp = PsumPool(c, PSA, 4, evac_sem="ps")
    v_sb = c.sb["VS"]
    wbc = c.sb["TRASH2"]
    trash = c.sb["TRASH"]
    esel = esel_bf

    piece_war = []
    ttr_done_prev = None
    prev_layer_evacs = None
    for e in range(E):
        wb_prev = None
        for tt2 in range(NTT):
            def wb(t, e=e, tt2=tt2):
                return t.matmul(PSB[:, 512:1024], esel[0:8, e * 128:(e + 1) * 128],
                                WTKT[:, tt2 * TT:(tt2 + 1) * TT],
                                start=True, stop=True)
            waits = [("pv", topk_done), esl]
            if wb_prev is not None:
                waits.append(("pv", wb_prev))
            if ttr_done_prev is not None:
                waits.append(("pv", ttr_done_prev))
            wbd = c.on("t", wb, waits=waits, inc="pt", chain=False)
            wb_prev = c.on("v", lambda v, tt2=tt2: v.tensor_copy(
                wbc[:, tt2 * TT:(tt2 + 1) * TT], PSB[:, 512:1024]),
                waits=[("pt", wbd)])
        wbe = wb_prev

        for layer in (1, 2, 3):
            wkey = f"{nm}_w{layer}"
            bname = f"{nm}_b{layer}_{e}"
            src_buf = {1: XH, 2: SB3, 3: SB2}[layer]
            dst_buf = {1: SB3, 2: SB2, 3: SB3}[layer]
            ptargets = []
            for piece in range(2):
                k = len(piece_war) + len(ptargets)
                war = piece_war[k - 2] if k >= 2 else None
                tgt = c.dma("g", WT[:, piece * 4096:(piece + 1) * 4096]
                            .rearrange("p (kc o) -> p kc o", kc=DC),
                            c.dram[wkey][e].rearrange("(kc p) o -> p kc o", p=128)
                            [:, :, piece * 512:(piece + 1) * 512],
                            waits=[("pt", war)] if war is not None else [],
                            sem=f"d{3 + piece}")
                ptargets.append((f"d{3 + piece}", tgt))

            def lhsT(kc, oc):
                slot = oc // 4
                return WT[:, slot * 4096 + kc * 512 + (oc % 4) * 128:
                          slot * 4096 + kc * 512 + (oc % 4) * 128 + 128]

            def rhs(kc, tt2, src_buf=src_buf):
                if src_buf is XH:
                    return xh_rhs(c, kc, tt2)
                return src_buf[:, kc * T + tt2 * TT: kc * T + (tt2 + 1) * TT]

            boff = c.vb_off[bname]

            def evac(oc, tt2, ps, s, dst_buf=dst_buf, boff=boff):
                return s.activation(dst_buf[:, oc * T + tt2 * TT: oc * T + (tt2 + 1) * TT],
                                    ps, AF.Relu, bias=c.sb["VB"][:, boff + oc:boff + oc + 1])

            deps_prev = prev_layer_evacs if layer > 1 else None
            extra_dep = ttr_done_prev if (layer == 1 and ttr_done_prev) else None

            def pw(oc, tt2, ptargets=ptargets, deps_prev=deps_prev,
                   extra_dep=extra_dep):
                w = [ptargets[oc // 4], ("pv", topk_done)]
                if deps_prev is not None:
                    w.append(("ps", max(x for row in deps_prev for x in row)))
                if extra_dep is not None:
                    w.append(("pv", extra_dep))
                return w

            prev_layer_evacs = linear_T(c, pp, lhsT, rhs, evac, DC, DC,
                                        pre_wait=pw, evac_eng="s",
                                        loop_order="oc")
            piece_war.extend([c.now("pt"), c.now("pt")])

        y_done = max(x for row in prev_layer_evacs for x in row)

        def vred(v, e=e):
            last = None
            va = v_sb.tensor.ap()[:, 64:65] if False else c.sb["VS"][:, 64:65]
            for oc in range(DC):
                v.tensor_tensor(trash[:, 0:1024], SB3[:, oc * T:oc * T + 1024],
                                wbc[:, 0:1024], ALU.mult)
                v.tensor_reduce(va, trash[:, 0:1024], AX.X, ALU.add)
                v.tensor_tensor(trash[:, 0:1024], SB3[:, oc * T + 1024:(oc + 1) * T],
                                wbc[:, 1024:2048], ALU.mult)
                v.tensor_reduce(v_sb[:, e * 8 + oc:e * 8 + oc + 1],
                                trash[:, 0:1024], AX.X, ALU.add)
                last = v.tensor_tensor(v_sb[:, e * 8 + oc:e * 8 + oc + 1],
                                       v_sb[:, e * 8 + oc:e * 8 + oc + 1], va,
                                       ALU.add)
            return last
        ttr_done_prev = c.on("v", vred, waits=[("ps", y_done), ("pv", wbe)])

    vtp = c.on("t", lambda t: t.transpose(PSB[0:64, 128:256], v_sb[:, 0:64],
                                          c.sb["IDENTF"]),
               waits=[("pv", ttr_done_prev)], inc="pt", chain=False)
    VTF = VEC[0:64, 1792:1920]
    vte = c.on("v", lambda v: v.tensor_copy(VTF, PSB[0:64, 128:256]),
               waits=[("pt", vtp)])
    vdma = c.dma("g", c.dram["v_loc"], VTF, waits=[("pv", vte)], sem="d5")

    def coll(g):
        g.wait_ge(c.sem["d5"], vdma)
        g.collective_compute(
            "AllReduce", ALU.add, replica_groups=[list(range(8))],
            ins=[c.dram["v_loc"].opt()], outs=[c.dram["v_red"].opt()],
        ).then_inc(c.sem["d6"])
    c.block.gpsimd(coll)
    cred = c.inc("d6", 1)

    v_red_sb = VEC.bitcast(BF16)[0:8, T:T + 1024]
    vrl = c.dma("g", v_red_sb, c.dram["v_red"].rearrange("(e o) p -> e (o p)", e=8),
                waits=[("d6", cred)], sem="d7")

    pp2 = PsumPool(c, PSA, 4, evac_sem="pv")

    def olhsT(kc, oc):
        return v_red_sb[:, oc * 128:(oc + 1) * 128]

    def orhs(kc, tt2):
        return INDT[:, tt2 * TT:(tt2 + 1) * TT]

    def oevac(oc, tt2, ps, v):
        dst = res_slice(c, oc, tt2)
        return v.scalar_tensor_tensor(dst, ps, scale, dst, ALU.mult, ALU.add)

    def opw(oc, tt2):
        return [("d7", vrl), ("pv", ttr_done_prev)]

    linear_T(c, pp2, olhsT, orhs, oevac, 1, DC, pre_wait=opw, evac_eng="v")



def emit_l0a(c, scale):
    XH, SB2, SB3, VEC, WT = (c.sb["XH"], c.sb["SB2"], c.sb["SB3"],
                             c.sb["VEC"], c.sb["WT"])
    PSA, PSB = c.sb["PSA"], c.sb["PSB"]
    QS, KS, VSTG, PBUF = c.sb["QS"], c.sb["KS"], c.sb["VSTG"], c.sb["PBUF"]
    VT = c.sb["TRASH"]
    nm = "L0_attn"
    HALF = 1024
    NQT = 8
    KW = 1152
    XIN = SB2[:, 8192:16384]

    unit = 0
    wslot_war = []
    ctx_done_prev = None
    ffn_done = None

    for hf in range(2):
        base = hf * HALF
        lt0 = base - (128 if hf == 1 else 0)
        hw = HALF + (128 if hf == 1 else 0)

        ln1 = None
        nsub = 2 if hw > 1088 else 1
        for sub in range(nsub):
            st0 = lt0 + sub * (hw // 2)
            sw = hw // 2 if nsub == 2 else hw
            pw = [("pv", ffn_done)] if (ffn_done and sub == 0) else (
                [("pv", ln1)] if sub else ())
            ln1 = emit_ln_g(c,
                            src_fn=lambda kc, sl: XH[:, kc * T + sl.start: kc * T + sl.stop],
                            dst_fn=lambda kc, sl, hw=hw, lt0=lt0: SB2[:, kc * hw + (sl.start - lt0): kc * hw + (sl.stop - lt0)],
                            sq_fn=lambda kc, sl, hw=hw, lt0=lt0: SB3[:, kc * hw + (sl.start - lt0): kc * hw + (sl.stop - lt0)],
                            srcb_fn=None, tok0=st0, ntok=sw, stat_off=0,
                            w_off=c.vb_off[nm + "_ln1_w"], b_off=c.vb_off[nm + "_ln1_b"],
                            pre_waits=pw)

        ctx_evac_last = [None] * 8
        for hp in range(8 if L0A_STAGE >= 2 else 0):
            slot = unit % 2
            wof = slot * 4096
            war = wslot_war[unit - 2] if unit >= 2 else None
            wwaits = [("pt", war)] if war is not None else []
            sem = f"d{3 + slot}"
            c.dma("g", WT[:, wof + 0:wof + 1024].rearrange("p (kc m) -> p kc m", kc=DC),
                  c.dram[nm + "_Wq"].rearrange("(kc p) m -> p kc m", p=128)[:, :, hp * 128:(hp + 1) * 128],
                  waits=wwaits, sem=sem)
            c.dma("g", WT[:, wof + 1024:wof + 2048].rearrange("p (kc m) -> p kc m", kc=DC),
                  c.dram[nm + "_Wk"].rearrange("(kc p) m -> p kc m", p=128)[:, :, hp * 128:(hp + 1) * 128],
                  sem=sem)
            tv = c.dma("g", WT[:, wof + 2048:wof + 3072].rearrange("p (kc m) -> p kc m", kc=DC),
                       c.dram[nm + "_Wv"].rearrange("(kc p) m -> p kc m", p=128)[:, :, hp * 128:(hp + 1) * 128],
                       sem=sem)
            wsem = (sem, tv)

            qk_jobs = [("q", 0, TT, 0), ("q", TT, TT, TT),
                       ("k", 0, TT, 128), ("k", TT, TT, 128 + TT)]
            if hf == 1:
                qk_jobs.append(("k", -128, 128, 0))
            qkv_done = []
            j = 0
            for proj, t0, ntokj, dcol in qk_jobs:
                ps = PSA[:, (j % 2) * TT:(j % 2) * TT + ntokj]
                waits = [wsem, ("pv", ln1)]
                if ctx_done_prev is not None:
                    waits.append(("pv", ctx_done_prev))
                if j >= 2:
                    waits.append(("ps", qkv_done[j - 2]))
                wo0 = 0 if proj == "q" else 1024
                dstbuf = QS if proj == "q" else KS
                dw = 1024 if proj == "q" else KW
                bcol = c.vb_off[nm + ("_bq" if proj == "q" else "_bk")] + hp

                def mmq(t, wo0=wo0, t0=t0, ntokj=ntokj, ps=ps, hw=hw, hf=hf):
                    last = None
                    off = 128 if hf == 1 else 0
                    for kc in range(DC):
                        col = kc * hw + off + t0
                        last = t.matmul(ps, WT[:, wof + wo0 + kc * 128: wof + wo0 + (kc + 1) * 128],
                                        SB2[:, col:col + ntokj],
                                        start=(kc == 0), stop=(kc == DC - 1))
                    return last
                mmd = c.on("t", mmq, waits=waits, inc="pt", chain=False)

                def ev(s, ps=ps, dstbuf=dstbuf, bcol=bcol, dcol=dcol,
                       ntokj=ntokj, dw=dw):
                    s.activation(dstbuf[0:64, dcol:dcol + ntokj],
                                 ps[0:64, 0:ntokj], AF.Identity,
                                 bias=c.sb["VB"][0:64, bcol:bcol + 1])
                    return s.activation(dstbuf[0:64, dw + dcol:dw + dcol + ntokj],
                                        ps[64:128, 0:ntokj], AF.Identity,
                                        bias=c.sb["VB"][64:128, bcol:bcol + 1])
                qkv_done.append(c.on("s", ev, waits=[("pt", mmd)]))
                j += 1

            v_jobs = [(0, TT, 1), (TT, TT, 5)]
            if hf == 1:
                v_jobs.append((-128, 128, 0))
            vt_evacs = []
            bcolv = c.vb_off[nm + "_bv"] + hp
            for t0, ntokj, ci0 in v_jobs:
                ps = PSA[:, (j % 2) * TT:(j % 2) * TT + ntokj]
                waits = [wsem, ("pv", ln1)]
                if ctx_done_prev is not None:
                    waits.append(("pv", ctx_done_prev))
                if j >= 2:
                    waits.append(("ps", qkv_done[j - 2]))
                if vt_evacs:
                    waits.append(("pv", vt_evacs[-1]))

                def mmv(t, t0=t0, ntokj=ntokj, ps=ps, hw=hw, hf=hf):
                    last = None
                    off = 128 if hf == 1 else 0
                    for kc in range(DC):
                        col = kc * hw + off + t0
                        last = t.matmul(ps, WT[:, wof + 2048 + kc * 128: wof + 2048 + (kc + 1) * 128],
                                        SB2[:, col:col + ntokj],
                                        start=(kc == 0), stop=(kc == DC - 1))
                    return last
                mmd = c.on("t", mmv, waits=waits, inc="pt", chain=False)

                def vev(s, ps=ps, ntokj=ntokj):
                    s.activation(VSTG[0:64, 0:ntokj], ps[0:64, 0:ntokj],
                                 AF.Identity, bias=c.sb["VB"][0:64, bcolv:bcolv + 1])
                    return s.activation(VSTG[0:64, 512:512 + ntokj],
                                        ps[64:128, 0:ntokj], AF.Identity,
                                        bias=c.sb["VB"][64:128, bcolv:bcolv + 1])
                vevd = c.on("s", vev, waits=[("pt", mmd)])
                qkv_done.append(vevd)
                j += 1
                nch = ntokj // 128
                for h2 in range(2):
                    for cc in range(nch):
                        ci = ci0 + cc
                        pcol = ((h2 * nch + cc) % 2) * 1024

                        def vtr1(t, h2=h2, cc=cc, pcol=pcol):
                            return t.transpose(
                                c.sb["PST"][:, pcol:pcol + 64],
                                VSTG[0:64, h2 * 512 + cc * 128: h2 * 512 + (cc + 1) * 128],
                                c.sb["IDENTB"][0:64, 0:64])
                        w2_ = [("ps", vevd)]
                        if len(vt_evacs) >= 2:
                            w2_.append(("pv", vt_evacs[-2]))
                        vtd = c.on("t", vtr1, waits=w2_, inc="pt", chain=False)
                        vt_evacs.append(c.on(
                            "v", lambda v, h2=h2, ci=ci, pcol=pcol:
                            v.tensor_copy(VT[:, h2 * 576 + ci * 64: h2 * 576 + (ci + 1) * 64],
                                          c.sb["PST"][:, pcol:pcol + 64]),
                            waits=[("pt", vtd)]))
            vt_all = vt_evacs[-1]
            qkv_all = qkv_done[-1]

            TMPF = VEC[:, 1792:2048]
            RCP = VEC[0:1, 2048:2176]
            PTILDE = PBUF[:, 0:256]
            ctx_evacs = []
            rcp_prev = None
            for h2 in range(2 if L0A_STAGE >= 3 else 0):
                for qt in range(NQT):
                    idx = h2 * NQT + qt
                    gqt = hf * NQT + qt
                    has_left = gqt > 0
                    sps = PSA[:, (qt % 2) * TT:(qt % 2) * TT + 256]
                    cps = PSA[:, 1024 + (qt % 2) * TT:1024 + (qt % 2) * TT + 128]
                    qsl = QS[0:64, h2 * 1024 + qt * 128: h2 * 1024 + (qt + 1) * 128]

                    def smm(t, h2=h2, qt=qt, sps=sps, qsl=qsl, has_left=has_left):
                        if has_left:
                            kleft = KS[0:64, h2 * KW + qt * 128: h2 * KW + (qt + 1) * 128]
                            t.matmul(sps[:, 0:128], kleft, qsl, start=True, stop=True)
                        kdiag = KS[0:64, h2 * KW + 128 + qt * 128: h2 * KW + 128 + (qt + 1) * 128]
                        return t.matmul(sps[:, 128:256], kdiag, qsl,
                                        start=True, stop=True)
                    sw = [("pv", vt_all), ("ps", qkv_all)]
                    if idx >= 2:
                        sw.append(("pv", ctx_evacs[idx - 2]))
                    smd = c.on("t", smm, waits=sw, inc="pt", chain=False)

                    def mask(v, sps=sps, has_left=has_left):
                        if has_left:
                            v.tensor_tensor(TMPF[:, 0:128], sps[:, 0:128],
                                            c.sb["MASKT"][:, 0:128], ALU.add)
                        return v.tensor_tensor(TMPF[:, 128:256], sps[:, 128:256],
                                               c.sb["MASKT"][:, 128:256], ALU.add)
                    mkd = c.on("v", mask, waits=[("pt", smd)])

                    def pexp(s, has_left=has_left):
                        if has_left:
                            s.activation(PTILDE[:, 0:128], TMPF[:, 0:128], AF.Exp,
                                         scale=0.125)
                        return s.activation(PTILDE[:, 128:256], TMPF[:, 128:256],
                                            AF.Exp, scale=0.125)
                    expd = c.on("s", pexp, waits=[("pv", mkd)])

                    def summ(t, has_left=has_left):
                        if has_left:
                            t.matmul(PSB[0:1, 0:128], c.sb["ONECOL"], PTILDE[:, 0:128],
                                     start=True, stop=False)
                        return t.matmul(PSB[0:1, 0:128], c.sb["ONECOL"],
                                        PTILDE[:, 128:256],
                                        start=(not has_left), stop=True)
                    sumw = [("ps", expd)]
                    if rcp_prev is not None:
                        sumw.append(("pv", rcp_prev))
                    sumd = c.on("t", summ, waits=sumw, inc="pt", chain=False)

                    rcp_prev = c.on("v", lambda v: v.reciprocal(RCP, PSB[0:1, 0:128]),
                                    waits=[("pt", sumd)])

                    def bcst(t):
                        return t.matmul(PSB[0:64, 512:640], c.sb["ONESF"][0:1, 0:64],
                                        RCP, start=True, stop=True)
                    bcw = [("pv", rcp_prev)]
                    if ctx_evacs:
                        bcw.append(("pv", ctx_evacs[-1]))
                    bcd = c.on("t", bcst, waits=bcw, inc="pt", chain=False)

                    def cmm(t, h2=h2, qt=qt, cps=cps, has_left=has_left):
                        if has_left:
                            vleft = VT[:, h2 * 576 + qt * 64: h2 * 576 + (qt + 1) * 64]
                            t.matmul(cps[0:64, :], vleft, PTILDE[:, 0:128],
                                     start=True, stop=False)
                        vdiag = VT[:, h2 * 576 + (qt + 1) * 64: h2 * 576 + (qt + 2) * 64]
                        return t.matmul(cps[0:64, :], vdiag, PTILDE[:, 128:256],
                                        start=(not has_left), stop=True)
                    cmd = c.on("t", cmm, waits=[("ps", expd)], inc="pt", chain=False)

                    def cev(v, h2=h2, qt=qt, cps=cps, hp=hp):
                        CT = VEC[0:64, 1792:1920]
                        v.tensor_copy(CT, cps[0:64, :])
                        v.tensor_tensor(CT, CT, PSB[0:64, 512:640], ALU.mult)
                        return v.tensor_copy(
                            SB3[h2 * 64:h2 * 64 + 64,
                                hp * 1024 + qt * 128: hp * 1024 + (qt + 1) * 128], CT)
                    ctx_evacs.append(c.on("v", cev, waits=[("pt", cmd), ("pt", bcd)]))
            ctx_done_prev = ctx_evacs[-1] if ctx_evacs else (qkv_done[-1] if False else ctx_done_prev)
            if ctx_evacs:
                pass
            else:
                ctx_done_prev = None
            ctx_evac_last[hp] = ctx_done_prev
            wslot_war.append(c.now("pt"))
            unit += 1

        if L0A_STAGE < 4:
            c.join()
            continue
        attn_all = ctx_done_prev
        wo_l = c.dma("g", WT[:, 0:8192].rearrange("p (kc m) -> p kc m", kc=DC),
                     c.dram[nm + "_Wo"].rearrange("(kc p) m -> p kc m", p=128),
                     waits=[("pt", wslot_war[-1])], sem="d3")
        pp = PsumPool(c, PSA, 4)
        xin_done = []
        for jj in range(16):
            oc, tt2 = jj % DC, jj // DC
            ps = pp.bank()
            war2 = pp.war_wait()
            w = [("d3", wo_l), ("pv", attn_all)]
            if war2 is not None:
                w.append(war2)

            def wmm(t, oc=oc, tt2=tt2, ps=ps):
                last = None
                for kc in range(DC):
                    last = t.matmul(ps, WT[:, kc * 1024 + oc * 128: kc * 1024 + (oc + 1) * 128],
                                    SB3[:, kc * 1024 + tt2 * TT: kc * 1024 + (tt2 + 1) * TT],
                                    start=(kc == 0), stop=(kc == DC - 1))
                return last
            wod = c.on("t", wmm, waits=w, inc="pt", chain=False)

            def wev(v, oc=oc, tt2=tt2, ps=ps):
                dsl = XIN[:, oc * 1024 + tt2 * TT: oc * 1024 + (tt2 + 1) * TT]
                sl_a = slice(base + tt2 * TT, base + (tt2 + 1) * TT)
                v.tensor_tensor(dsl, ps,
                                XH[:, oc * T + sl_a.start: oc * T + sl_a.stop],
                                ALU.add)
                bcol = c.vb_off[nm + "_bo"] + oc
                return v.tensor_scalar(dsl, dsl, c.sb["VB"][:, bcol:bcol + 1],
                                       None, ALU.add)
            val = c.on("v", wev, waits=[("pt", wod)])
            xin_done.append(val)
            pp.push(("pv", val))
        xin_all = xin_done[-1]

        for tt2 in range(2 if L0A_STAGE >= 5 else 0):
            sl_a = slice(base + tt2 * TT, base + (tt2 + 1) * TT)

            ln2 = emit_ln_g(
                c,
                src_fn=lambda kc, sl, tt2=tt2: XIN[:, kc * 1024 + tt2 * TT: kc * 1024 + (tt2 + 1) * TT],
                dst_fn=lambda kc, sl: SB2[:, kc * TT:(kc + 1) * TT],
                sq_fn=lambda kc, sl: SB2[:, 4096 + kc * TT:4096 + (kc + 1) * TT],
                srcb_fn=None, tok0=base + tt2 * TT, ntok=TT, stat_off=0,
                w_off=c.vb_off[nm + "_ln2_w"], b_off=c.vb_off[nm + "_ln2_b"],
                pre_waits=[("pv", xin_all)])

            piece_t = []
            piece_mm = []
            gchain = []
            for piece in range(8 if L0A_STAGE >= 5.5 else 0):
                pslot = piece % 2
                if piece >= 2:
                    lw = [("pt", piece_mm[piece - 2])]
                else:
                    lw = [("pt", c.now("pt"))]
                tgt = c.dma("g", WT[:, pslot * 4096:(pslot + 1) * 4096]
                            .rearrange("p (kc m) -> p kc m", kc=DC),
                            c.dram[nm + "_W1"].rearrange("(kc p) m -> p kc m", p=128)
                            [:, :, piece * 512:(piece + 1) * 512],
                            waits=lw, sem=f"d{3 + pslot}")
                piece_t.append((f"d{3 + pslot}", tgt))
                for sub in range(4):
                    hc = piece * 4 + sub
                    jidx = len(gchain)
                    ps = pp.bank()
                    war2 = pp.war_wait()
                    w = [piece_t[piece], ("pv", ln2)]
                    if war2 is not None:
                        w.append(war2)

                    def hmm(t, sub=sub, pslot=pslot, ps=ps):
                        last = None
                        for kc in range(DC):
                            last = t.matmul(
                                ps, WT[:, pslot * 4096 + kc * 512 + sub * 128:
                                       pslot * 4096 + kc * 512 + sub * 128 + 128],
                                SB2[:, kc * TT:(kc + 1) * TT],
                                start=(kc == 0), stop=(kc == DC - 1))
                        return last
                    hmd = c.on("t", hmm, waits=w, inc="pt", chain=False)

                    pre = c.sb["TRASH"][:, (jidx % 2) * 512:(jidx % 2) * 512 + 512]
                    gt = c.sb["TRASH2"][:, (jidx % 2) * 512:(jidx % 2) * 512 + 512]
                    slot_w = [("ps" if L0A_STAGE < 5.8 else "pv",
                               gchain[jidx - 2])] if jidx >= 2 else []
                    if L0A_STAGE < 5.8:
                        def hevp(s, ps=ps, hc=hc):
                            bcol = c.vb_off[nm + "_b1"] + hc
                            return s.activation(SB3[:, hc * TT:(hc + 1) * TT], ps,
                                                AF.Relu,
                                                bias=c.sb["VB"][:, bcol:bcol + 1])
                        gdone = c.on("s", hevp, waits=[("pt", hmd)])
                        gchain.append(gdone)
                        pp.push(("ps", gdone))
                        continue

                    def s1(s, ps=ps, pre=pre, gt=gt, hc=hc):
                        bcol = c.vb_off[nm + "_b1"] + hc
                        s.activation(pre, ps, AF.Identity,
                                     bias=c.sb["VB"][:, bcol:bcol + 1])
                        return s.activation(gt, pre, AF.Square)
                    s1d = c.on("s", s1, waits=[("pt", hmd)] + slot_w)

                    def v1(v, pre=pre, gt=gt):
                        v.tensor_scalar(gt, gt, 0.0356774081, 0.7978845608,
                                        ALU.mult, ALU.add)
                        return v.tensor_tensor(gt, gt, pre, ALU.mult)
                    v1d = c.on("v", v1, waits=[("ps", s1d)] + slot_w)

                    s2d = c.on("s", lambda s, gt=gt: s.activation(gt, gt, AF.Tanh),
                               waits=[("pv", v1d)])

                    def v2(v, pre=pre, gt=gt, hc=hc):
                        v.tensor_scalar(gt, gt, 1.0, 0.5, ALU.add, ALU.mult)
                        return v.tensor_tensor(SB3[:, hc * TT:(hc + 1) * TT],
                                               gt, pre, ALU.mult)
                    gdone = c.on("v", v2, waits=[("ps", s2d)])
                    gchain.append(gdone)
                    pp.push(("pv", gdone))
                piece_mm.append(c.now("pt"))

            w2_t = []
            w2_mm = []
            for oc in range(DC if L0A_STAGE >= 6 else 0):
                pslot = oc % 2
                lw = [("pt", w2_mm[oc - 2] if oc >= 2 else piece_mm[-1])]
                tgt = c.dma("g", WT[:, pslot * 4096:pslot * 4096 + 4096]
                            .rearrange("p (kc m) -> p kc m", kc=32),
                            c.dram[nm + "_W2"].rearrange("(kc p) m -> p kc m", p=128)
                            [:, :, oc * 128:(oc + 1) * 128],
                            waits=lw, sem=f"d{3 + pslot}")
                w2_t.append((f"d{3 + pslot}", tgt))
                ps = pp.bank()
                war2 = pp.war_wait()
                w = [w2_t[oc], ("pv", gchain[-1])]
                if war2 is not None:
                    w.append(war2)

                def omm(t, pslot=pslot, ps=ps):
                    last = None
                    for kc in range(32):
                        last = t.matmul(ps, WT[:, pslot * 4096 + kc * 128:
                                               pslot * 4096 + (kc + 1) * 128],
                                        SB3[:, kc * TT:(kc + 1) * TT],
                                        start=(kc == 0), stop=(kc == 31))
                    return last
                omd = c.on("t", omm, waits=w, inc="pt", chain=False)
                w2_mm.append(c.now("pt"))

                def oev2(v, oc=oc, tt2=tt2, ps=ps, sl_a=sl_a):
                    xsl = XIN[:, oc * 1024 + tt2 * TT: oc * 1024 + (tt2 + 1) * TT]
                    tmp = c.sb["TRASH"][:, 0:512].bitcast(F32)[:, 0:256] \
                        if False else None
                    bcol = c.vb_off[nm + "_b2"] + oc
                    v.tensor_tensor(xsl, xsl, ps, ALU.add)
                    v.tensor_scalar(xsl, xsl, c.sb["VB"][:, bcol:bcol + 1],
                                    None, ALU.add)
                    rsl = c.sb["RES"][:, oc * T + sl_a.start: oc * T + sl_a.stop]
                    return v.scalar_tensor_tensor(rsl, xsl, scale, rsl,
                                                  ALU.mult, ALU.add)
                val = c.on("v", oev2, waits=[("pt", omd)])
                pp.push(("pv", val))
            ffn_done = c.now("pv")
            c.join()
        if L0A_STAGE < 5:
            ffn_done = None
            c.join()




def _fp4_core(c, v, src_ap, dst_ap, scrA, scrB, VECQ, nf):
    MR = 12582912.0
    nb = nf // 16
    bm, rs, m8f = VECQ
    m8 = (m8f[0][:, 0:nf], m8f[1][:, 0:nf])
    scrA = scrA[:, 0:nf]
    scrB = scrB[:, 0:nf]
    v.tensor_reduce(bm[:, 0:nb], src_ap.rearrange("p (b k) -> p b k", k=16),
                    AX.X, ALU.max, apply_absolute_value=True)
    v.tensor_scalar(bm[:, 0:nb], bm[:, 0:nb], 1.0 / 6.0, None, ALU.mult)
    v.tensor_scalar(bm[:, 0:nb], bm[:, 0:nb], 1e-12, None, ALU.max)
    v.reciprocal(rs[:, 0:nb], bm[:, 0:nb])
    xn = scrA
    v.tensor_tensor(xn.rearrange("p (b k) -> p b k", k=16),
                    src_ap.rearrange("p (b k) -> p b k", k=16),
                    rs[:, 0:nb].rearrange("p (b o) -> p b o", o=1)
                    .broadcast_to((128, nb, 16)), ALU.mult)
    q = scrB
    v.tensor_scalar(q, xn, -1.0, None, ALU.mult)
    v.tensor_tensor(q, q, xn, ALU.max)
    v.tensor_scalar(m8[0], q, 4.5, None, ALU.is_le)
    v.tensor_scalar(m8[1], q, 2.25, None, ALU.is_le)
    v.tensor_scalar(q, xn, 0.5, MR, ALU.mult, ALU.add)
    v.tensor_scalar(q, q, MR, 2.0, ALU.subtract, ALU.mult)
    v.tensor_scalar(xn, xn, MR, MR, ALU.add, ALU.subtract)
    v.select(q, m8[0], xn, q)
    r1 = xn
    v.tensor_tensor(r1.rearrange("p (b k) -> p b k", k=16),
                    src_ap.rearrange("p (b k) -> p b k", k=16),
                    rs[:, 0:nb].rearrange("p (b o) -> p b o", o=1)
                    .broadcast_to((128, nb, 16)), ALU.mult)
    v.tensor_scalar(r1, r1, 2.0, MR, ALU.mult, ALU.add)
    v.tensor_scalar(r1, r1, MR, 0.5, ALU.subtract, ALU.mult)
    v.select(q, m8[1], r1, q)
    return v.tensor_tensor(dst_ap.rearrange("p (b k) -> p b k", k=16),
                           q.rearrange("p (b k) -> p b k", k=16),
                           bm[:, 0:nb].rearrange("p (b o) -> p b o", o=1)
                           .broadcast_to((128, nb, 16)), ALU.mult)


def emit_l2f(c, scale):
    XH, SB2, SB3, VEC, WT, SCR = (c.sb["XH"], c.sb["SB2"], c.sb["SB3"],
                                  c.sb["VEC"], c.sb["WT"], c.sb["SCR"])
    PSA, PSB, PST = c.sb["PSA"], c.sb["PSB"], c.sb["PST"]
    RES = c.sb["RES"]
    I2 = 2816
    IC = 22
    XQ = SB2

    def vq_scr():
        bm = VEC[:, 1024:1088]
        rs = VEC[:, 1088:1152]
        m8 = (SB3.bitcast(U8)[:, 16384:16896], SB3.bitcast(U8)[:, 16896:17408])
        return bm, rs, m8

    if not getattr(c, "l2f_q_done", False):
        c.l2f_q_done = True
        pcs = [("L2_ffn_w_gate", "l2f_wg_scr", 1024, 22),
               ("L2_ffn_w_up", "l2f_wu_scr", 1024, 22),
               ("L2_ffn_w_down", "l2f_wd_scr", I2, 8)]
        prev_spill = None
        prev_q = None
        for wname, sname, pw, npiece in pcs:
            nchunks = pw // 128
            for pi in range(npiece):
                lw = [("pv", prev_q)] if prev_q is not None else []
                ld = c.dma("y", SCR[:, 0:pw], c.dram[wname][pi * 128:(pi + 1) * 128, :],
                           waits=lw, sem="d0")

                def qf2(v, pw=pw):
                    last = None
                    m8w = (SB3.bitcast(U8)[:, 18432:18944],
                           SB3.bitcast(U8)[:, 18944:19456])
                    for off in range(0, pw, 512):
                        w = min(512, pw - off)
                        last = _fp4_core(
                            c, v, SCR[:, off:off + w],
                            SB3[:, 2 * pw + off:2 * pw + off + w],
                            VEC[:, 0:w], VEC[:, 512:512 + w],
                            (VEC[:, 1024:1088], VEC[:, 1088:1152], m8w), w)
                    return last
                qd = c.on("v", qf2, waits=[("d0", ld)])
                prev_q = qd
                TK = SB3[:, 12288:12288 + nchunks * 128]
                tprev = None
                tevs = []
                for kk in range(nchunks):
                    def tp(t, kk=kk, pw=pw):
                        return t.transpose(
                            PST[:, (kk % 2) * 1024:(kk % 2) * 1024 + 128],
                            SB3[:, 2 * pw + kk * 128:2 * pw + (kk + 1) * 128],
                            c.sb["IDENTB2"])
                    w_ = [("pv", qd)]
                    if len(tevs) >= 2:
                        w_.append(("pv", tevs[-2]))
                    td = c.on("t", tp, waits=w_, inc="pt", chain=False)
                    evw = [("pt", td)]
                    if kk == 0 and prev_spill is not None:
                        evw.append(prev_spill)
                    tevs.append(c.on("v", lambda v, kk=kk: v.tensor_copy(
                        TK[:, kk * 128:(kk + 1) * 128],
                        PST[:, (kk % 2) * 1024:(kk % 2) * 1024 + 128]),
                        waits=evw))
                prev_q = tevs[-1]
                sp = c.dma("g", c.dram[sname]
                           .rearrange("(kc p) f -> p kc f", p=128)[:, :, pi * 128:(pi + 1) * 128],
                           TK.rearrange("p (kc f) -> p kc f", f=128),
                           waits=[("pv", prev_q)], sem="d1")
                prev_spill = ("d1", sp)
        c.l2f_spill = prev_spill
        c.join()
    spill_w = [c.l2f_spill] if getattr(c, "l2f_spill", None) else []

    wslot = [None, None]
    prev_ev = {}
    for qt4 in range(4):
        tok0 = qt4 * TT

        lnpre = list(spill_w)
        if prev_ev.get("down"):
            lnpre.append(("pv", prev_ev["down"]))
        ln = emit_ln_g(
            c,
            src_fn=lambda kc, sl: RES[:, kc * T + sl.start: kc * T + sl.stop],
            dst_fn=lambda kc, sl: SCR[:, kc * TT:(kc + 1) * TT],
            sq_fn=lambda kc, sl: SCR[:, 4096 + kc * TT:4096 + (kc + 1) * TT],
            srcb_fn=None, tok0=tok0, ntok=TT, stat_off=0, f32_dst=True,
            f32_sums=True, pre_waits=lnpre)

        xq_done = []
        for g in range(4):
            tevs = []
            for kc in range(DC):
                def tpx(t, g=g, kc=kc):
                    return t.transpose(PSB[:, (kc % 2) * 512:(kc % 2) * 512 + 128],
                                       SCR[:, kc * TT + g * 128: kc * TT + (g + 1) * 128],
                                       c.sb["IDENTF"])
                w_ = [("pv", ln)]
                if len(tevs) >= 2:
                    w_.append(("pv", tevs[-2]))
                if xq_done:
                    w_.append(("pv", xq_done[-1]))
                td = c.on("t", tpx, waits=w_, inc="pt", chain=False)
                tevs.append(c.on("v", lambda v, kc=kc: v.tensor_copy(
                    VEC[:, 1152 + kc * 128:1152 + (kc + 1) * 128],
                    PSB[:, (kc % 2) * 512:(kc % 2) * 512 + 128]),
                    waits=[("pt", td)]))

            def qx(v, g=g):
                last = None
                for off in range(0, 1024, 512):
                    last = _fp4_core(
                        c, v, VEC[:, 1152 + off:1152 + off + 512],
                        c.sb["TRASH"][:, off:off + 512],
                        VEC[:, 0:512], VEC[:, 512:1024], vq_scr(), 512)
                return last
            qxd = c.on("v", qx, waits=[("pv", tevs[-1])])

            bevs = []
            for kc in range(DC):
                def tpb(t, kc=kc):
                    return t.transpose(PST[:, (kc % 2) * 1024:(kc % 2) * 1024 + 128],
                                       c.sb["TRASH"][:, kc * 128:(kc + 1) * 128],
                                       c.sb["IDENTB2"])
                w_ = [("pv", qxd)]
                if len(bevs) >= 2:
                    w_.append(("pv", bevs[-2]))
                td = c.on("t", tpb, waits=w_, inc="pt", chain=False)
                bevs.append(c.on("v", lambda v, kc=kc, g=g: v.tensor_copy(
                    XQ[:, kc * T + tok0 + g * 128: kc * T + tok0 + (g + 1) * 128],
                    PST[:, (kc % 2) * 1024:(kc % 2) * 1024 + 128]),
                    waits=[("pt", td)]))
            xq_done.append(bevs[-1])
        xq_all = xq_done[-1]

        pp = PsumPool(c, PSA, 4)
        fchunks = [(0, 512), (512, 512), (1024, 512), (1536, 512),
                   (2048, 512), (2560, 256)]
        for wi, (wname, act) in enumerate((("l2f_wg_scr", "gate"),
                                           ("l2f_wu_scr", "up"))):
            for ci, (f0, fw) in enumerate(fchunks):
                slot = (wi * len(fchunks) + ci) % 2
                lw = list(spill_w)
                if wslot[slot] is not None:
                    lw.append(("pt", wslot[slot]))
                tgt = c.dma("g", WT[:, slot * 4096:slot * 4096 + DC * fw]
                            .rearrange("p (kc f) -> p kc f", kc=DC),
                            c.dram[wname].rearrange("(kc p) f -> p kc f", p=128)
                            [:, :, f0:f0 + fw],
                            waits=lw, sem=f"d{3 + slot}")
                for g in range(4):
                    ps = pp.bank()
                    war2 = pp.war_wait()
                    w_ = [(f"d{3 + slot}", tgt), ("pv", xq_all)]
                    if war2 is not None:
                        w_.append(war2)

                    def gmm(t, g=g, slot=slot, fw=fw, ps=ps):
                        last = None
                        for kc in range(DC):
                            last = t.matmul(
                                ps[:, 0:fw],
                                XQ[:, kc * T + tok0 + g * 128: kc * T + tok0 + (g + 1) * 128],
                                WT[:, slot * 4096 + kc * fw: slot * 4096 + (kc + 1) * fw],
                                start=(kc == 0), stop=(kc == DC - 1))
                        return last
                    gmd = c.on("t", gmm, waits=w_, inc="pt", chain=False)

                    if act == "gate":
                        def gev(s, g=g, f0=f0, fw=fw, ps=ps):
                            return s.activation(
                                SB3[:, g * I2 + f0: g * I2 + f0 + fw],
                                ps[:, 0:fw], AF.Silu)
                        pp.push(("ps", c.on("s", gev, waits=[("pt", gmd)])))
                    else:
                        def uev(v, g=g, f0=f0, fw=fw, ps=ps):
                            dsl = SB3[:, g * I2 + f0: g * I2 + f0 + fw]
                            return v.tensor_tensor(dsl, dsl, ps[:, 0:fw], ALU.mult)
                        pp.push(("pv", c.on("v", uev, waits=[("pt", gmd)])))
                wslot[slot] = c.now("pt")
        h_done = c.now("pv")

        hq_done = []
        mh8 = (SB3.bitcast(U8)[:, 22528:23040], SB3.bitcast(U8)[:, 23040:23552])
        for g in range(4):
            def qh(v, g=g):
                last = None
                for off in range(0, I2, 512):
                    w = min(512, I2 - off)
                    last = _fp4_core(
                        c, v, SB3[:, g * I2 + off: g * I2 + off + w],
                        SB3[:, g * I2 + off: g * I2 + off + w],
                        VEC[:, 0:512], VEC[:, 512:1024],
                        (VEC[:, 1024:1088], VEC[:, 1088:1152], mh8), w)
                return last
            hq_done.append(c.on("v", qh, waits=[("pv", h_done)] if g == 0 else
                                [("pv", hq_done[-1])]))

        hkt_done = []
        for g in range(4):
            tevs = []
            for kk in range(IC):
                def tph(t, g=g, kk=kk):
                    return t.transpose(PST[:, (kk % 2) * 1024:(kk % 2) * 1024 + 128],
                                       SB3[:, g * I2 + kk * 128: g * I2 + (kk + 1) * 128],
                                       c.sb["IDENTB2"])
                w_ = [("pv", hq_done[g])]
                if len(tevs) >= 2:
                    w_.append(("pv", tevs[-2]))
                td = c.on("t", tph, waits=w_, inc="pt", chain=False)
                tevs.append(c.on("v", lambda v, g=g, kk=kk: v.tensor_copy(
                    SB3[:, g * I2 + kk * 128: g * I2 + (kk + 1) * 128],
                    PST[:, (kk % 2) * 1024:(kk % 2) * 1024 + 128]),
                    waits=[("pt", td)]))
            hkt_done.append(tevs[-1])
        hkt_all = hkt_done[-1]

        down_evs = []
        for wave in range(2):
            ocs = range(wave * 4, wave * 4 + 4)
            mm_last = None
            for kk in range(IC):
                slot = kk % 2
                lw = list(spill_w)
                if wslot[slot] is not None:
                    lw.append(("pt", wslot[slot]))
                tgt = c.dma("g", WT[:, slot * 4096:slot * 4096 + 1024],
                            c.dram["l2f_wd_scr"][kk * 128:(kk + 1) * 128, :],
                            waits=lw, sem=f"d{3 + slot}")

                def dmm(t, kk=kk, slot=slot, ocs=ocs, wave=wave):
                    last = None
                    for oc in ocs:
                        ocl = oc - wave * 4
                        for g in range(4):
                            ps = PSA[:, ocl * 512 + g * 128: ocl * 512 + (g + 1) * 128]
                            last = t.matmul(
                                ps, WT[:, slot * 4096 + oc * 128: slot * 4096 + (oc + 1) * 128],
                                SB3[:, g * I2 + kk * 128: g * I2 + (kk + 1) * 128],
                                start=(kk == 0 and g == 0 and oc == ocs[0] if False
                                       else (kk == 0 and g == 0)),
                                stop=(kk == IC - 1 and g == 3))
                    return last
                w_ = [(f"d{3 + slot}", tgt), ("pv", hkt_all)]
                if down_evs:
                    w_.append(("pv", down_evs[-1]))
                mm_last = c.on("t", dmm, waits=w_, inc="pt", chain=False)
                wslot[slot] = c.now("pt")

            def dev(v, ocs=ocs, wave=wave, tok0=tok0):
                last = None
                for oc in ocs:
                    ocl = oc - wave * 4
                    ps = PSA[:, ocl * 512:(ocl + 1) * 512]
                    rsl = RES[:, oc * T + tok0: oc * T + tok0 + 512]
                    last = v.scalar_tensor_tensor(rsl, ps, scale, rsl,
                                                  ALU.mult, ALU.add)
                return last
            down_evs.append(c.on("v", dev, waits=[("pt", mm_last)]))
        prev_ev["down"] = down_evs[-1]
        c.join()


def emit_fp4_quant(c, v, s, src, dst, nf, scratch_f32, scratch2_f32, nb):
    MR = 12582912.0

    def step1(v):
        bm = scratch2_f32[:, 0:nb]
        v.tensor_reduce(bm, src.rearrange("p (b k) -> p b k", k=16), AX.X,
                        ALU.max, apply_absolute_value=True)
        v.tensor_scalar(bm, bm, 1.0 / 6.0, 1e-12, ALU.mult, ALU.max)
        rs = scratch2_f32[:, nb:2 * nb]
        v.reciprocal(rs, bm)
        xn = scratch_f32[:, 0:nf]
        v.tensor_tensor(xn.rearrange("p (b k) -> p b k", k=16),
                        src.rearrange("p (b k) -> p b k", k=16),
                        rs.rearrange("p (b o) -> p b o", o=1).broadcast_to((128, nb, 16)),
                        ALU.mult)
        aa = scratch2_f32[:, 2 * nb:2 * nb + nf]
        return v.tensor_scalar(aa, xn, 0.0, None, ALU.abs_max)
    p1 = c.on("v", step1)

    xn = scratch_f32[:, 0:nf]
    aa = scratch2_f32[:, 2 * nb:2 * nb + nf]

    def step2(v):
        r1 = scratch2_f32[:, 2 * nb + nf:2 * nb + 2 * nf]
        v.tensor_scalar(r1, xn, 2.0, MR, ALU.mult, ALU.add)
        v.tensor_scalar(r1, r1, MR, 0.5, ALU.subtract, ALU.mult)
        r3 = scratch_f32[:, nf:2 * nf] if False else None
        q = scratch2_f32[:, 2 * nb + 2 * nf:2 * nb + 3 * nf]
        v.tensor_scalar(q, xn, 0.5, MR, ALU.mult, ALU.add)
        v.tensor_scalar(q, q, MR, 2.0, ALU.subtract, ALU.mult)
        r2 = xn
        v.tensor_scalar(r2, xn, MR, None, ALU.add)
        v.tensor_scalar(r2, r2, MR, None, ALU.subtract)
        m = scratch2_f32.bitcast(U8)[:, 0:nf]
        v.tensor_scalar(m, aa, 4.5, None, ALU.is_le)
        v.select(q, m, r2, q)
        v.tensor_scalar(m, aa, 2.25, None, ALU.is_le)
        v.select(q, m, r1, q)
        return v.tensor_tensor(dst.rearrange("p (b k) -> p b k", k=16),
                               q.rearrange("p (b k) -> p b k", k=16),
                               scratch2_f32[:, 0:nb]
                               .rearrange("p (b o) -> p b o", o=1)
                               .broadcast_to((128, nb, 16)),
                               ALU.mult)
    return c.on("v", step2, waits=[("pv", p1)])



def emit_l1f(c, scale):
    XH, SB2, SB3, VEC, WT, SCR = (c.sb["XH"], c.sb["SB2"], c.sb["SB3"],
                                  c.sb["VEC"], c.sb["WT"], c.sb["SCR"])
    PSA, PSB, PST = c.sb["PSA"], c.sb["PSB"], c.sb["PST"]
    PBUF = c.sb["PBUF"]
    nm = "L1_ffn"
    NQT = 16
    QS = SB2[0:64, 0:4096]
    KS = SB2[0:64, 4096:8192]
    VT = SB2[:, 8192:10240]
    VSTG = SB2[0:64, 10240:11264]

    unit = 0
    wslot_war = []
    ctx_done_prev = None
    for hp in range(8):
        slot = unit % 2
        wof = slot * 4096
        war = wslot_war[unit - 2] if unit >= 2 else None
        wwaits = [("pt", war)] if war is not None else []
        sem = f"d{3 + slot}"
        c.dma("g", WT[:, wof + 0:wof + 1024].rearrange("p (kc m) -> p kc m", kc=DC),
              c.dram[nm + "_Wq"].rearrange("(kc p) m -> p kc m", p=128)[:, :, hp * 128:(hp + 1) * 128],
              waits=wwaits, sem=sem)
        c.dma("g", WT[:, wof + 1024:wof + 2048].rearrange("p (kc m) -> p kc m", kc=DC),
              c.dram[nm + "_Wk"].rearrange("(kc p) m -> p kc m", p=128)[:, :, hp * 128:(hp + 1) * 128],
              sem=sem)
        tv = c.dma("g", WT[:, wof + 2048:wof + 3072].rearrange("p (kc m) -> p kc m", kc=DC),
                   c.dram[nm + "_Wv"].rearrange("(kc p) m -> p kc m", p=128)[:, :, hp * 128:(hp + 1) * 128],
                   sem=sem)
        wsem = (sem, tv)

        qkv_done = []
        j = 0
        for proj, wo0, dstbuf, dw, bname in (
                ("q", 0, QS, 4096, "_bq"), ("k", 1024, KS, 4096, "_bk")):
            bcol = c.vb_off[nm + bname] + hp
            for tt2 in range(4):
                ps = PSA[:, (j % 2) * TT:(j % 2) * TT + TT]
                waits = [wsem]
                if ctx_done_prev is not None:
                    waits.append(("pv", ctx_done_prev))
                if j >= 2:
                    waits.append(("ps", qkv_done[j - 2]))

                def mmq(t, wo0=wo0, tt2=tt2, ps=ps):
                    last = None
                    for kc in range(DC):
                        last = t.matmul(ps, WT[:, wof + wo0 + kc * 128: wof + wo0 + (kc + 1) * 128],
                                        xh_rhs(c, kc, tt2),
                                        start=(kc == 0), stop=(kc == DC - 1))
                    return last
                mmd = c.on("t", mmq, waits=waits, inc="pt", chain=False)

                def ev(s, tt2=tt2, ps=ps, dstbuf=dstbuf, bcol=bcol, dw=dw):
                    s.activation(dstbuf[0:64, tt2 * TT:(tt2 + 1) * TT],
                                 ps[0:64, :], AF.Identity,
                                 bias=c.sb["VB"][0:64, bcol:bcol + 1])
                    return s.activation(dstbuf[0:64, dw // 2 * 0 + 2048 + tt2 * TT:2048 + (tt2 + 1) * TT],
                                        ps[64:128, :], AF.Identity,
                                        bias=c.sb["VB"][64:128, bcol:bcol + 1])
                qkv_done.append(c.on("s", ev, waits=[("pt", mmd)]))
                j += 1

        vt_evacs = []
        bcolv = c.vb_off[nm + "_bv"] + hp
        for tt2 in range(4):
            ps = PSA[:, (j % 2) * TT:(j % 2) * TT + TT]
            waits = [wsem]
            if ctx_done_prev is not None:
                waits.append(("pv", ctx_done_prev))
            if j >= 2:
                waits.append(("ps", qkv_done[j - 2]))
            if vt_evacs:
                waits.append(("pv", vt_evacs[-1]))

            def mmv(t, tt2=tt2, ps=ps):
                last = None
                for kc in range(DC):
                    last = t.matmul(ps, WT[:, wof + 2048 + kc * 128: wof + 2048 + (kc + 1) * 128],
                                    xh_rhs(c, kc, tt2),
                                    start=(kc == 0), stop=(kc == DC - 1))
                return last
            mmd = c.on("t", mmv, waits=waits, inc="pt", chain=False)

            def vev(s, ps=ps):
                s.activation(VSTG[0:64, 0:TT], ps[0:64, :], AF.Identity,
                             bias=c.sb["VB"][0:64, bcolv:bcolv + 1])
                return s.activation(VSTG[0:64, 512:1024], ps[64:128, :],
                                    AF.Identity,
                                    bias=c.sb["VB"][64:128, bcolv:bcolv + 1])
            vevd = c.on("s", vev, waits=[("pt", mmd)])
            qkv_done.append(vevd)
            j += 1
            for h2 in range(2):
                for cc in range(4):
                    ci = tt2 * 4 + cc
                    idx = len(vt_evacs)
                    pcol = (idx % 2) * 1024

                    def vtr1(t, h2=h2, cc=cc, pcol=pcol):
                        return t.transpose(
                            c.sb["PST"][:, pcol:pcol + 64],
                            VSTG[0:64, h2 * 512 + cc * 128: h2 * 512 + (cc + 1) * 128],
                            c.sb["IDENTB"][0:64, 0:64])
                    w2_ = [("ps", vevd)]
                    if idx >= 2:
                        w2_.append(("pv", vt_evacs[idx - 2]))
                    vtd = c.on("t", vtr1, waits=w2_, inc="pt", chain=False)
                    vt_evacs.append(c.on(
                        "v", lambda v, h2=h2, ci=ci, pcol=pcol:
                        v.tensor_copy(VT[:, h2 * 1024 + ci * 64: h2 * 1024 + (ci + 1) * 64],
                                      c.sb["PST"][:, pcol:pcol + 64]),
                        waits=[("pt", vtd)]))
        vt_all = vt_evacs[-1]
        qkv_all = qkv_done[-1]

        RCP = VEC[0:1, 2048:2176]
        PTILDE = PBUF[:, 0:128]
        ctx_evacs = []
        rcp_prev = None
        for h2 in range(2):
            for qt in range(NQT):
                idx = h2 * NQT + qt
                qsl = QS[0:64, h2 * 2048 + qt * 128: h2 * 2048 + (qt + 1) * 128]
                cps = PSA[:, 1024 + (qt % 2) * TT:1024 + (qt % 2) * TT + 128]
                exp_prev = None
                for kc16 in range(16):
                    sps = PSA[:, (kc16 % 2) * TT:(kc16 % 2) * TT + 128]

                    def smm(t, h2=h2, kc16=kc16, sps=sps, qsl=qsl):
                        return t.matmul(sps, KS[0:64, h2 * 2048 + kc16 * 128: h2 * 2048 + (kc16 + 1) * 128],
                                        qsl, start=True, stop=True)
                    sw = [("pv", vt_all), ("ps", qkv_all)]
                    if exp_prev is not None and kc16 >= 2:
                        sw.append(("ps", exp_hist[kc16 - 2]))
                    if idx >= 2 and kc16 == 0:
                        sw.append(("pv", ctx_evacs[idx - 2]))
                    smd = c.on("t", smm, waits=sw, inc="pt", chain=False)
                    if kc16 == 0:
                        exp_hist = []

                    def pexp(s, sps=sps, kc16=kc16):
                        return s.activation(PTILDE, sps, AF.Exp, scale=0.125)
                    pw_ = [("pt", smd)]
                    if exp_prev is not None:
                        pw_.append(("pt", ctxmm_hist[kc16 - 1]))
                    exp_prev = c.on("s", pexp, waits=pw_)
                    exp_hist.append(exp_prev)
                    if kc16 == 0:
                        ctxmm_hist = []

                    def summ_ctx(t, h2=h2, qt=qt, kc16=kc16, cps=cps):
                        t.matmul(PSB[0:1, 0:128], c.sb["ONECOL"], PTILDE,
                                 start=(kc16 == 0), stop=(kc16 == 15))
                        return t.matmul(cps[0:64, :],
                                        VT[:, h2 * 1024 + kc16 * 64: h2 * 1024 + (kc16 + 1) * 64],
                                        PTILDE, start=(kc16 == 0), stop=(kc16 == 15))
                    scw = [("ps", exp_prev)]
                    if rcp_prev is not None and kc16 == 0:
                        scw.append(("pv", rcp_prev))
                    ctxmm_hist.append(c.on("t", summ_ctx, waits=scw, inc="pt",
                                           chain=False))

                rcp_prev = c.on("v", lambda v: v.reciprocal(RCP, PSB[0:1, 0:128]),
                                waits=[("pt", ctxmm_hist[-1])])

                def bcst(t):
                    return t.matmul(PSB[0:64, 512:640], c.sb["ONESF"][0:1, 0:64],
                                    RCP, start=True, stop=True)
                bcw = [("pv", rcp_prev)]
                if ctx_evacs:
                    bcw.append(("pv", ctx_evacs[-1]))
                bcd = c.on("t", bcst, waits=bcw, inc="pt", chain=False)

                def cev(v, h2=h2, qt=qt, cps=cps, hp=hp):
                    CT = VEC[0:64, 1792:1920]
                    v.tensor_copy(CT, cps[0:64, :])
                    v.tensor_tensor(CT, CT, PSB[0:64, 512:640], ALU.mult)
                    return v.tensor_copy(
                        SB3[h2 * 64:h2 * 64 + 64,
                            hp * 2048 + qt * 128: hp * 2048 + (qt + 1) * 128], CT)
                ctx_evacs.append(c.on("v", cev, waits=[("pt", cmd0 := bcd),
                                                       ("pt", ctxmm_hist[-1])]))
        ctx_done_prev = ctx_evacs[-1]
        wslot_war.append(c.now("pt"))
        unit += 1

    attn_all = ctx_done_prev
    wo_l = c.dma("g", WT[:, 0:8192].rearrange("p (kc m) -> p kc m", kc=DC),
                 c.dram[nm + "_Wo"].rearrange("(kc p) m -> p kc m", p=128),
                 waits=[("pt", wslot_war[-1])], sem="d3")
    AO = SB2
    pp = PsumPool(c, PSA, 4)
    ao_done = []
    for jj in range(32):
        oc, tt2 = jj % DC, jj // DC
        ps = pp.bank()
        war2 = pp.war_wait()
        w = [("d3", wo_l), ("pv", attn_all)]
        if war2 is not None:
            w.append(war2)

        def wmm(t, oc=oc, tt2=tt2, ps=ps):
            last = None
            for kc in range(DC):
                last = t.matmul(ps, WT[:, kc * 1024 + oc * 128: kc * 1024 + (oc + 1) * 128],
                                SB3[:, kc * T + tt2 * TT: kc * T + (tt2 + 1) * TT],
                                start=(kc == 0), stop=(kc == DC - 1))
            return last
        wod = c.on("t", wmm, waits=w, inc="pt", chain=False)

        def wev(s, oc=oc, tt2=tt2, ps=ps):
            bcol = c.vb_off[nm + "_bo"] + oc
            return s.activation(AO[:, oc * T + tt2 * TT: oc * T + (tt2 + 1) * TT],
                                ps, AF.Identity,
                                bias=c.sb["VB"][:, bcol:bcol + 1])
        val = c.on("s", wev, waits=[("pt", wod)])
        ao_done.append(val)
        pp.push(("ps", val))
    ao_all = ao_done[-1]

    ln1 = emit_ln_g(
        c,
        src_fn=lambda kc, sl: AO[:, kc * T + sl.start: kc * T + sl.stop],
        dst_fn=lambda kc, sl: SB3[:, kc * T + sl.start: kc * T + sl.stop],
        sq_fn=lambda kc, sl: SB3[:, kc * T + sl.start: kc * T + sl.stop],
        srcb_fn=None, tok0=0, ntok=1024, stat_off=0,
        w_off=c.vb_off[nm + "_g1"], b_off=c.vb_off[nm + "_be1"],
        pre_waits=[("ps", ao_all)])
    ln1b = emit_ln_g(
        c,
        src_fn=lambda kc, sl: AO[:, kc * T + sl.start: kc * T + sl.stop],
        dst_fn=lambda kc, sl: SB3[:, kc * T + sl.start: kc * T + sl.stop],
        sq_fn=lambda kc, sl: SB3[:, kc * T + sl.start: kc * T + sl.stop],
        srcb_fn=None, tok0=1024, ntok=1024, stat_off=0,
        w_off=c.vb_off[nm + "_g1"], b_off=c.vb_off[nm + "_be1"],
        pre_waits=[("ps", ao_all), ("pv", ln1)])

    x1d = c.on("v", lambda v: v.tensor_tensor(XH[:, 0:DC * T], XH[:, 0:DC * T],
                                              SB3[:, 0:DC * T], ALU.add),
               waits=[("pv", ln1b)])

    gl = ("d0", c.dma("g", c.sb["TRASH"][0:128, 0:E8 * DC]
                      .rearrange("p (kc e) -> p kc e", kc=DC),
                      c.dram[nm + "_Wg"].rearrange("(kc p) e -> p kc e", p=128),
                      sem="d0"))
    gbl = ("d1", c.dma("y", VEC[0:8, 2048:2049], c.dram["L1_gb8"], sem="d1",
                       waits=[("pv", x1d)]))
    esl = ("d2", c.dma("g", c.sb["AUX"].bitcast(BF16)[0:8, 0:1024],
                       c.dram["esel"], sem="d2"))
    bexl = ("d6", c.dma("g", SB2[0:8, 15360:16384], c.dram[nm + "_bexp"],
                        sem="d6", waits=[("pv", x1d)]))

    def gmm(t):
        last = None
        for tt2 in range(NTT):
            for kc in range(DC):
                last = t.matmul(PSA[0:8, tt2 * TT:(tt2 + 1) * TT],
                                c.sb["TRASH"][:, kc * E8:(kc + 1) * E8],
                                xh_rhs(c, kc, tt2), start=(kc == 0),
                                stop=(kc == DC - 1))
        return last
    gdone = c.on("t", gmm, waits=[gl, ("pv", x1d)], inc="pt", chain=False)

    z8 = VEC[0:8, 0:T]
    gz = c.on("v", lambda v: v.tensor_scalar(z8, PSA[0:8, 0:T],
                                             VEC[0:8, 2048:2049], None, ALU.add),
              waits=[("pt", gdone), gbl])

    PT = SCR[:, 0:128]
    MX1 = SCR[:, 128:144]
    MX2 = SCR[:, 144:160]
    GE1 = SCR[:, 160:288]
    WTK = SCR[:, 288:416]
    SM = SCR[:, 416:432]

    def ptr(t):
        last = None
        for i in range(16):
            last = t.transpose(PSB[:, i * 8:(i + 1) * 8], z8[:, i * 128:(i + 1) * 128],
                               c.sb["IDENTF"][0:8, 0:8])
        return last
    ptd = c.on("t", ptr, waits=[("pv", gz)], inc="pt", chain=False)

    def sfm(v):
        v.tensor_copy(PT, PSB[:, 0:128])
        v.tensor_reduce(MX1, PT.rearrange("p (i e) -> p i e", e=8), AX.X, ALU.max,
                        negate=True)
        v.tensor_tensor(PT.rearrange("p (i e) -> p i e", e=8),
                        PT.rearrange("p (i e) -> p i e", e=8),
                        MX1.rearrange("p (i o) -> p i o", o=1).broadcast_to((128, 16, 8)),
                        ALU.add)
        return PT
    sf1 = c.on("v", sfm, waits=[("pt", ptd)])
    sf2 = c.on("s", lambda s: s.activation(PT, PT, AF.Exp), waits=[("pv", sf1)])

    def sfm2(v):
        v.tensor_reduce(SM, PT.rearrange("p (i e) -> p i e", e=8), AX.X, ALU.add)
        v.reciprocal(SM, SM)
        v.tensor_tensor(PT.rearrange("p (i e) -> p i e", e=8),
                        PT.rearrange("p (i e) -> p i e", e=8),
                        SM.rearrange("p (i o) -> p i o", o=1).broadcast_to((128, 16, 8)),
                        ALU.mult)
        v.tensor_reduce(MX1, PT.rearrange("p (i e) -> p i e", e=8), AX.X, ALU.max)
        v.tensor_tensor(GE1.rearrange("p (i e) -> p i e", e=8),
                        PT.rearrange("p (i e) -> p i e", e=8),
                        MX1.rearrange("p (i o) -> p i o", o=1).broadcast_to((128, 16, 8)),
                        ALU.is_ge)
        v.scalar_tensor_tensor(GE1, GE1, -10.0, PT, ALU.mult, ALU.add)
        v.tensor_reduce(MX2, GE1.rearrange("p (i e) -> p i e", e=8), AX.X, ALU.max)
        v.tensor_tensor(GE1.rearrange("p (i e) -> p i e", e=8),
                        PT.rearrange("p (i e) -> p i e", e=8),
                        MX2.rearrange("p (i o) -> p i o", o=1).broadcast_to((128, 16, 8)),
                        ALU.is_ge)
        return v.tensor_tensor(WTK, PT, GE1, ALU.mult)
    t2d = c.on("v", sfm2, waits=[("ps", sf2)])

    WTKT = VEC.bitcast(BF16)[0:8, 2048:2048 + T]
    tb_prev = t2d
    for i in range(16):
        def tb(t, i=i):
            return t.transpose(PSB[0:8, 128:256], WTK[:, i * 8:(i + 1) * 8],
                               c.sb["IDENTF"])
        tbd = c.on("t", tb, waits=[("pv", tb_prev)], inc="pt", chain=False)
        tb_prev = c.on("v", lambda v, i=i: v.tensor_copy(
            WTKT[:, i * 128:(i + 1) * 128], PSB[0:8, 128:256]), waits=[("pt", tbd)])
    topk_done = tb_prev
    _l1f_moe(c, scale, topk_done, esl, x1d, bexl)



E8 = 8


def _l1f_moe(c, scale, topk_done, esl, x1d, bexl):
    XH, SB2, SB3, VEC, WT = (c.sb["XH"], c.sb["SB2"], c.sb["SB3"],
                             c.sb["VEC"], c.sb["WT"])
    PSA, PSB = c.sb["PSA"], c.sb["PSB"]
    nm = "L1_ffn"
    WTKT = VEC.bitcast(BF16)[0:8, 2048:2048 + T]
    esel_bf = c.sb["AUX"].bitcast(BF16)[0:8, 0:1024]
    bexp_sb = SB2[0:8, 15360:16384]

    wslot2 = [None, None]
    prev_tile_ev = None
    for tt2 in range(NTT):
        wb_done = []
        for e in range(E8):
            def wb(t, e=e, tt2=tt2):
                return t.matmul(PSB[:, 512:1024], esel_bf[0:8, e * 128:(e + 1) * 128],
                                WTKT[:, tt2 * TT:(tt2 + 1) * TT], start=True,
                                stop=True)
            w_ = [("pv", topk_done), esl]
            if wb_done:
                w_.append(("pv", wb_done[-1]))
            elif prev_tile_ev is not None:
                w_.append(("pv", prev_tile_ev))
            wbd = c.on("t", wb, waits=w_, inc="pt", chain=False)
            wb_done.append(c.on("v", lambda v, e=e: v.tensor_copy(
                SB2[:, 11264 + e * 512:11264 + (e + 1) * 512], PSB[:, 512:1024]),
                waits=[("pt", wbd)]))
        wb_all = wb_done[-1]

        epoch_ev = []
        for half in range(2):
            mm_last = None
            for e in range(E8):
                def xwf(v, tt2=tt2, e=e):
                    return v.tensor_tensor(
                        SB2[:, 0:DC * TT].rearrange("p (kc f) -> p kc f", f=TT),
                        XH.rearrange("p (kc f) -> p kc f", f=T)[:, :, tt2 * TT:(tt2 + 1) * TT],
                        SB2[:, 11264 + e * 512:11264 + (e + 1) * 512]
                        .rearrange("p (o f) -> p o f", o=1).broadcast_to((128, DC, TT)),
                        ALU.mult)
                xww = [("pv", wb_all), ("pv", x1d)]
                if mm_last is not None:
                    xww.append(("pt", mm_last))
                elif epoch_ev:
                    xww.append(("pv", epoch_ev[-1]))
                xwd = c.on("v", xwf, waits=xww)

                slot = e % 2
                lw = [("pv", xwd)]
                if wslot2[slot] is not None:
                    lw.append(("pt", wslot2[slot]))
                tgt = c.dma("g", WT[:, slot * 4096:(slot + 1) * 4096]
                            .rearrange("p (kc m) -> p kc m", kc=DC),
                            c.dram[nm + "_We"][e].rearrange("(kc p) m -> p kc m", p=128)
                            [:, :, half * 512:(half + 1) * 512],
                            waits=lw, sem=f"d{3 + slot}")

                def emm(t, e=e, slot=slot):
                    last = None
                    for oc4 in range(4):
                        ps = PSA[:, oc4 * 512:(oc4 + 1) * 512]
                        for kc in range(DC):
                            last = t.matmul(
                                ps, WT[:, slot * 4096 + kc * 512 + oc4 * 128:
                                       slot * 4096 + kc * 512 + oc4 * 128 + 128],
                                SB2[:, kc * TT:(kc + 1) * TT],
                                start=(e == 0 and kc == 0), stop=False)
                    return last
                mm_last = c.on("t", emm, waits=[(f"d{3 + slot}", tgt), ("pv", xwd),
                                                bexl], inc="pt", chain=False)
                wslot2[slot] = c.now("pt")

            def bmm(t, tt2=tt2, half=half):
                last = None
                for oc4 in range(4):
                    oc = half * 4 + oc4
                    ps = PSA[:, oc4 * 512:(oc4 + 1) * 512]
                    last = t.matmul(ps, bexp_sb[0:8, oc * 128:(oc + 1) * 128],
                                    WTKT[:, tt2 * TT:(tt2 + 1) * TT],
                                    start=False, stop=True)
                return last
            bmd = c.on("t", bmm, waits=[("pt", mm_last)], inc="pt", chain=False)

            def mev(v, tt2=tt2, half=half):
                last = None
                for oc4 in range(4):
                    oc = half * 4 + oc4
                    ps = PSA[:, oc4 * 512:(oc4 + 1) * 512]
                    last = v.tensor_copy(
                        SB3[:, oc * T + tt2 * TT: oc * T + (tt2 + 1) * TT], ps)
                return last
            epoch_ev.append(c.on("v", mev, waits=[("pt", bmd)]))
        prev_tile_ev = epoch_ev[-1]
    moe_all = prev_tile_ev
    if L1F_SKIP_LN2:
        return

    lnm = None
    for hf in range(2):
        lnm_pre = [("pv", moe_all)] + ([("pv", lnm)] if lnm else [])
        lnm = emit_ln_g(
            c,
            src_fn=lambda kc, sl: SB3[:, kc * T + sl.start: kc * T + sl.stop],
            dst_fn=lambda kc, sl: SB2[:, kc * 1024 + (sl.start - hf * 1024):
                                      kc * 1024 + (sl.stop - hf * 1024)],
            sq_fn=lambda kc, sl: SB2[:, 8192 + kc * 1024 + (sl.start - hf * 1024):
                                     8192 + kc * 1024 + (sl.stop - hf * 1024)],
            srcb_fn=None, tok0=hf * 1024, ntok=1024, stat_off=0,
            w_off=c.vb_off[nm + "_g2"], b_off=c.vb_off[nm + "_be2"],
            pre_waits=lnm_pre)

        def fin(v, hf=hf):
            last = None
            for kc in range(DC):
                for tt2 in range(2):
                    sl = slice(hf * 1024 + tt2 * TT, hf * 1024 + (tt2 + 1) * TT)
                    lsl = SB2[:, kc * 1024 + tt2 * TT: kc * 1024 + (tt2 + 1) * TT]
                    rsl = c.sb["RES"][:, kc * T + sl.start: kc * T + sl.stop]
                    xsl = XH[:, kc * T + sl.start: kc * T + sl.stop]
                    v.scalar_tensor_tensor(rsl, xsl, scale, rsl, ALU.mult, ALU.add)
                    last = v.scalar_tensor_tensor(rsl, lsl, scale, rsl, ALU.mult,
                                                  ALU.add)
            return last
        c.on("v", fin, waits=[("pv", lnm)])


DI = 2048
NDCI = 16
MAMBA_TRUNC = 0
MA_ALOG, MA_BDT, MA_D, MA_CW, MA_CB, MA_NW, MA_NEGB = 0, 256, 272, 288, 352, 368, 376


def mamba_dir_load(c, aux_dram):
    AUX = c.sb["AUX"]
    ld = c.dma("y", AUX[:, 0:MA_NEGB], aux_dram, sem="d7")

    def prepa(s):
        return s.activation(AUX[:, 0:256], AUX[:, 0:256], AF.Exp)
    pa = c.on("s", prepa, waits=[("d7", ld)])

    def prepn(v):
        v.tensor_scalar(AUX[:, 0:256], AUX[:, 0:256], -1.0, None, ALU.mult)
        return v.tensor_scalar(AUX[:, MA_NEGB:MA_NEGB + 16],
                               AUX[:, MA_BDT:MA_BDT + 16], -1.0, None, ALU.mult)
    return c.on("v", prepn, waits=[("ps", pa)])


def emit_rms(c, w_col, dst_base, dst_key="SB3", eps=LN_EPS):
    XH, SB2, VEC, PSA, PSB = (c.sb["XH"], c.sb["SB2"], c.sb["VEC"],
                              c.sb["PSA"], c.sb["PSB"])
    DST = c.sb[dst_key]
    last = None
    for hf in range(2):
        t0 = hf * 1024

        def sq(se):
            l = None
            for kc in range(DC):
                l = se.activation(SB2[:, kc * 2048 + 0:kc * 2048 + 1024],
                                  XH[:, kc * T + t0:kc * T + t0 + 1024],
                                  AF.Square)
            return l
        sqd = c.on("s", sq, waits=([("pv", last)] if last else []))

        def mm(t):
            l = None
            for tt2 in range(2):
                ps = PSA[0:1, tt2 * TT:(tt2 + 1) * TT]
                for kc in range(DC):
                    l = t.matmul(ps, c.sb["ONECOL"],
                                 SB2[:, kc * 2048 + tt2 * TT:kc * 2048 + (tt2 + 1) * TT],
                                 start=(kc == 0), stop=(kc == DC - 1))
            return l
        mmw = [("ps", sqd)] + ([("pv", c.now("pv"))] if hf else [])
        mmd = c.on("t", mm, waits=mmw, inc="pt", chain=False)

        def stats(v):
            return v.tensor_scalar(VEC[0:1, 0:1024], PSA[0:1, 0:1024],
                                   1.0 / D, eps, ALU.mult, ALU.add)
        st = c.on("v", stats, waits=[("pt", mmd)])

        def rsf(s):
            s.activation(VEC[0:1, 0:1024], VEC[0:1, 0:1024], AF.Ln)
            return s.activation(VEC[0:1, 0:1024], VEC[0:1, 0:1024], AF.Exp,
                                scale=-0.5)
        rs = c.on("s", rsf, waits=[("pv", st)])

        def bc(t):
            t.matmul(PSB[:, 0:512], c.sb["ONESF"][0:1, :], VEC[0:1, 0:512],
                     start=True, stop=True)
            return t.matmul(PSB[:, 512:1024], c.sb["ONESF"][0:1, :],
                            VEC[0:1, 512:1024], start=True, stop=True)
        bcd = c.on("t", bc, waits=[("ps", rs)], inc="pt", chain=False)

        def norm(v, t0=t0):
            l = None
            for kc in range(DC):
                d = DST[:, dst_base + kc * T + t0:dst_base + kc * T + t0 + 1024]
                v.tensor_tensor(d, XH[:, kc * T + t0:kc * T + t0 + 1024],
                                PSB[:, 0:1024], ALU.mult)
                l = v.tensor_scalar(d, d, w_col(kc), None, ALU.mult)
            return l
        last = c.on("v", norm, waits=[("pt", bcd)])
    return last


def dmabar(c, *sems):
    tgt = [(s, c.cnt[s]) for s in sems if c.cnt[s] > 0]
    for key in ("t", "v", "s", "g", "y"):
        def f(e, tgt=tgt):
            for s_, v_ in tgt:
                e.wait_ge(c.sem[s_], v_)
        c._sec(key)(f)


def _stream_lin(c, dram_ap, piece_ids, wt_off, rhs_fn, evac_fn, pool,
                slot_w=4096, nslots=2, kc_n=8, dma_sem="d4", dma_q="g",
                evac_eng="v", pre=(), post_evac=None):
    WT = c.sb["WT"]
    slot_user = [None] * nslots
    eng_prog = {"v": "pv", "s": "ps"}[evac_eng]
    for gi, pid in enumerate(piece_ids):
        sl = gi % nslots
        base = wt_off + sl * slot_w
        w = [("pt", slot_user[sl])] if slot_user[sl] else []
        dmad = c.dma(dma_q, WT[:, base:base + slot_w], dram_ap[pid],
                     waits=w + list(pre), sem=dma_sem)
        for ob in range(slot_w // (kc_n * 128)):
            for tt in range(NTT):
                ps = pool.bank()
                war = pool.war_wait()
                waits = [(dma_sem, dmad)]
                if war is not None:
                    waits.append(war)

                def mm(t, base=base, ob=ob, tt=tt, ps=ps):
                    l = None
                    for kc in range(kc_n):
                        l = t.matmul(ps, WT[:, base + kc * 512 + ob * 128:
                                            base + kc * 512 + ob * 128 + 128],
                                     rhs_fn(kc, tt), start=(kc == 0),
                                     stop=(kc == kc_n - 1))
                    return l
                mmd = c.on("t", mm, waits=waits, inc="pt", chain=False)
                slot_user[sl] = mmd

                def ev(e, gi=gi, ob=ob, tt=tt, ps=ps):
                    return evac_fn(gi, ob, tt, ps, e)
                val = c.on(evac_eng, ev, waits=[("pt", mmd)], inc=eng_prog)
                pool.push(val)
                if post_evac is not None:
                    post_evac(gi, ob, tt, val)
    return slot_user


def _mamba_conv(c, slot_ap, acc_ap, cw_base, cb_col, waits, src_ap=None):
    AUX = c.sb["AUX"]
    if src_ap is None:
        src_ap = slot_ap

    def conv(v):
        v.scalar_tensor_tensor(acc_ap, src_ap, AUX[:, cw_base + 3:cw_base + 4],
                               acc_ap, ALU.mult, ALU.bypass)
        v.scalar_tensor_tensor(acc_ap[:, 1:T], src_ap[:, 0:T - 1],
                               AUX[:, cw_base + 2:cw_base + 3],
                               acc_ap[:, 1:T], ALU.mult, ALU.add)
        v.scalar_tensor_tensor(acc_ap[:, 2:T], src_ap[:, 0:T - 2],
                               AUX[:, cw_base + 1:cw_base + 2],
                               acc_ap[:, 2:T], ALU.mult, ALU.add)
        return v.scalar_tensor_tensor(acc_ap[:, 3:T], src_ap[:, 0:T - 3],
                                      AUX[:, cw_base + 0:cw_base + 1],
                                      acc_ap[:, 3:T], ALU.mult, ALU.add)
    cd = c.on("v", conv, waits=list(waits))

    def silu(s):
        return s.activation(slot_ap, acc_ap, AF.Silu, bias=AUX[:, cb_col:cb_col + 1])
    return c.on("s", silu, waits=[("pv", cd)])


def _mamba_bphase(c, M, dc, xs_slot, xs_wait):
    PSA, PSB, VEC, TRASH, VB, AUX, WT = (c.sb["PSA"], c.sb["PSB"], c.sb["VEC"],
                                         c.sb["TRASH"], c.sb["VB"], c.sb["AUX"],
                                         c.sb["WT"])
    par = dc % 2
    dbuf, dxbuf = M["dbuf"][par], M["dxbuf"][par]
    st = M["state"]
    for th in range(2):
        def mm(t, th=th):
            l = None
            for i in range(2):
                l = t.matmul(PSB[:, i * 512:(i + 1) * 512],
                             M["wdt"][:, dc * 128:(dc + 1) * 128],
                             M["dt"][0:64, th * 1024 + i * 512:th * 1024 + (i + 1) * 512],
                             start=True, stop=True)
            return l
        w = [("pv", st["psb_free"])] if st["psb_free"] else []
        mmd = c.on("t", mm, waits=w, inc="pt", chain=False)
        s1 = dbuf[:, th * 1024:(th + 1) * 1024]
        s2 = TRASH[:, 0:1024]
        bcol = AUX[:, MA_BDT + dc:MA_BDT + dc + 1]
        nbcol = AUX[:, MA_NEGB + dc:MA_NEGB + dc + 1]

        def sp1(s, s1=s1, bcol=bcol, nbcol=nbcol):
            s.activation(s1, PSB[:, 0:1024], AF.Exp, bias=bcol)
            return s.activation(s2, PSB[:, 0:1024], AF.Exp, bias=nbcol, scale=-1.0)
        spw = [("pt", mmd)]
        if st["s2_free"]:
            spw.append(("pv", st["s2_free"]))
        if st["dbuf_free"][par]:
            spw.append(("pv", st["dbuf_free"][par]))
        v1 = c.on("s", sp1, waits=spw)

        def spmin(v, s1=s1):
            return v.tensor_tensor(s2, s2, s1, ALU.min)
        v2 = c.on("v", spmin, waits=[("ps", v1)])

        def sp2(s, s1=s1, bcol=bcol):
            s.activation(s2, s2, AF.Ln, bias=c.sb["ONECOLF"][:, 0:1])
            return s.activation(s1, PSB[:, 0:1024], AF.Relu, bias=bcol)
        v3 = c.on("s", sp2, waits=[("pv", v2)])

        def spadd(v, s1=s1):
            return v.tensor_tensor(s1, s1, s2, ALU.add)
        v4 = c.on("v", spadd, waits=[("ps", v3)])
        st["s2_free"] = v4
        st["psb_free"] = v4

    def dxm(v):
        return v.tensor_tensor(dxbuf, dbuf, xs_slot, ALU.mult)
    dxw = list(xs_wait)
    if st["dxbuf_free"][par]:
        dxw.append(("pt", st["dxbuf_free"][par]))
    dxd = c.on("v", dxm, waits=dxw)

    ps_dx = PSA[:, 0:1024]
    ps_d = PSA[:, 1024:2048]
    y_done_pv = None
    for th in range(2):
        for j in range(16):
            jp = j % 2
            bsb, hsb = M["bsb"][jp], M["hsb"][jp]
            asb = VEC[:, 0:1024] if jp == 0 else VEC[:, 1040:2064]
            rep = WT[:, j * 128:(j + 1) * 128]
            selw = WT[:, 2048 + 128 - j * 8:2048 + 256 - j * 8]

            def mdx(t, th=th, rep=rep):
                l = None
                for i in range(2):
                    l = t.matmul(ps_dx[:, i * 512:(i + 1) * 512], rep,
                                 dxbuf[:, th * 1024 + i * 512:th * 1024 + (i + 1) * 512],
                                 start=True, stop=True)
                return l
            w = [("pv", dxd if (th == 0 and j == 0) else st["b_prev"])]
            mdxd = c.on("t", mdx, waits=w, inc="pt", chain=False)

            def bmul(v, th=th, bsb=bsb):
                return v.tensor_tensor(bsb, ps_dx,
                                       M["btil"][:, th * 1024:(th + 1) * 1024],
                                       ALU.mult)
            bw = [("pt", mdxd)]
            if st["sel_prev2"]:
                bw.append(("pt", st["sel_prev2"]))
            st["b_prev"] = c.on("v", bmul, waits=bw)

            def md(t, th=th, rep=rep):
                l = None
                for i in range(2):
                    l = t.matmul(ps_d[:, i * 512:(i + 1) * 512], rep,
                                 dbuf[:, th * 1024 + i * 512:th * 1024 + (i + 1) * 512],
                                 start=True, stop=True)
                return l
            mw = [] if st["a_prev"] is None else [("ps", st["a_prev"])]
            mdd = c.on("t", md, waits=mw, inc="pt", chain=False)
            st["rep_d_last"] = mdd

            acol = AUX[:, MA_ALOG + dc * 16 + j:MA_ALOG + dc * 16 + j + 1]

            def aexp(s, asb=asb, acol=acol):
                return s.activation(asb, ps_d, AF.Exp, scale=acol)
            aw = [("pt", mdd)]
            if st["scan_prev2"]:
                aw.append(("pv", st["scan_prev2"]))
            st["a_prev"] = c.on("s", aexp, waits=aw)

            hcol = c.sb["PBUF"][:, j:j + 1]

            def scan(v, th=th, j=j, asb=asb, bsb=bsb, hsb=hsb, hcol=hcol):
                init = 0.0 if th == 0 else hcol
                v.tensor_tensor_scan(out=hsb, data0=asb, data1=bsb,
                                     initial=init, op0=ALU.mult, op1=ALU.add)
                if th == 0:
                    v.tensor_copy(hcol, hsb[:, 1023:1024])
                return v.tensor_tensor(bsb, hsb,
                                       M["ctil"][:, th * 1024:(th + 1) * 1024],
                                       ALU.mult)
            tmpd = c.on("v", scan, waits=[("ps", st["a_prev"])])
            st["scan_prev2"] = st["scan_prev1"]
            st["scan_prev1"] = tmpd

            def msel(t, th=th, j=j, selw=selw, bsb=bsb):
                l = None
                for i in range(2):
                    l = t.matmul(PSB[:, i * 512:(i + 1) * 512], selw,
                                 bsb[:, i * 512:(i + 1) * 512],
                                 start=(j == 0), stop=(j == 15))
                return l
            sw = [("pv", tmpd)]
            if th == 0 and j == 0 and st["psb_free"]:
                sw.append(("pv", st["psb_free"]))
            if th == 1 and j == 0:
                sw.append(("pv", st["yevac_prev"]))
            seld = c.on("t", msel, waits=sw, inc="pt", chain=False)
            st["sel_prev2"] = st["sel_prev1"]
            st["sel_prev1"] = seld

        ysl = xs_slot[:, th * 1024:(th + 1) * 1024]
        dcol = AUX[:, MA_D + dc:MA_D + dc + 1]

        def yev(v, th=th):
            return v.scalar_tensor_tensor(ysl, ysl, dcol, PSB[:, 0:1024],
                                          ALU.mult, ALU.add)
        yv = c.on("v", yev, waits=[("pt", st["sel_prev1"])])
        st["yevac_prev"] = yv
        y_done_pv = M["gate"](c, dc, th, ysl, yv)
        st["psb_free"] = y_done_pv
    st["dbuf_free"][par] = y_done_pv
    st["dxbuf_free"][par] = c.now("pt")
    return y_done_pv


def _mamba_bstate():
    return {"psb_free": None, "s2_free": None, "b_prev": None, "a_prev": None,
            "scan_prev1": None, "scan_prev2": None, "sel_prev1": None,
            "sel_prev2": None, "yevac_prev": None, "rep_d_last": None,
            "dbuf_free": [None, None], "dxbuf_free": [None, None]}


def emit_l1a(c, scale):
    XH, SB2, SB3, VEC, TRASH2, AUX, QS, WT = (
        c.sb["XH"], c.sb["SB2"], c.sb["SB3"], c.sb["VEC"], c.sb["TRASH2"],
        c.sb["AUX"], c.sb["QS"], c.sb["WT"])
    vo = c.vb_off

    def resxh(v):
        l = None
        for kc in range(DC):
            l = v.tensor_tensor(c.sb["RES"][:, kc * T:(kc + 1) * T],
                                c.sb["RES"][:, kc * T:(kc + 1) * T],
                                XH[:, kc * T:(kc + 1) * T], ALU.add)
        return l
    c.on("v", resxh)

    mamba_dir_load(c, c.dram["l1a_aux"])
    wd = c.dma("g", QS[0:64, 0:2048], c.dram["l1a_wdt"], sem="d4")

    emit_rms(c, lambda kc: AUX[:, MA_NW + kc:MA_NW + kc + 1], 0, "SB3")
    c.join()
    if MAMBA_TRUNC == 1:
        return

    painA = PsumPool(c, c.sb["PSA"], 4, 512)
    xpp = PsumPool(c, c.sb["PSB"], 2, 512)

    def xn_rhs(kc, tt):
        return SB3[:, kc * T + tt * TT:kc * T + (tt + 1) * TT]

    spill_xs = None
    res_stage_dma = [None] * 4
    res_ctr = [0]
    for h in range(2):
        def xc_ev(gi, ob, tt, ps, e, h=h):
            lc = gi * 4 + ob
            return e.tensor_copy(SB2[:, lc * T + tt * TT:lc * T + (tt + 1) * TT], ps)
        pre = ([("d3", spill_xs)] if spill_xs else []) + [("pt", c.now("pt"))]
        _stream_lin(c, c.dram["l1a_win"], [h * 2, h * 2 + 1], 0, xn_rhs,
                    xc_ev, painA, pre=pre)
        silu_done = []
        for lc in range(DC):
            dcg = h * 8 + lc
            cvw = [("pv", c.now("pv")), ("ps", c.now("ps"))]
            if c.now("d3") > 0:
                cvw.append(("d3", c.now("d3")))
            sd = _mamba_conv(c, SB2[:, lc * T:(lc + 1) * T], TRASH2,
                             MA_CW + dcg * 4, MA_CB + dcg, waits=cvw)
            silu_done.append(sd)
        xw = c.dma("g", c.sb["TRASH"][:, 0:768], c.dram["l1a_wxp"][h],
                   waits=[("pt", c.now("pt"))] if h else [], sem="d4")
        for tt in range(NTT):
            ps = xpp.bank()
            war = xpp.war_wait()
            w = [("d4", xw), ("ps", silu_done[-1])] + ([war] if war else [])

            def mm(t, tt=tt, ps=ps):
                l = None
                for kcl in range(DC):
                    l = t.matmul(ps[0:96, :],
                                 c.sb["TRASH"][:, kcl * 96:(kcl + 1) * 96],
                                 SB2[:, kcl * T + tt * TT:kcl * T + (tt + 1) * TT],
                                 start=(kcl == 0), stop=(kcl == DC - 1))
                return l
            mmd = c.on("t", mm, waits=w, inc="pt", chain=False)

            def ev(v, tt=tt, ps=ps, h=h):
                if h == 0:
                    v.tensor_copy(VEC[0:64, tt * TT:(tt + 1) * TT], ps[0:64, :])
                    return v.tensor_copy(VEC[64:96, tt * TT:(tt + 1) * TT],
                                         ps[64:96, :])
                v.tensor_tensor(VEC[0:64, tt * TT:(tt + 1) * TT],
                                VEC[0:64, tt * TT:(tt + 1) * TT], ps[0:64, :], ALU.add)
                return v.tensor_tensor(VEC[64:96, tt * TT:(tt + 1) * TT],
                                       VEC[64:96, tt * TT:(tt + 1) * TT],
                                       ps[64:96, :], ALU.add)
            val = c.on("v", ev, waits=[("pt", mmd)], inc="pv")
            xpp.push(val)
        def res_ev(gi, ob, tt, ps, e, h=h):
            i = res_ctr[0] % 4
            w = res_stage_dma[i]
            if w:
                e.wait_ge(c.sem["d3"], w)
            return e.tensor_copy(TRASH2[:, i * 512:(i + 1) * 512], ps)

        def res_post(gi, ob, tt, val, h=h):
            i = res_ctr[0] % 4
            cc = (h * 8 + gi * 4 + ob) * T + tt * TT
            res_stage_dma[i] = c.dma("y", c.dram["l1a_res"][:, cc:cc + TT],
                                     TRASH2[:, i * 512:(i + 1) * 512],
                                     waits=[("pv", val)], sem="d3")
            res_ctr[0] += 1
        _stream_lin(c, c.dram["l1a_wres"], [h * 2, h * 2 + 1], 0, xn_rhs,
                    res_ev, painA, post_evac=res_post,
                    pre=[("pt", c.now("pt"))])
        if h == 0:
            spill_xs = c.dma("y", c.dram["l1a_xsA"], SB2[:, :],
                             waits=[("ps", silu_done[-1])], sem="d3")
    c.join()
    dmabar(c, "d3", "d4")
    if MAMBA_TRUNC == 2:
        return

    rsl = c.dma("y", WT[:, 0:2304], c.dram["repsel"], sem="d7")

    def dtcp(v):
        return v.tensor_copy(SB3[0:64, 0:2048], VEC[0:64, 0:2048])
    c.on("v", dtcp)
    bco = c.dma("y", c.dram["l1a_bc"], VEC[64:96, 0:2048], sem="d6")
    BT, CT = SB3[:, 2048:4096], SB3[:, 4096:6144]
    for n in range(16):
        c.dma("g", BT[n * 8:(n + 1) * 8, :],
              c.dram["l1a_bc"][n:n + 1, :].broadcast_to((8, 2048)),
              waits=[("d6", bco)] if n == 0 else [], sem="d6")
        c.dma("g", CT[n * 8:(n + 1) * 8, :],
              c.dram["l1a_bc"][16 + n:17 + n, :].broadcast_to((8, 2048)),
              sem="d6")
    c.join()
    dmabar(c, "d6", "d7")
    if MAMBA_TRUNC == 3:
        return

    M = {"dt": SB3, "wdt": QS, "btil": BT, "ctil": CT,
         "dbuf": [TRASH2, SB3[:, 12288:14336]],
         "dxbuf": [SB3[:, 6144:8192], SB3[:, 14336:16384]],
         "bsb": [SB3[:, 8192:9216], SB3[:, 9216:10240]],
         "hsb": [SB3[:, 10240:11264], SB3[:, 11264:12288]],
         "state": _mamba_bstate()}

    def gate(c2, dc, th, ysl, yv):
        par = dc % 2
        gsl = M["dbuf"][par][:, th * 1024:(th + 1) * 1024]
        gd = c2.dma("y", gsl, c.dram["l1a_res"][:, dc * T + th * 1024:
                                               dc * T + (th + 1) * 1024],
                    waits=[("pt", M["state"]["rep_d_last"]), ("pv", yv)],
                    sem="d5")

        def sg(s):
            return s.activation(gsl, gsl, AF.Silu)
        sgd = c2.on("s", sg, waits=[("d5", gd)])

        def gm(v):
            return v.tensor_tensor(ysl, ysl, gsl, ALU.mult)
        return c2.on("v", gm, waits=[("ps", sgd)])
    M["gate"] = gate

    def y_rhs(kc, tt):
        return SB2[:, kc * T + tt * TT:kc * T + (tt + 1) * TT]

    woutp = PsumPool(c, c.sb["PSA"], 4, 512)
    for halfsel in (1, 0):
        xs_ld = [None] * DC
        if halfsel == 0:
            for lc in range(DC):
                xs_ld[lc] = c.dma("y", SB2[:, lc * T:(lc + 1) * T],
                                  c.dram["l1a_xsA"][:, lc * T:(lc + 1) * T],
                                  waits=[("pt", c.now("pt"))], sem="d5")
        for lc in range(DC):
            dc = halfsel * 8 + lc
            xw = [("d5", xs_ld[lc])] if xs_ld[lc] else []
            _mamba_bphase(c, M, dc, SB2[:, lc * T:(lc + 1) * T], xw)
        c.join()
        if MAMBA_TRUNC == 4 and halfsel == 1:
            return

        def wo_ev(gi, ob, tt, ps, e, halfsel=halfsel):
            oc = gi * 4 + ob
            r = res_slice(c, oc, tt)
            return e.tensor_tensor(r, r, ps, ALU.add)
        _stream_lin(c, c.dram["l1a_wout"], [halfsel * 2, halfsel * 2 + 1],
                    2560, y_rhs, wo_ev, woutp, nslots=1)
        c.join()


def emit_l3a(c, scale):
    XH, SB2, SB3, VEC, TRASH2, AUX, QS, WT = (
        c.sb["XH"], c.sb["SB2"], c.sb["SB3"], c.sb["VEC"], c.sb["TRASH2"],
        c.sb["AUX"], c.sb["QS"], c.sb["WT"])

    def resxh(v):
        l = None
        for kc in range(DC):
            l = v.tensor_tensor(c.sb["RES"][:, kc * T:(kc + 1) * T],
                                c.sb["RES"][:, kc * T:(kc + 1) * T],
                                XH[:, kc * T:(kc + 1) * T], ALU.add)
        return l
    c.on("v", resxh)

    mamba_dir_load(c, c.dram["l3a_aux_f"])
    emit_rms(c, lambda kc: AUX[:, MA_NW + kc:MA_NW + kc + 1], 0, "SB3")
    c.join()

    painA = PsumPool(c, c.sb["PSA"], 4, 512)
    xpp = PsumPool(c, c.sb["PSB"], 2, 512)

    def xn_rhs(kc, tt):
        return SB3[:, kc * T + tt * TT:kc * T + (tt + 1) * TT]

    spill_xi = None
    z_stage_dma = [None] * 4
    z_ctr = [0]
    for h in range(2):
        def xi_ev(gi, ob, tt, ps, e):
            lc = gi * 4 + ob
            return e.tensor_copy(SB2[:, lc * T + tt * TT:lc * T + (tt + 1) * TT], ps)
        pre = ([("d3", spill_xi)] if spill_xi else []) + [("pt", c.now("pt"))]
        _stream_lin(c, c.dram["l3a_wipx"], [h * 2, h * 2 + 1], 0, xn_rhs,
                    xi_ev, painA, pre=pre)
        spill_xi = c.dma("y", c.dram["l3a_xi"][:, h * 16384:(h + 1) * 16384],
                         SB2[:, :], waits=[("pv", c.now("pv"))], sem="d3")

        def z_ev(gi, ob, tt, ps, e):
            i = z_ctr[0] % 4
            w = z_stage_dma[i]
            if w:
                e.wait_ge(c.sem["d3"], w)
            return e.tensor_copy(TRASH2[:, i * 512:(i + 1) * 512], ps)

        def z_post(gi, ob, tt, val, h=h):
            i = z_ctr[0] % 4
            cc = (h * 8 + gi * 4 + ob) * T + tt * TT
            z_stage_dma[i] = c.dma("y", c.dram["l3a_z"][:, cc:cc + TT],
                                   TRASH2[:, i * 512:(i + 1) * 512],
                                   waits=[("pv", val)], sem="d3")
            z_ctr[0] += 1
        _stream_lin(c, c.dram["l3a_wipz"], [h * 2, h * 2 + 1], 0, xn_rhs,
                    z_ev, painA, post_evac=z_post, pre=[("pt", c.now("pt"))])
    c.join()
    dmabar(c, "d3", "d4")
    rsl = c.dma("y", WT[:, 0:2304], c.dram["repsel"], sem="d7")

    for dr in ("b", "f"):
        mamba_dir_load(c, c.dram[f"l3a_aux_{dr}"])
        wd = c.dma("g", QS[0:64, 0:2048], c.dram[f"l3a_wdt_{dr}"], sem="d4")
        XTMP = SB3[:, 0:2048]
        prev_rcv = None
        for h in range(2):
            silu_done = []
            for lc in range(DC):
                dc = h * 8 + lc
                slot = SB2[:, lc * T:(lc + 1) * T]
                xiw = [("d3", c.now("d3")), ("pt", c.now("pt")),
                       ("pv", c.now("pv"))]
                if dr == "b":
                    xl = c.dma("y", XTMP, c.dram["l3a_xi"][:, dc * T:(dc + 1) * T],
                               waits=(xiw if prev_rcv is None else
                                      [("pv", prev_rcv)] + xiw[:1]), sem="d5")

                    def rcv(v, slot=slot):
                        return v.tensor_copy(slot, XTMP[:, ::-1])
                    prev_rcv = c.on("v", rcv, waits=[("d5", xl)])
                else:
                    xl = c.dma("y", slot, c.dram["l3a_xi"][:, dc * T:(dc + 1) * T],
                               waits=xiw, sem="d5")
                cvw = [("pv", c.now("pv")), ("ps", c.now("ps"))]
                if dr == "f":
                    cvw.append(("d5", xl))
                if c.now("d3") > 0:
                    cvw.append(("d3", c.now("d3")))
                sd = _mamba_conv(c, slot, TRASH2, MA_CW + dc * 4, MA_CB + dc,
                                 waits=cvw)
                silu_done.append(sd)
            xw = c.dma("g", c.sb["TRASH"][:, 0:768], c.dram[f"l3a_wxp_{dr}"][h],
                       waits=[("pt", c.now("pt"))], sem="d4")
            for tt in range(NTT):
                ps = xpp.bank()
                war = xpp.war_wait()
                w = [("d4", xw), ("ps", silu_done[-1])] + ([war] if war else [])

                def mm(t, tt=tt, ps=ps):
                    l = None
                    for kcl in range(DC):
                        l = t.matmul(ps[0:96, :],
                                     c.sb["TRASH"][:, kcl * 96:(kcl + 1) * 96],
                                     SB2[:, kcl * T + tt * TT:kcl * T + (tt + 1) * TT],
                                     start=(kcl == 0), stop=(kcl == DC - 1))
                    return l
                mmd = c.on("t", mm, waits=w, inc="pt", chain=False)

                def ev(v, tt=tt, ps=ps, h=h):
                    if h == 0:
                        v.tensor_copy(VEC[0:64, tt * TT:(tt + 1) * TT], ps[0:64, :])
                        return v.tensor_copy(VEC[64:96, tt * TT:(tt + 1) * TT],
                                             ps[64:96, :])
                    v.tensor_tensor(VEC[0:64, tt * TT:(tt + 1) * TT],
                                    VEC[0:64, tt * TT:(tt + 1) * TT], ps[0:64, :],
                                    ALU.add)
                    return v.tensor_tensor(VEC[64:96, tt * TT:(tt + 1) * TT],
                                           VEC[64:96, tt * TT:(tt + 1) * TT],
                                           ps[64:96, :], ALU.add)
                val = c.on("v", ev, waits=[("pt", mmd)], inc="pv")
                xpp.push(val)
            if h == 0:
                c.dma("y", c.dram["l1a_xsA"], SB2[:, :],
                      waits=[("ps", silu_done[-1])], sem="d3")
        c.join()
        dmabar(c, "d3", "d4", "d5", "d7")

        def dtcp(v):
            return v.tensor_copy(SB3[0:64, 0:2048], VEC[0:64, 0:2048])
        c.on("v", dtcp)
        bco = c.dma("y", c.dram["l1a_bc"], VEC[64:96, 0:2048], sem="d6")
        BT, CT = SB3[:, 2048:4096], SB3[:, 4096:6144]
        for n in range(16):
            c.dma("g", BT[n * 8:(n + 1) * 8, :],
                  c.dram["l1a_bc"][n:n + 1, :].broadcast_to((8, 2048)),
                  waits=[("d6", bco)] if n == 0 else [], sem="d6")
            c.dma("g", CT[n * 8:(n + 1) * 8, :],
                  c.dram["l1a_bc"][16 + n:17 + n, :].broadcast_to((8, 2048)),
                  sem="d6")
        c.join()
        dmabar(c, "d6")

        M = {"dt": SB3, "wdt": QS, "btil": BT, "ctil": CT,
             "dbuf": [TRASH2, SB3[:, 12288:14336]],
             "dxbuf": [SB3[:, 6144:8192], SB3[:, 14336:16384]],
             "bsb": [SB3[:, 8192:9216], SB3[:, 9216:10240]],
             "hsb": [SB3[:, 10240:11264], SB3[:, 11264:12288]],
             "state": _mamba_bstate()}

        def gate(c2, dc, th, ysl, yv, dr=dr):
            par = dc % 2
            gsl = M["dbuf"][par][:, th * 1024:(th + 1) * 1024]
            sth = (1 - th) if dr == "b" else th
            gd = c2.dma("y", gsl,
                        c.dram["l3a_z"][:, dc * T + sth * 1024:dc * T + (sth + 1) * 1024],
                        waits=[("pt", M["state"]["rep_d_last"]), ("pv", yv)],
                        sem="d5")

            def sg(s, gsl=gsl):
                return s.activation(gsl, gsl, AF.Silu)
            sgd = c2.on("s", sg, waits=[("d5", gd)])

            def gm(v, gsl=gsl, dr=dr):
                src = gsl[:, ::-1] if dr == "b" else gsl
                return v.tensor_tensor(ysl, ysl, src, ALU.mult)
            return c2.on("v", gm, waits=[("ps", sgd)])
        M["gate"] = gate

        woutp = PsumPool(c, c.sb["PSA"], 4, 512)
        for halfsel in (1, 0):
            xs_ld = [None] * DC
            if halfsel == 0:
                for lc in range(DC):
                    xs_ld[lc] = c.dma("y", SB2[:, lc * T:(lc + 1) * T],
                                      c.dram["l1a_xsA"][:, lc * T:(lc + 1) * T],
                                      waits=[("pt", c.now("pt")),
                                             ("d3", c.now("d3"))], sem="d5")
            for lc in range(DC):
                dc = halfsel * 8 + lc
                xw = [("d5", xs_ld[lc])] if xs_ld[lc] else []
                yd = _mamba_bphase(c, M, dc, SB2[:, lc * T:(lc + 1) * T], xw)
                if dr == "b":
                    c.dma("y", c.dram["l3a_yb"][:, dc * T:(dc + 1) * T],
                          SB2[:, lc * T:(lc + 1) * T], waits=[("pv", yd)],
                          sem="d3")
            c.join()
            if dr == "f":
                for lc in range(DC):
                    dc = halfsel * 8 + lc
                    par = dc % 2
                    stg = M["dbuf"][par]
                    yl = c.dma("y", stg, c.dram["l3a_yb"][:, dc * T:(dc + 1) * T],
                               waits=[("pv", c.now("pv"))], sem="d5")

                    def comb(v, lc=lc, stg=stg):
                        slot = SB2[:, lc * T:(lc + 1) * T]
                        v.tensor_tensor(slot, slot, stg[:, ::-1], ALU.add)
                        return v.tensor_scalar(slot, slot, 0.5, None, ALU.mult)
                    c.on("v", comb, waits=[("d5", yl)])
                c.join()

                def wo_ev(gi, ob, tt, ps, e):
                    oc = gi * 4 + ob
                    r = res_slice(c, oc, tt)
                    return e.tensor_tensor(r, r, ps, ALU.add)

                def y_rhs(kc, tt):
                    return SB2[:, kc * T + tt * TT:kc * T + (tt + 1) * TT]
                _stream_lin(c, c.dram["l3a_wop"], [halfsel * 2, halfsel * 2 + 1],
                            2560, y_rhs, wo_ev, woutp, nslots=1)
            c.join()
        dmabar(c, "d3", "d5")


L2EPS = 1e-6
LA_CW, LA_CB, LA_NIN, LA_AB, LA_NAB, LA_BB = 0, 32, 40, 48, 56, 64
LB_U1B, LB_U2B, LB_FFW = 0, 1, 9


def emit_l2apre(c, scale):
    XH, SB2, SB3, AUX, TRASH, TRASH2, WT, VEC = (
        c.sb["XH"], c.sb["SB2"], c.sb["SB3"], c.sb["AUX"], c.sb["TRASH"],
        c.sb["TRASH2"], c.sb["WT"], c.sb["VEC"])
    ax = c.dma("y", AUX[:, 0:72], c.dram["l2a_aux1"], sem="d7")

    def negab(v):
        return v.tensor_scalar(AUX[:, LA_NAB:LA_NAB + 8],
                               AUX[:, LA_AB:LA_AB + 8], -1.0, None, ALU.mult)
    c.on("v", negab, waits=[("d7", ax)])
    emit_rms(c, lambda kc: AUX[:, LA_NIN + kc:LA_NIN + kc + 1], 0, "SB3",
             eps=L2EPS)
    c.join()
    for lc in range(DC):
        _mamba_conv(c, SB2[:, lc * T:(lc + 1) * T], TRASH2,
                    LA_CW + lc * 4, LA_CB + lc,
                    waits=[("pv", c.now("pv")), ("ps", c.now("ps"))],
                    src_ap=SB3[:, lc * T:(lc + 1) * T])
    c.join()

    def h_rhs(kc, tt):
        return SB2[:, kc * T + tt * TT:kc * T + (tt + 1) * TT]
    pp = PsumPool(c, c.sb["PSA"], 4, 512)
    st_ctr = [0]
    st_dma = [None] * 4
    for wn, outn in (("l2a_qw", "l2a_q"), ("l2a_kw", "l2a_k"),
                     ("l2a_vw", "l2a_v")):
        for piece in range(2):
            pw = [("pt", c.now("pt"))]
            dmad = c.dma("g", WT[:, piece * 4096:(piece + 1) * 4096],
                         c.dram[wn][:, piece * 4096:(piece + 1) * 4096],
                         waits=pw, sem="d4")
            for fcl in range(4):
                fc = piece * 4 + fcl
                for tt in range(NTT):
                    ps = pp.bank()
                    war = pp.war_wait()
                    w = [("d4", dmad)] + ([war] if war else [])

                    def mm(t, piece=piece, fcl=fcl, tt=tt, ps=ps):
                        l = None
                        for kc in range(DC):
                            l = t.matmul(ps, WT[:, piece * 4096 + fcl * 1024 + kc * 128:
                                                piece * 4096 + fcl * 1024 + kc * 128 + 128],
                                         h_rhs(kc, tt), start=(kc == 0),
                                         stop=(kc == DC - 1))
                        return l
                    mmd = c.on("t", mm, waits=w, inc="pt", chain=False)
                    i = st_ctr[0] % 4
                    stg = TRASH2[:, i * 512:(i + 1) * 512]
                    sw = [("pt", mmd)]
                    if st_dma[i]:
                        sw.append(("d2", st_dma[i]))

                    def ev(v, stg=stg, ps=ps):
                        return v.tensor_copy(stg, ps)
                    evd = c.on("v", ev, waits=sw)
                    c.dma("y", c.dram[outn][:, (2 * fc) * T + tt * TT:
                                            (2 * fc) * T + (tt + 1) * TT],
                          stg[0:64, :], waits=[("pv", evd)], sem="d2")
                    st_dma[i] = c.dma("y", c.dram[outn][:, (2 * fc + 1) * T + tt * TT:
                                                        (2 * fc + 1) * T + (tt + 1) * TT],
                                      stg[64:128, :], sem="d2")
                    pp.push(evd)
                    st_ctr[0] += 1
    for piece in range(2):
        pw = [("pt", c.now("pt"))]
        dmad = c.dma("g", WT[:, piece * 4096:(piece + 1) * 4096],
                     c.dram["l2a_aw"][:, piece * 4096:(piece + 1) * 4096],
                     waits=pw, sem="d4")
        for fcl in range(4):
            fc = piece * 4 + fcl
            for tt in range(NTT):
                ps = pp.bank()
                war = pp.war_wait()
                w = [("d4", dmad)] + ([war] if war else [])

                def mm(t, piece=piece, fcl=fcl, tt=tt, ps=ps):
                    l = None
                    for kc in range(DC):
                        l = t.matmul(ps, WT[:, piece * 4096 + fcl * 1024 + kc * 128:
                                            piece * 4096 + fcl * 1024 + kc * 128 + 128],
                                     h_rhs(kc, tt), start=(kc == 0),
                                     stop=(kc == DC - 1))
                    return l
                mmd = c.on("t", mm, waits=w, inc="pt", chain=False)
                i = st_ctr[0] % 4
                stg = TRASH2[:, i * 512:(i + 1) * 512]
                s2 = TRASH[:, 0:512]
                abc = AUX[:, LA_AB + fc:LA_AB + fc + 1]
                nbc = AUX[:, LA_NAB + fc:LA_NAB + fc + 1]

                def sp1(s, stg=stg, abc=abc, nbc=nbc, ps=ps):
                    s.activation(stg, ps, AF.Exp, bias=abc)
                    return s.activation(s2, ps, AF.Exp, bias=nbc, scale=-1.0)
                sw = [("pt", mmd), ("ps", c.now("ps")), ("pv", c.now("pv"))]
                if st_dma[i]:
                    sw.append(("d2", st_dma[i]))
                v1 = c.on("s", sp1, waits=sw)

                def spv(v, stg=stg, ps=ps, nbc=nbc):
                    return v.tensor_tensor(s2, s2, stg, ALU.min)
                v2 = c.on("v", spv, waits=[("ps", v1)])

                def sp2(s, stg=stg, ps=ps, nbc=nbc):
                    s.activation(s2, s2, AF.Ln, bias=c.sb["ONECOLF"][:, 0:1])
                    return s.activation(stg, ps, AF.Relu, bias=nbc, scale=-1.0)
                v3 = c.on("s", sp2, waits=[("pv", v2)])

                def sp3(v, stg=stg):
                    return v.tensor_tensor(stg, stg, s2, ALU.add)
                v4 = c.on("v", sp3, waits=[("ps", v3)])
                st_dma[i] = c.dma("y", c.dram["l2a_sp"][:, (2 * fc) * T + tt * TT:
                                                         (2 * fc) * T + (tt + 1) * TT],
                                  stg[0:64, :], waits=[("pv", v4)], sem="d2")
                st_dma[i] = c.dma("y", c.dram["l2a_sp"][:, (2 * fc + 1) * T + tt * TT:
                                                        (2 * fc + 1) * T + (tt + 1) * TT],
                                  stg[64:128, :], sem="d2")
                pp.push(("d2", st_dma[i]))
                st_ctr[0] += 1
    bw = c.dma("g", TRASH[:, 512:640], c.dram["l2a_bw"], sem="d4")
    bev = None
    for tt in range(NTT):
        def mm(t, tt=tt):
            l = None
            for kc in range(DC):
                l = t.matmul(c.sb["PSB"][0:16, 0:512],
                             TRASH[:, 512 + kc * 16:512 + (kc + 1) * 16],
                             h_rhs(kc, tt), start=(kc == 0), stop=(kc == DC - 1))
            return l
        w = [("d4", bw)]
        if bev:
            w.append(("pv", bev))
        mmd = c.on("t", mm, waits=w, inc="pt", chain=False)

        def bsig(s, tt=tt):
            return s.activation(VEC[0:16, tt * TT:(tt + 1) * TT],
                                c.sb["PSB"][0:16, 0:512], AF.Sigmoid,
                                bias=AUX[0:16, LA_BB:LA_BB + 1])
        bsd = c.on("s", bsig, waits=[("pt", mmd)])

        def bcp(v, tt=tt):
            return v.tensor_copy(VEC[0:16, tt * TT:(tt + 1) * TT],
                                 VEC[0:16, tt * TT:(tt + 1) * TT])
        bev = c.on("v", bcp, waits=[("ps", bsd)])
    c.dma("y", c.dram["l2a_beta"], VEC[0:16, 0:2048], waits=[("pv", bev)],
          sem="d3")
    c.join()
    dmabar(c, "d2", "d3", "d4")


def build_l2a_seg():
    nc = bass.Bass("TRN2", target_bir_lowering=False,
                   detect_race_conditions=DETECT_RACES)
    dram = {}
    for nm in ("sg_q", "sg_k", "sg_v", "sg_sp"):
        dram[nm] = nc.dram_tensor(nm, [64, 2048], BF16, kind="ExternalInput").ap()
    dram["sg_beta"] = nc.dram_tensor("sg_beta", [128, 16], F32, kind="ExternalInput").ap()
    dram["sg_sin"] = nc.dram_tensor("sg_sin", [64, 1024], F32, kind="ExternalInput").ap()
    dram["sg_hnw"] = nc.dram_tensor("sg_hnw", [64, 16], F32, kind="ExternalInput").ap()
    dram["sg_masks"] = nc.dram_tensor("sg_masks", [128, 256], BF16, kind="ExternalInput").ap()
    dram["sg_ib"] = nc.dram_tensor("sg_ib", [64, 64], BF16, kind="ExternalInput").ap()
    dram["sg_ib2"] = nc.dram_tensor("sg_ib2", [128, 128], BF16, kind="ExternalInput").ap()
    dram["sg_sout"] = nc.dram_tensor("sg_sout", [64, 1024], F32, kind="ExternalOutput").ap()
    for p_ in (0, 1):
        dram[f"sg_dx{p_}"] = nc.dram_tensor(f"sg_dx{p_}", [128, 128], BF16, kind="Internal").ap()
        dram[f"sg_dr{p_}"] = nc.dram_tensor(f"sg_dr{p_}", [128, 64], BF16, kind="Internal").ap()
        dram[f"sg_dm{p_}"] = nc.dram_tensor(f"sg_dm{p_}", [128, 64], BF16, kind="Internal").ap()
        dram[f"sg_do{p_}"] = nc.dram_tensor(f"sg_do{p_}", [16, 512], F32, kind="Internal").ap()
    dram["sg_o"] = nc.dram_tensor("sg_o", [128, 1024], BF16, kind="ExternalOutput").ap()
    stack = ExitStack()
    with stack:
        sb = {}

        def alloc(name, shape, dt):
            sb[name] = stack.enter_context(nc.sbuf_tensor(name, shape, dt)).ap()
        for nm in ("Qa", "Ka", "Va", "SPa"):
            alloc(nm, [64, 2048], BF16)
        alloc("BETA", [128, 16], F32)
        alloc("SST", [64, 1024], F32)
        alloc("HNW", [64, 16], F32)
        alloc("MASKS", [128, 256], BF16)
        alloc("IB", [64, 64], BF16)
        alloc("IB2", [128, 128], BF16)
        alloc("CSA0", [64, 128], F32)
        alloc("CSA1", [64, 128], F32)
        alloc("CSB0", [64, 128], F32)
        alloc("CSB1", [64, 128], F32)
        alloc("EP0", [64, 128], BF16)
        alloc("EP1", [64, 128], BF16)
        alloc("EM0", [64, 128], BF16)
        alloc("EM1", [64, 128], BF16)
        alloc("EMC0", [64, 8], F32)
        alloc("EMC1", [64, 8], F32)
        alloc("ERC0", [64, 8], F32)
        alloc("ERC1", [64, 8], F32)
        alloc("KK0", [64, 128], BF16)
        alloc("KK1", [64, 128], BF16)
        alloc("RT0", [64, 128], BF16)
        alloc("RT1", [64, 128], BF16)
        alloc("BCOL0", [128, 1], F32)
        alloc("BCOL1", [128, 1], F32)
        alloc("NBCOL0", [128, 1], F32)
        alloc("NBCOL1", [128, 1], F32)
        alloc("Asb0", [128, 128], BF16)
        alloc("Asb1", [128, 128], BF16)
        alloc("ATsb0", [128, 128], BF16)
        alloc("ATsb1", [128, 128], BF16)
        alloc("A2sb0", [128, 128], BF16)
        alloc("A2sb1", [128, 128], BF16)
        alloc("A2Tsb0", [128, 128], BF16)
        alloc("A2Tsb1", [128, 128], BF16)
        alloc("A4sb0", [128, 128], BF16)
        alloc("A4sb1", [128, 128], BF16)
        alloc("A4Tsb0", [128, 128], BF16)
        alloc("A4Tsb1", [128, 128], BF16)
        alloc("A8Tsb0", [128, 128], BF16)
        alloc("A8Tsb1", [128, 128], BF16)
        alloc("MqTsb0", [128, 128], BF16)
        alloc("MqTsb1", [128, 128], BF16)
        alloc("Xsb0", [128, 128], BF16)
        alloc("Xsb1", [128, 128], BF16)
        alloc("MQVsb0", [128, 64], BF16)
        alloc("MQVsb1", [128, 64], BF16)
        alloc("MQYsb0", [128, 64], BF16)
        alloc("MQYsb1", [128, 64], BF16)
        alloc("QCsb0", [64, 128], BF16)
        alloc("QCsb1", [64, 128], BF16)
        alloc("Xfl0", [16, 1024], BF16)
        alloc("Xfl1", [16, 1024], BF16)
        alloc("RTP0", [128, 64], BF16)
        alloc("RTP1", [128, 64], BF16)
        alloc("RFL0", [16, 512], BF16)
        alloc("RFL1", [16, 512], BF16)
        alloc("MQVfl0", [16, 512], BF16)
        alloc("MQVfl1", [16, 512], BF16)
        alloc("OFL0", [16, 512], F32)
        alloc("OFL1", [16, 512], F32)
        alloc("ASTsb0", [64, 64], BF16)
        alloc("ASTsb1", [64, 64], BF16)
        alloc("PCsb0", [64, 64], BF16)
        alloc("PCsb1", [64, 64], BF16)
        alloc("S0sb0", [64, 64], BF16)
        alloc("S0sb1", [64, 64], BF16)
        alloc("OTP0", [128, 64], BF16)
        alloc("OTP1", [128, 64], BF16)
        alloc("SQ0", [128, 64], F32)
        alloc("SQ1", [128, 64], F32)
        alloc("RS0", [128, 1], F32)
        alloc("RS1", [128, 1], F32)
        alloc("OPAIR", [128, 1024], BF16)
        alloc("ONEF", [128, 1], F32)
        psl = {}
        for p_ in (0, 1):
            psl[f"p1{p_}"] = stack.enter_context(nc.psum_tensor(f"P1{p_}", [128, 512], F32)).ap()
            psl[f"p2{p_}"] = stack.enter_context(nc.psum_tensor(f"P2{p_}", [128, 512], F32)).ap()
            psl[f"p3{p_}"] = stack.enter_context(nc.psum_tensor(f"P3{p_}", [128, 512], F32)).ap()
            psl[f"pt{p_}"] = stack.enter_context(nc.psum_tensor(f"PT{p_}", [128, 1024], BF16)).ap()

        with nc.Block() as block:
            c = Ctx(nc, block, stack)
            c.dram = dram
            c.sb = sb
            Q, K, V128, SP = sb["Qa"], sb["Ka"], sb["Va"], sb["SPa"]

            def ginit(g):
                return g.memset(sb["ONEF"][:, :], 1.0)
            gi = c.on("g", ginit, inc="pg")
            ld = 0
            for nm, dst in (("sg_q", Q), ("sg_k", K), ("sg_v", V128),
                            ("sg_sp", SP)):
                ld = c.dma("y", dst, dram[nm], sem="di")
            ld = c.dma("y", sb["BETA"], dram["sg_beta"], sem="di")
            ld = c.dma("y", sb["SST"], dram["sg_sin"], sem="di")
            ld = c.dma("y", sb["HNW"], dram["sg_hnw"], sem="di")
            ld = c.dma("y", sb["MASKS"], dram["sg_masks"], sem="di")
            ld = c.dma("y", sb["IB"], dram["sg_ib"], sem="di")
            ld = c.dma("y", sb["IB2"], dram["sg_ib2"], sem="di")
            for key in ("t", "v", "s", "g"):
                def wall(e):
                    e.wait_ge(c.sem["di"], ld)
                    e.wait_ge(c.sem["pg"], gi)
                c._sec(key)(wall)

            MSTR = sb["MASKS"][:, 0:128]
            MINCT = sb["MASKS"][:, 128:256]
            tile_done = {0: None, 1: None}
            last_odo = {0: 0, 1: 0}
            for h in range(16):
                par = h % 2
                sbp = {nm: sb[f"{nm}{par}"] for nm in (
                    "CSA", "CSB", "EP", "EM", "EMC", "ERC", "KK", "RT",
                    "BCOL", "NBCOL", "Asb", "ATsb", "A2sb", "A2Tsb", "A4sb",
                    "A4Tsb", "A8Tsb", "MqTsb", "Xsb", "MQVsb", "MQYsb",
                    "QCsb", "Xfl", "RTP", "RFL", "MQVfl", "OFL", "ASTsb",
                    "PCsb", "S0sb", "OTP", "SQ", "RS")}
                p1, p2, p3, pt = (psl[f"p1{par}"], psl[f"p2{par}"],
                                  psl[f"p3{par}"], psl[f"pt{par}"])
                td = tile_done[par] or {}
                tw = [(k_, v_) for k_, v_ in td.items()]
                q = Q[:, h * 128:(h + 1) * 128]
                k = K[:, h * 128:(h + 1) * 128]
                vv_ = V128[:, h * 128:(h + 1) * 128]
                sp = SP[:, h * 128:(h + 1) * 128]
                CSA, CSB = sbp["CSA"], sbp["CSB"]
                EP, EM, EMC, ERC = sbp["EP"], sbp["EM"], sbp["EMC"], sbp["ERC"]
                KK, RT = sbp["KK"], sbp["RT"]

                def cums(v, sp=sp):
                    v.tensor_copy(CSA, sp)
                    src, dst = CSA, CSB
                    for lvl in (1, 2, 4, 8):
                        s3 = src.rearrange("p (c t) -> p c t", c=8)
                        d3 = dst.rearrange("p (c t) -> p c t", c=8)
                        v.tensor_tensor(d3[:, :, lvl:16], s3[:, :, lvl:16],
                                        s3[:, :, 0:16 - lvl], ALU.add)
                        v.tensor_copy(d3[:, :, 0:lvl], s3[:, :, 0:lvl])
                        src, dst = dst, src
                    c3 = src.rearrange("p (c t) -> p c t", c=8)
                    v.tensor_tensor(dst.rearrange("p (c t) -> p c t", c=8),
                                    c3[:, :, 7:8].broadcast_to((64, 8, 16)),
                                    c3, ALU.subtract)
                    return v.tensor_tensor(ERC, src[:, 7::16],
                                           src[:, 15::16], ALU.subtract)
                cu = c.on("v", cums, waits=tw)

                def exps(s):
                    s.activation(EP, CSB, AF.Exp)
                    s.activation(EM, CSB, AF.Exp, scale=-1.0)
                    s.activation(EMC, CSA[:, 7::16], AF.Exp, scale=-1.0)
                    return s.activation(ERC, ERC, AF.Exp)
                ex = c.on("s", exps, waits=[("pv", cu)])

                def scal(v, q=q, k=k, h=h):
                    v.tensor_tensor(KK, k, EP, ALU.mult)
                    v.tensor_tensor(RT, k, EM, ALU.mult)
                    v.tensor_tensor(q, q, EP, ALU.mult)
                    v.tensor_copy(sbp["BCOL"], sb["BETA"][:, h:h + 1])
                    return v.tensor_scalar(sbp["NBCOL"], sbp["BCOL"], -1.0, None,
                                           ALU.mult)
                sc = c.on("v", scal, waits=[("ps", ex)])

                def mms(t, q=q):
                    t.matmul(p1[:, 0:128], KK, RT, start=True, stop=True)
                    return t.matmul(p2[:, 0:128], RT, q, start=True, stop=True)
                md = c.on("t", mms, waits=[("pv", sc)], inc="pt", chain=False)

                def evA(v):
                    v.scalar_tensor_tensor(sbp["Asb"], p1[:, 0:128], sbp["NBCOL"],
                                           MSTR, ALU.mult, ALU.mult)
                    return v.tensor_tensor(sbp["MqTsb"], p2[:, 0:128], MINCT,
                                           ALU.mult)
                ea = c.on("v", evA, waits=[("pt", md)])

                def trs(t, vv_=vv_):
                    t.transpose(pt[:, 0:128], sbp["Asb"], sb["IB2"])
                    t.transpose(pt[:, 256:320], vv_, sb["IB"])
                    t.transpose(pt[:, 896:960], RT, sb["IB"])
                    return t.transpose(pt[:, 384:448], KK, sb["IB"])
                tr = c.on("t", trs, waits=[("pv", ea)], inc="pt", chain=False)

                def evT(v):
                    v.tensor_copy(sbp["ATsb"], pt[:, 0:128])
                    v.tensor_copy(sbp["RTP"], pt[:, 896:960])
                    v.scalar_tensor_tensor(sbp["Xsb"][:, 0:64], pt[:, 256:320],
                                           sbp["BCOL"], sbp["Xsb"][:, 0:64],
                                           ALU.mult, ALU.bypass)
                    return v.scalar_tensor_tensor(sbp["Xsb"][:, 64:128],
                                                  pt[:, 384:448], sbp["BCOL"],
                                                  sbp["Xsb"][:, 64:128],
                                                  ALU.mult, ALU.bypass)
                et = c.on("v", evT, waits=[("pt", tr)])

                def mp1(t):
                    t.matmul(p1[:, 0:128], sbp["ATsb"], sbp["Asb"], start=True, stop=True)
                    return t.matmul(p2[:, 0:128], sbp["Asb"], sbp["ATsb"], start=True, stop=True)
                m1 = c.on("t", mp1, waits=[("pv", et)], inc="pt", chain=False)

                def ev1(v):
                    v.tensor_copy(sbp["A2sb"], p1[:, 0:128])
                    return v.tensor_copy(sbp["A2Tsb"], p2[:, 0:128])
                e1 = c.on("v", ev1, waits=[("pt", m1)])

                def mp2(t):
                    t.matmul(p1[:, 0:128], sbp["A2Tsb"], sbp["A2sb"], start=True, stop=True)
                    return t.matmul(p2[:, 0:128], sbp["A2sb"], sbp["A2Tsb"], start=True, stop=True)
                m2 = c.on("t", mp2, waits=[("pv", e1)], inc="pt", chain=False)

                def ev2(v):
                    v.tensor_copy(sbp["A4sb"], p1[:, 0:128])
                    return v.tensor_copy(sbp["A4Tsb"], p2[:, 0:128])
                e2 = c.on("v", ev2, waits=[("pt", m2)])

                def mp3(t):
                    return t.matmul(p1[:, 0:128], sbp["A4sb"], sbp["A4Tsb"],
                                    start=True, stop=True)
                m3 = c.on("t", mp3, waits=[("pv", e2)], inc="pt", chain=False)

                def ev3(v):
                    return v.tensor_copy(sbp["A8Tsb"], p1[:, 0:128])
                e3 = c.on("v", ev3, waits=[("pt", m3)])

                prevv = e3
                for pw_ in ("ATsb", "A2Tsb", "A4Tsb", "A8Tsb"):
                    def app_mm(t, pw_=pw_):
                        return t.matmul(p2[:, 0:128], sbp[pw_], sbp["Xsb"],
                                        start=True, stop=True)
                    am = c.on("t", app_mm, waits=[("pv", prevv)], inc="pt",
                              chain=False)

                    def app_add(v):
                        return v.tensor_tensor(sbp["Xsb"], sbp["Xsb"],
                                               p2[:, 0:128], ALU.add)
                    prevv = c.on("v", app_add, waits=[("pt", am)])

                def mqx(t):
                    return t.matmul(p1[:, 0:128], sbp["MqTsb"], sbp["Xsb"],
                                    start=True, stop=True)
                mq = c.on("t", mqx, waits=[("pv", prevv)], inc="pt", chain=False)

                def evq(v):
                    v.tensor_copy(sbp["MQVsb"], p1[:, 0:64])
                    return v.tensor_copy(sbp["MQYsb"], p1[:, 64:128])
                eq = c.on("v", evq, waits=[("pt", mq)])

                def trq(t):
                    return t.transpose(pt[0:64, 512:640], sbp["MQYsb"], sb["IB2"])
                tq = c.on("t", trq, waits=[("pv", eq)], inc="pt", chain=False)

                def evqc(v, q=q):
                    return v.tensor_tensor(sbp["QCsb"], q, pt[0:64, 512:640],
                                           ALU.subtract)
                qc = c.on("v", evqc, waits=[("pt", tq)])

                c.dma("y", c.dram[f"sg_dx{par}"], sbp["Xsb"],
                      waits=[("pv", prevv)] + tw, sem="d1")
                f1 = c.dma("y", sbp["Xfl"].rearrange("t (c n) -> t c n", c=8),
                           c.dram[f"sg_dx{par}"].rearrange("(c t) n -> t c n", c=8),
                           sem="d1")
                c.dma("y", c.dram[f"sg_dr{par}"], sbp["RTP"],
                      waits=[("pv", et)] + tw, sem="d1")
                f2_ = c.dma("y", sbp["RFL"].rearrange("t (c d) -> t c d", c=8),
                            c.dram[f"sg_dr{par}"].rearrange("(c t) d -> t c d", c=8),
                            sem="d1")
                c.dma("y", c.dram[f"sg_dm{par}"], sbp["MQVsb"],
                      waits=[("pv", eq)] + tw, sem="d1")
                f3_ = c.dma("y", sbp["MQVfl"].rearrange("t (c n) -> t c n", c=8),
                            c.dram[f"sg_dm{par}"].rearrange("(c t) n -> t c n", c=8),
                            sem="d1")

                for ch in range(8):
                    def pcry(t, ch=ch):
                        t.matmul(p3[0:64, 0:128], sbp["RFL"][:, ch * 64:(ch + 1) * 64],
                                 sbp["Xfl"][:, ch * 128:(ch + 1) * 128],
                                 start=True, stop=True)
                        return t.matmul(p3[0:64, 128:192],
                                        sbp["Xfl"][:, ch * 128 + 64:(ch + 1) * 128],
                                        sbp["RFL"][:, ch * 64:(ch + 1) * 64],
                                        start=True, stop=True)
                    w = [("d1", f2_), ("d1", f1), ("pv", c.now("pv"))]
                    pm = c.on("t", pcry, waits=w, inc="pt", chain=False)

                    def evpc(v, ch=ch):
                        v.tensor_copy(sbp["PCsb"], p3[0:64, 0:64])
                        return v.tensor_tensor(sbp["ASTsb"], sb["IB"],
                                               p3[0:64, 128:192], ALU.subtract)
                    pe = c.on("v", evpc, waits=[("pt", pm)])

                    def s0scale(s, ch=ch, h=h):
                        return s.activation(sbp["S0sb"],
                                            sb["SST"][:, h * 64:(h + 1) * 64],
                                            AF.Identity,
                                            scale=EMC[:, ch:ch + 1])
                    s0 = c.on("s", s0scale, waits=[("pv", pe)])

                    def omm(t, ch=ch):
                        return t.matmul(p1[0:16, 256:320],
                                        sbp["QCsb"][:, ch * 16:(ch + 1) * 16],
                                        sbp["S0sb"], start=True, stop=True)
                    om = c.on("t", omm, waits=[("ps", s0), ("pv", qc)],
                              inc="pt", chain=False)

                    def oev(v, ch=ch):
                        return v.tensor_tensor(sbp["OFL"][:, ch * 64:(ch + 1) * 64],
                                               p1[0:16, 256:320],
                                               sbp["MQVfl"][:, ch * 64:(ch + 1) * 64],
                                               ALU.add)
                    ovw = [("pt", om), ("d1", f3_)]
                    if ch == 0 and last_odo[par]:
                        ovw.append(("d1", last_odo[par]))
                    ov = c.on("v", oev, waits=ovw)

                    def smm(t, ch=ch):
                        t.matmul(p2[0:64, 256:320], sbp["ASTsb"], sbp["S0sb"],
                                 start=True, stop=False)
                        return t.matmul(p2[0:64, 256:320], sb["IB"], sbp["PCsb"],
                                        start=False, stop=True)
                    sm = c.on("t", smm, waits=[("ps", s0), ("pv", pe)],
                              inc="pt", chain=False)

                    def snew(s, ch=ch, h=h):
                        return s.activation(sb["SST"][:, h * 64:(h + 1) * 64],
                                            p2[0:64, 256:320], AF.Identity,
                                            scale=ERC[:, ch:ch + 1])
                    c.on("s", snew, waits=[("pt", sm)])

                odo = c.dma("y", c.dram[f"sg_do{par}"], sbp["OFL"],
                            waits=[("pv", c.now("pv")), ("ps", c.now("ps"))],
                            sem="d1")
                od = None
                for ch in range(8):
                    od = c.dma("g", sbp["OTP"][ch * 16:(ch + 1) * 16, :],
                               c.dram[f"sg_do{par}"][:, ch * 64:(ch + 1) * 64],
                               waits=(([("d1", odo)] + tw) if ch == 0 else []),
                               sem="d1")

                def hsq(s):
                    return s.activation(sbp["SQ"], sbp["OTP"], AF.Square)
                h1 = c.on("s", hsq, waits=[("d1", od)])

                def hred(v):
                    v.tensor_reduce(sbp["RS"], sbp["SQ"], AX.X, ALU.add)
                    return v.tensor_scalar(sbp["RS"], sbp["RS"], 1.0 / 64,
                                           L2EPS, ALU.mult, ALU.add)
                h2 = c.on("v", hred, waits=[("ps", h1)])

                def hrs(s):
                    s.activation(sbp["RS"], sbp["RS"], AF.Ln)
                    return s.activation(sbp["RS"], sbp["RS"], AF.Exp, scale=-0.5)
                h3 = c.on("s", hrs, waits=[("pv", h2)])

                def hmul(v):
                    return v.scalar_tensor_tensor(sbp["OTP"], sbp["OTP"],
                                                  sbp["RS"], sbp["OTP"],
                                                  ALU.mult, ALU.bypass)
                h4 = c.on("v", hmul, waits=[("ps", h3)])

                def otr(t):
                    return t.transpose(pt[0:64, 768:896], sbp["OTP"], sb["IB2"])
                h5 = c.on("t", otr, waits=[("pv", h4)], inc="pt", chain=False)

                def ofin(v, h=h):
                    dst = sb["OPAIR"][(h % 2) * 64:(h % 2) * 64 + 64,
                                      (h // 2) * 128:(h // 2) * 128 + 128]
                    return v.scalar_tensor_tensor(dst, pt[0:64, 768:896],
                                                  sb["HNW"][:, h:h + 1],
                                                  sbp["QCsb"], ALU.mult,
                                                  ALU.bypass)
                c.on("v", ofin, waits=[("pt", h5)])
                tile_done[par] = {"pt": c.now("pt"), "pv": c.now("pv"),
                                  "ps": c.now("ps")}
                last_odo[par] = odo
            c.join()

            def yout(g):
                g.dma_start(out=dram["sg_sout"], in_=sb["SST"]).then_inc(c.sem["d9"], 16)
                g.dma_start(out=dram["sg_o"], in_=sb["OPAIR"]).then_inc(c.sem["d9"], 16)
                g.wait_ge(c.sem["d9"], 32)
            c.block.gpsimd(yout)
            c.inc("d9", 32)
    return nc


def emit_l2apost(c, scale):
    XH, SB2, SB3, AUX, TRASH, TRASH2, WT = (
        c.sb["XH"], c.sb["SB2"], c.sb["SB3"], c.sb["AUX"], c.sb["TRASH"],
        c.sb["TRASH2"], c.sb["WT"])
    ax = c.dma("y", AUX[:, 0:32], c.dram["l2a_aux2"], sem="d7")
    cast_done = 0
    for wn, scr in (("l2a_f1", "l2a_f1s"), ("l2a_f3", "l2a_f3s"),
                    ("l2a_f2", "l2a_f2s")):
        for piece in range(8):
            lv = c.dma("g", WT[:, (piece % 2) * 4096:(piece % 2) * 4096 + 4096],
                       c.dram[wn][:, piece * 4096:(piece + 1) * 4096],
                       waits=[("d6", cast_done)] if cast_done else [], sem="d5")
            cast_done = c.dma("y", c.dram[scr][:, piece * 4096:(piece + 1) * 4096],
                              WT[:, (piece % 2) * 4096:(piece % 2) * 4096 + 4096],
                              waits=[("d5", lv)], sem="d6")
    ol = c.dma("y", SB2[:, :], c.dram["l2a_o"], sem="d3")
    u1 = c.dma("g", TRASH[:, 0:512], c.dram["l2a_u1"], sem="d4")
    u2 = c.dma("g", SB3[0:64, 0:1024], c.dram["l2a_u2"], sem="d4")
    g1ev = None
    for tt in range(NTT):
        def mm(t, tt=tt):
            l = None
            for kc in range(DC):
                l = t.matmul(c.sb["PSB"][0:64, 0:512],
                             TRASH[:, kc * 64:(kc + 1) * 64],
                             xh_rhs(c, kc, tt), start=(kc == 0),
                             stop=(kc == DC - 1))
            return l
        w = [("d4", u1), ("d7", ax)]
        if g1ev:
            w.append(("ps", g1ev))
        mmd = c.on("t", mm, waits=w, inc="pt", chain=False)

        def ev(s, tt=tt):
            return s.activation(SB3[0:64, 2048 + tt * TT:2048 + (tt + 1) * TT],
                                c.sb["PSB"][0:64, 0:512], AF.Silu,
                                bias=AUX[0:64, LB_U1B:LB_U1B + 1])
        g1ev = c.on("s", ev, waits=[("pt", mmd)])
    c.join()
    pp = PsumPool(c, c.sb["PSA"], 4, 512)
    ow0 = c.dma("g", WT[:, 0:4096], c.dram["l2a_ow"][:, 0:4096],
                waits=[("pt", c.now("pt"))], sem="d4")
    ow1 = c.dma("g", WT[:, 4096:8192], c.dram["l2a_ow"][:, 4096:8192],
                sem="d4")
    for oc in range(8):
        for tt in range(NTT):
            ps = pp.bank()
            war = pp.war_wait()
            w = [("d4", ow1 if oc >= 4 else ow0), ("d3", ol)] + ([war] if war else [])

            def mm(t, oc=oc, tt=tt, ps=ps):
                l = None
                for kc in range(DC):
                    l = t.matmul(ps, WT[:, oc * 1024 + kc * 128:oc * 1024 + (kc + 1) * 128],
                                 SB2[:, kc * T + tt * TT:kc * T + (tt + 1) * TT],
                                 start=(kc == 0), stop=(kc == DC - 1))
                return l
            mmd = c.on("t", mm, waits=w, inc="pt", chain=False)

            def gmm(t, oc=oc, tt=tt):
                return t.matmul(c.sb["PSB"][:, 0:512], SB3[0:64, oc * 128:(oc + 1) * 128],
                                SB3[0:64, 2048 + tt * TT:2048 + (tt + 1) * TT],
                                start=True, stop=True)
            gw = [("d4", u2), ("pv", c.now("pv"))]
            gmd = c.on("t", gmm, waits=gw, inc="pt", chain=False)

            def gsig(s, oc=oc, tt=tt):
                return s.activation(TRASH2[:, 0:512], c.sb["PSB"][:, 0:512],
                                    AF.Sigmoid, bias=AUX[:, LB_U2B + oc:LB_U2B + oc + 1])
            gsd = c.on("s", gsig, waits=[("pt", gmd)])

            def yev(v, oc=oc, tt=tt, ps=ps):
                xsl = XH[:, oc * T + tt * TT:oc * T + (tt + 1) * TT]
                v.tensor_tensor(TRASH2[:, 512:1024], ps, TRASH2[:, 0:512], ALU.mult)
                return v.tensor_tensor(xsl, xsl, TRASH2[:, 512:1024], ALU.add)
            val = c.on("v", yev, waits=[("pt", mmd), ("ps", gsd)])
            pp.push(val)
    c.join()
    def resy(v):
        l = None
        for kc in range(DC):
            l = v.tensor_tensor(c.sb["RES"][:, kc * T:(kc + 1) * T],
                                c.sb["RES"][:, kc * T:(kc + 1) * T],
                                XH[:, kc * T:(kc + 1) * T], ALU.add)
        return l
    c.on("v", resy)
    emit_rms(c, lambda kc: AUX[:, LB_FFW + kc:LB_FFW + kc + 1], 0, "SB3",
             eps=L2EPS)
    c.join()
    dmabar(c, "d5", "d6")

    def xn_rhs(kc, tt):
        return SB3[:, kc * T + tt * TT:kc * T + (tt + 1) * TT]
    for qu in range(NTT):
        for piece in range(8):
            w = [("pt", c.now("pt"))]
            d1 = c.dma("g", WT[:, 0:4096], c.dram["l2a_f1s"][:, piece * 4096:(piece + 1) * 4096],
                       waits=w, sem="d4")
            d3 = c.dma("g", WT[:, 4096:8192], c.dram["l2a_f3s"][:, piece * 4096:(piece + 1) * 4096],
                       sem="d4")
            for hcl in range(4):
                hc = piece * 4 + hcl

                def mm1(t, hcl=hcl, qu=qu):
                    l = None
                    for kc in range(DC):
                        l = t.matmul(c.sb["PSB"][:, 0:512],
                                     WT[:, hcl * 1024 + kc * 128:hcl * 1024 + (kc + 1) * 128],
                                     xn_rhs(kc, qu), start=(kc == 0),
                                     stop=(kc == DC - 1))
                    return l
                w1 = [("d4", d1), ("ps", c.now("ps"))]
                m1 = c.on("t", mm1, waits=w1, inc="pt", chain=False)

                def ev1(s):
                    return s.activation(TRASH2[:, 0:512], c.sb["PSB"][:, 0:512],
                                        AF.Silu)
                e1 = c.on("s", ev1, waits=[("pt", m1)])

                def mm3(t, hcl=hcl, qu=qu):
                    l = None
                    for kc in range(DC):
                        l = t.matmul(c.sb["PSB"][:, 512:1024],
                                     WT[:, 4096 + hcl * 1024 + kc * 128:4096 + hcl * 1024 + (kc + 1) * 128],
                                     xn_rhs(kc, qu), start=(kc == 0),
                                     stop=(kc == DC - 1))
                    return l
                m3 = c.on("t", mm3, waits=[("d4", d3), ("ps", e1),
                                           ("pv", c.now("pv"))], inc="pt",
                          chain=False)

                def ev3(v, hc=hc):
                    return v.tensor_tensor(SB2[:, hc * 512:(hc + 1) * 512],
                                           TRASH2[:, 0:512],
                                           c.sb["PSB"][:, 512:1024], ALU.mult)
                c.on("v", ev3, waits=[("pt", m3)])
        c.join()
        for wave in range(2):
            down_mm = None
            for piece in range(8):
                w = [("pt", c.now("pt"))]
                d2 = c.dma("g", WT[:, 0:4096],
                           c.dram["l2a_f2s"][:, piece * 4096:(piece + 1) * 4096],
                           waits=w, sem="d4")
                for hcl in range(4):
                    hc = piece * 4 + hcl

                    def mmd_(t, piece=piece, hcl=hcl, hc=hc, wave=wave):
                        l = None
                        for ocl in range(4):
                            oc = wave * 4 + ocl
                            l = t.matmul(c.sb["PSA"][:, ocl * 512:(ocl + 1) * 512],
                                         WT[:, hcl * 1024 + oc * 128:hcl * 1024 + oc * 128 + 128],
                                         SB2[:, hc * 512:(hc + 1) * 512],
                                         start=(piece == 0 and hcl == 0),
                                         stop=(piece == 7 and hcl == 3))
                        return l
                    down_mm = c.on("t", mmd_, waits=[("d4", d2)], inc="pt",
                                   chain=False)

            def dev(v, qu=qu, wave=wave):
                l = None
                for ocl in range(4):
                    oc = wave * 4 + ocl
                    rsl = c.sb["RES"][:, oc * T + qu * TT:oc * T + (qu + 1) * TT]
                    l = v.tensor_tensor(rsl, rsl,
                                        c.sb["PSA"][:, ocl * 512:(ocl + 1) * 512],
                                        ALU.add)
                return l
            c.on("v", dev, waits=[("pt", down_mm)])
            c.join()


VB_VECS = ([("L3_ffn_bias", 8)] +
           [(f"L0_ffn_b{l}_{e}", 8) for e in range(8) for l in (1, 2, 3)] +
           [("L1_ffn_bq", 8), ("L1_ffn_bk", 8), ("L1_ffn_bv", 8),
            ("L1_ffn_bo", 8), ("L1_ffn_g1", 8), ("L1_ffn_be1", 8),
            ("L1_ffn_g2", 8), ("L1_ffn_be2", 8)] +
           [("L0_attn_bq", 8), ("L0_attn_bk", 8), ("L0_attn_bv", 8),
            ("L0_attn_bo", 8), ("L0_attn_b1", 32), ("L0_attn_b2", 8),
            ("L0_attn_ln1_w", 8), ("L0_attn_ln1_b", 8),
            ("L0_attn_ln2_w", 8), ("L0_attn_ln2_b", 8)])


def vb_fetch(inputs, nm):
    if nm.startswith("L0_ffn_b"):
        l, e = nm[8], nm[10:]
        return inputs[f"L0_ffn_b{l}"][int(e)]
    return inputs[nm]


def mamba_host_prep(W_in, W_xproj, W_dt, W_out, A_log, b_dt, Dp, conv_w,
                    conv_b, norm_w):
    out = {}
    arr = W_in[:DI].reshape(2, 2, 4, 128, 8, 128)
    out["win"] = np.ascontiguousarray(arr.transpose(0, 1, 5, 4, 2, 3)
                                      .reshape(4, 128, 4096))
    if W_in.shape[0] > DI:
        arr = W_in[DI:].reshape(2, 2, 4, 128, 8, 128)
        out["wres"] = np.ascontiguousarray(arr.transpose(0, 1, 5, 4, 2, 3)
                                           .reshape(4, 128, 4096))
    arr = W_xproj.reshape(96, 2, 8, 128)
    out["wxp"] = np.ascontiguousarray(arr.transpose(1, 3, 2, 0).reshape(2, 128, 768))
    out["wdt"] = np.ascontiguousarray(W_dt.T)
    arr = W_out.reshape(2, 4, 128, 2, 8, 128)
    out["wout"] = np.ascontiguousarray(arr.transpose(3, 0, 5, 4, 1, 2)
                                       .reshape(4, 128, 4096))
    aux = np.zeros((128, MA_NEGB), np.float32)
    arr = A_log.reshape(16, 16, 8, 16)
    aux[:, MA_ALOG:MA_ALOG + 256] = arr.transpose(3, 2, 0, 1).reshape(128, 256)
    aux[:, MA_BDT:MA_BDT + 16] = b_dt.reshape(16, 128).T
    aux[:, MA_D:MA_D + 16] = Dp.reshape(16, 128).T
    aux[:, MA_CW:MA_CW + 64] = conv_w.reshape(16, 128, 4).transpose(1, 0, 2).reshape(128, 64)
    aux[:, MA_CB:MA_CB + 16] = conv_b.reshape(16, 128).T
    if norm_w is not None:
        aux[:, MA_NW:MA_NW + 8] = norm_w.reshape(8, 128).T
    out["aux"] = aux
    return out


def repsel_host():
    rs = np.zeros((128, 2304), np.float32)
    for j in range(16):
        for m in range(128):
            rs[j * 8 + m % 8, j * 128 + m] = 1.0
    for p in range(128):
        rs[p, 2048 + 128 + p % 8] = 1.0
    return rs.astype(ml_dtypes.bfloat16)


def vb_offsets():
    off, o = {}, 0
    for nm, w in VB_VECS:
        off[nm] = o
        o += w
    return off, o


def prep_inputs(inputs, core):
    d = {}
    d["x_in"] = np.ascontiguousarray(inputs["L0_attn_x"][core].T)
    d["L3_ffn_kernel_T"] = np.ascontiguousarray(inputs["L3_ffn_kernel"])
    d["L3_ffn_a_T"] = np.ascontiguousarray(inputs["L3_ffn_a"])
    d["L3_ffn_b_T"] = np.ascontiguousarray(inputs["L3_ffn_b"])
    vecs = [np.ascontiguousarray(vb_fetch(inputs, nm).reshape(w, 128).T)
            for nm, w in VB_VECS]
    d["vecbank"] = np.concatenate(vecs, axis=1).astype(np.float32)
    d["ident"] = np.eye(128, dtype=np.float32)
    d["L0_ffn_gate_w"] = np.ascontiguousarray(inputs["L0_ffn_gate_w"])
    d["L0_gate_b8"] = np.ascontiguousarray(inputs["L0_ffn_gate_b"].reshape(8, 1))
    for l in (1, 2, 3):
        d[f"L0_ffn_w{l}"] = np.ascontiguousarray(inputs[f"L0_ffn_w{l}"])
    es = np.zeros((8, 8 * 128), np.float32)
    for e in range(8):
        es[e, e * 128:(e + 1) * 128] = 1.0
    d["esel"] = es.astype(ml_dtypes.bfloat16)
    for wn in ("Wq", "Wk", "Wv", "Wo", "W1", "W2"):
        d[f"L0_attn_{wn}"] = np.ascontiguousarray(inputs[f"L0_attn_{wn}"])
    jl = np.arange(128)[:, None]
    il = np.arange(128)[None, :]
    mleft = np.where(jl >= il, 0.0, -1e30).astype(np.float32)
    mdiag = np.where(jl <= il, 0.0, -1e30).astype(np.float32)
    d["maskT"] = np.concatenate([mleft, mdiag], axis=1).astype(ml_dtypes.bfloat16)
    d["identb"] = np.eye(64).astype(ml_dtypes.bfloat16)
    d["identb128"] = np.eye(128).astype(ml_dtypes.bfloat16)
    for wn in ("w_gate", "w_up", "w_down"):
        d[f"L2_ffn_{wn}"] = np.ascontiguousarray(inputs[f"L2_ffn_{wn}"])
    for wn in ("Wq", "Wk", "Wv", "Wo", "Wg", "We", "bexp"):
        d[f"L1_ffn_{wn}"] = np.ascontiguousarray(inputs[f"L1_ffn_{wn}"])
    d["L1_gb8"] = np.ascontiguousarray(inputs["L1_ffn_bg"].reshape(8, 1))
    m1 = mamba_host_prep(inputs["L1_attn_W_in"], inputs["L1_attn_W_xproj"],
                         inputs["L1_attn_W_dt"], inputs["L1_attn_W_out"],
                         inputs["L1_attn_A_log"], inputs["L1_attn_b_dt"],
                         inputs["L1_attn_D"], inputs["L1_attn_conv_w"],
                         inputs["L1_attn_conv_b"], inputs["L1_attn_norm_w"])
    for k, v in m1.items():
        d[f"l1a_{k}"] = v
    d["repsel"] = repsel_host()
    g = inputs
    mf = mamba_host_prep(g["L3_attn_in_proj_w"], g["L3_attn_xproj_w_f"],
                         g["L3_attn_dtproj_w_f"], g["L3_attn_out_proj_w"],
                         g["L3_attn_A_log_f"], g["L3_attn_dtproj_b_f"],
                         g["L3_attn_D_f"], g["L3_attn_conv_w_f"],
                         g["L3_attn_conv_b_f"], g["L3_attn_norm_w"])
    mb = mamba_host_prep(g["L3_attn_in_proj_w"], g["L3_attn_xproj_w_b"],
                         g["L3_attn_dtproj_w_b"], g["L3_attn_out_proj_w"],
                         g["L3_attn_A_log_b"], g["L3_attn_dtproj_b_b"],
                         g["L3_attn_D_b"], g["L3_attn_conv_w_b"],
                         g["L3_attn_conv_b_b"], None)
    d["l3a_wipx"], d["l3a_wipz"], d["l3a_wop"] = mf["win"], mf["wres"], mf["wout"]
    d["l3a_wxp_f"], d["l3a_wdt_f"], d["l3a_aux_f"] = mf["wxp"], mf["wdt"], mf["aux"]
    d["l3a_wxp_b"], d["l3a_wdt_b"], d["l3a_aux_b"] = mb["wxp"], mb["wdt"], mb["aux"]
    ax1 = np.zeros((128, 72), np.float32)
    ax1[:, 0:32] = g["L2_attn_conv_w"].reshape(8, 128, 4).transpose(1, 0, 2).reshape(128, 32)
    ax1[:, 32:40] = g["L2_attn_conv_b"].reshape(8, 128).T
    ax1[:, 40:48] = g["L2_attn_norm_in_w"].reshape(8, 128).T
    ax1[:, 48:56] = g["L2_attn_alpha_b"].reshape(8, 128).T
    ax1[0:16, 64] = g["L2_attn_beta_b"]
    d["l2a_aux1"] = ax1
    ax2 = np.zeros((128, 32), np.float32)
    ax2[0:64, 0] = g["L2_attn_u1_b"]
    ax2[:, 1:9] = g["L2_attn_u2_b"].reshape(8, 128).T
    ax2[:, 9:17] = g["L2_attn_ff_norm_w"].reshape(8, 128).T
    d["l2a_aux2"] = ax2

    def lh8(w):
        return np.ascontiguousarray(w.reshape(8, 128, 8, 128)
                                    .transpose(1, 2, 0, 3).reshape(128, 8192))
    d["l2a_qw"] = lh8(g["L2_attn_q_w"])
    d["l2a_kw"] = lh8(g["L2_attn_k_w"])
    d["l2a_vw"] = lh8(g["L2_attn_v_w"])
    d["l2a_aw"] = lh8(g["L2_attn_alpha_w"])
    d["l2a_ow"] = lh8(g["L2_attn_out_w"])
    d["l2a_bw"] = np.ascontiguousarray(g["L2_attn_beta_w"].reshape(8, 128, 16)
                                       .transpose(1, 0, 2).reshape(128, 128))
    d["l2a_u1"] = np.ascontiguousarray(g["L2_attn_u1_w"].reshape(8, 128, 64)
                                       .transpose(1, 0, 2).reshape(128, 512))
    d["l2a_u2"] = np.ascontiguousarray(g["L2_attn_u2_w"])
    for nm, w in (("l2a_f1", g["L2_attn_ff_w1"]), ("l2a_f3", g["L2_attn_ff_w3"])):
        d[nm] = np.ascontiguousarray(w.reshape(8, 128, 8, 4, 128)
                                     .transpose(1, 2, 3, 0, 4).reshape(128, 32768))
    d["l2a_f2"] = np.ascontiguousarray(g["L2_attn_ff_w2"].reshape(8, 4, 128, 8, 128)
                                       .transpose(2, 0, 1, 3, 4).reshape(128, 32768))
    d["l2a_o"] = np.zeros((128, 16384), ml_dtypes.bfloat16)
    return d


def seg_consts(inputs):
    mk = np.zeros((128, 256), np.float32)
    for c_ in range(8):
        for a in range(16):
            for b in range(16):
                if b < a:
                    mk[c_ * 16 + a, c_ * 16 + b] = 1.0
                if a <= b:
                    mk[c_ * 16 + a, 128 + c_ * 16 + b] = 1.0
    return {
        "sg_masks": mk.astype(ml_dtypes.bfloat16),
        "sg_ib": np.eye(64).astype(ml_dtypes.bfloat16),
        "sg_ib2": np.eye(128).astype(ml_dtypes.bfloat16),
        "sg_hnw": np.ascontiguousarray(
            inputs["L2_attn_head_norm_w"].reshape(16, 64).T.astype(np.float32)),
    }


EMITTERS = {"l3f": emit_l3f, "l0f": emit_l0f, "l0a": emit_l0a, "l2f": emit_l2f,
            "l1f": emit_l1f, "l1a": emit_l1a, "l3a": emit_l3a,
            "l2apre": emit_l2apre, "l2apost": emit_l2apost}
NO_OUTER_LN = {"l2f"}

SUB = [("l3f", 0.5)]
SUBM1 = [("l0f", 0.5), ("l0a", 1.0), ("l0f", 0.5),
         ("l1f", 0.5), ("l1a", 1.0), ("l1f", 0.5),
         ("l2f", 0.5), ("l2apre", 1.0)]
SUBM2 = [("l2apost", 1.0), ("l2f", 0.5),
         ("l3f", 0.5), ("l3a", 1.0), ("l3f", 0.5)]
DEBUG_DUMPS = []


def build_program(sublayers):
    nc = bass.Bass("TRN2", target_bir_lowering=False, detect_race_conditions=DETECT_RACES)
    stack = ExitStack()
    voff, vlen = vb_offsets()

    dram = {}
    dram["x_in"] = nc.dram_tensor("x_in", [D, T], F32, kind="ExternalInput").ap()
    dram["x_out"] = nc.dram_tensor("x_out", [D, T], F32, kind="ExternalOutput").ap()
    dram["vecbank"] = nc.dram_tensor("vecbank", [128, max(vlen, 1)], F32, kind="ExternalInput").ap()
    dram["ident"] = nc.dram_tensor("ident", [128, 128], F32, kind="ExternalInput").ap()
    dram["L3_ffn_kernel_T"] = nc.dram_tensor("L3_ffn_kernel_T", [1024, 1024], F32, kind="ExternalInput").ap()
    dram["L3_ffn_a_T"] = nc.dram_tensor("L3_ffn_a_T", [1024, 16], F32, kind="ExternalInput").ap()
    dram["L3_ffn_b_T"] = nc.dram_tensor("L3_ffn_b_T", [16, 1024], F32, kind="ExternalInput").ap()
    dram["L0_ffn_gate_w"] = nc.dram_tensor("L0_ffn_gate_w", [1024, 8], F32, kind="ExternalInput").ap()
    dram["L0_gate_b8"] = nc.dram_tensor("L0_gate_b8", [8, 1], F32, kind="ExternalInput").ap()
    for l in (1, 2, 3):
        dram[f"L0_ffn_w{l}"] = nc.dram_tensor(f"L0_ffn_w{l}", [8, 1024, 1024], F32, kind="ExternalInput").ap()
    dram["esel"] = nc.dram_tensor("esel", [8, 8 * 128], BF16, kind="ExternalInput").ap()
    for wn, shp in (("Wq", [1024, 1024]), ("Wk", [1024, 1024]), ("Wv", [1024, 1024]),
                    ("Wo", [1024, 1024]), ("W1", [1024, 4096]), ("W2", [4096, 1024])):
        dram[f"L0_attn_{wn}"] = nc.dram_tensor(f"L0_attn_{wn}", shp, F32, kind="ExternalInput").ap()
    dram["maskT"] = nc.dram_tensor("maskT", [128, 256], BF16, kind="ExternalInput").ap()
    dram["identb"] = nc.dram_tensor("identb", [64, 64], BF16, kind="ExternalInput").ap()
    dram["identb128"] = nc.dram_tensor("identb128", [128, 128], BF16, kind="ExternalInput").ap()
    for wn, shp in (("w_gate", [2816, 1024]), ("w_up", [2816, 1024]), ("w_down", [1024, 2816])):
        dram[f"L2_ffn_{wn}"] = nc.dram_tensor(f"L2_ffn_{wn}", shp, F32, kind="ExternalInput").ap()
    dram["l2f_wg_scr"] = nc.dram_tensor("l2f_wg_scr", [1024, 2816], BF16, kind="Internal").ap()
    dram["l2f_wu_scr"] = nc.dram_tensor("l2f_wu_scr", [1024, 2816], BF16, kind="Internal").ap()
    dram["l2f_wd_scr"] = nc.dram_tensor("l2f_wd_scr", [2816, 1024], BF16, kind="Internal").ap()
    for wn, shp in (("Wq", [1024, 1024]), ("Wk", [1024, 1024]), ("Wv", [1024, 1024]),
                    ("Wo", [1024, 1024]), ("Wg", [1024, 8]), ("We", [8, 1024, 1024]),
                    ("bexp", [8, 1024])):
        dram[f"L1_ffn_{wn}"] = nc.dram_tensor(f"L1_ffn_{wn}", shp, F32, kind="ExternalInput").ap()
    dram["L1_gb8"] = nc.dram_tensor("L1_gb8", [8, 1], F32, kind="ExternalInput").ap()
    dram["v_loc"] = nc.dram_tensor("v_loc", [64, 128], F32, kind="Internal").ap()
    dram["v_red"] = nc.dram_tensor("v_red", [64, 128], F32, kind="Internal").ap()
    for pre, shp in (("l1a_win", [4, 128, 4096]), ("l1a_wres", [4, 128, 4096]),
                     ("l1a_wxp", [2, 128, 768]), ("l1a_wdt", [64, 2048]),
                     ("l1a_wout", [4, 128, 4096]), ("l1a_aux", [128, MA_NEGB])):
        dram[pre] = nc.dram_tensor(pre, shp, F32, kind="ExternalInput").ap()
    dram["repsel"] = nc.dram_tensor("repsel", [128, 2304], BF16, kind="ExternalInput").ap()
    dram["l1a_xsA"] = nc.dram_tensor("l1a_xsA", [128, 16384], BF16, kind="Internal").ap()
    dram["l1a_res"] = nc.dram_tensor("l1a_res", [128, 32768], BF16, kind="Internal").ap()
    dram["l1a_bc"] = nc.dram_tensor("l1a_bc", [32, 2048], F32, kind="Internal").ap()
    for pre, shp in (("l3a_wipx", [4, 128, 4096]), ("l3a_wipz", [4, 128, 4096]),
                     ("l3a_wop", [4, 128, 4096]),
                     ("l3a_wxp_f", [2, 128, 768]), ("l3a_wxp_b", [2, 128, 768]),
                     ("l3a_wdt_f", [64, 2048]), ("l3a_wdt_b", [64, 2048]),
                     ("l3a_aux_f", [128, MA_NEGB]), ("l3a_aux_b", [128, MA_NEGB])):
        dram[pre] = nc.dram_tensor(pre, shp, F32, kind="ExternalInput").ap()
    for pre in ("l3a_xi", "l3a_z", "l3a_yb"):
        dram[pre] = nc.dram_tensor(pre, [128, 32768], BF16, kind="Internal").ap()
    for pre, shp in (("l2a_aux1", [128, 72]), ("l2a_aux2", [128, 32]),
                     ("l2a_qw", [128, 8192]), ("l2a_kw", [128, 8192]),
                     ("l2a_vw", [128, 8192]), ("l2a_aw", [128, 8192]),
                     ("l2a_bw", [128, 128]), ("l2a_ow", [128, 8192]),
                     ("l2a_u1", [128, 512]), ("l2a_u2", [64, 1024]),
                     ("l2a_f1", [128, 32768]), ("l2a_f3", [128, 32768]),
                     ("l2a_f2", [128, 32768])):
        dram[pre] = nc.dram_tensor(pre, shp, F32, kind="ExternalInput").ap()
    dram["l2a_o"] = nc.dram_tensor("l2a_o", [128, 16384], BF16, kind="ExternalInput").ap()
    for pre in ("l2a_q", "l2a_k", "l2a_v", "l2a_sp"):
        dram[pre] = nc.dram_tensor(pre, [64, 32768], BF16, kind="ExternalOutput").ap()
    dram["l2a_beta"] = nc.dram_tensor("l2a_beta", [16, 2048], F32, kind="ExternalOutput").ap()
    for pre in ("l2a_f1s", "l2a_f3s", "l2a_f2s"):
        dram[pre] = nc.dram_tensor(pre, [128, 32768], BF16, kind="Internal").ap()

    with stack:
        sb = {}
        def alloc(name, shape, dt):
            sb[name] = stack.enter_context(nc.sbuf_tensor(name, shape, dt)).ap()
        alloc("RES", [128, DC * T], F32)
        alloc("XH", [128, DC * T], BF16)
        alloc("SB2", [128, 16384], BF16)
        alloc("SB3", [128, 16384], BF16)
        alloc("WT", [128, 8192], BF16)
        alloc("VEC", [128, 2176], F32)
        sb["SCR"] = sb["SB3"].bitcast(F32)
        alloc("VB", [128, max(vlen, 1)], F32)
        alloc("ONECOL", [128, 1], BF16)
        alloc("ONECOLF", [128, 1], F32)
        alloc("TRASH", [128, 1152], BF16)
        alloc("TRASH2", [128, 2048], BF16)
        alloc("AUX", [128, 512], F32)
        alloc("VS", [128, 68], F32)
        alloc("QS", [64, 2048], BF16)
        alloc("KS", [64, 2304], BF16)
        alloc("VSTG", [64, 1024], BF16)
        alloc("PBUF", [128, 512], BF16)
        sb["MASKT"] = sb["PBUF"][:, 256:512]
        alloc("IDENTB", [64, 64], BF16)
        alloc("IDENTB2", [128, 128], BF16)
        alloc("ONESF", [64, 128], F32)
        alloc("IDENTF", [128, 128], F32)
        psum_al = nc.psum_tensor("PSA", [128, 2048], F32)
        psb_al = nc.psum_tensor("PSB", [128, 1024], F32)
        pst_al = nc.psum_tensor("PST", [128, 2048], BF16)
        sb["PSA"] = stack.enter_context(psum_al).ap()
        sb["PSB"] = stack.enter_context(psb_al).ap()
        sb["PST"] = stack.enter_context(pst_al).ap()

        with nc.Block() as block:
            c = Ctx(nc, block, stack)
            c.dram = dram
            c.sb = sb
            c.vb_off = voff

            def ginit(g):
                g.memset(sb["ONECOL"][:, :], 1.0)
                g.memset(sb["ONECOLF"][:, :], 1.0)
                g.memset(sb["VEC"][:, :], 0.0)
                g.memset(sb["TRASH"][:, :], 0.0)
                g.memset(sb["TRASH2"][:, :], 0.0)
                return g.memset(sb["ONESF"][:, :], 1.0)
            ginit_done = c.on("g", ginit, inc="pg")

            def yinit(y):
                y.dma_start(out=sb["RES"][:, :].rearrange("p (c t) -> p c t", c=DC),
                            in_=dram["x_in"].rearrange("(c p) t -> p c t", p=128)).then_inc(c.sem["di"], 16)
                y.dma_start(out=sb["VB"][:, :], in_=dram["vecbank"]).then_inc(c.sem["di"], 16)
                y.dma_start(out=sb["IDENTF"][:, :], in_=dram["ident"]).then_inc(c.sem["di"], 16)
                y.dma_start(out=sb["MASKT"][:, :], in_=dram["maskT"]).then_inc(c.sem["di"], 16)
                y.dma_start(out=sb["IDENTB"][:, :], in_=dram["identb"]).then_inc(c.sem["di"], 16)
                y.dma_start(out=sb["IDENTB2"][:, :], in_=dram["identb128"]).then_inc(c.sem["di"], 16)
            c.block.sync(yinit)
            x_loaded = c.inc("di", 96)

            for key, pkey in (("t", "pt"), ("v", "pv"), ("s", "ps"), ("g", "pg")):
                def wall(e, pkey=pkey):
                    e.wait_ge(c.sem["di"], x_loaded)
                    if pkey != "pg":
                        e.wait_ge(c.sem["pg"], ginit_done)
                c._sec(key)(wall)

            for name, scale in sublayers:
                if name not in NO_OUTER_LN:
                    emit_ln(c)
                    c.join()
                EMITTERS[name](c, scale)
                c.join()

            for dname, skey, cols, ddt in DEBUG_DUMPS:
                if skey.startswith("dram:"):
                    srckey = skey[5:]
                    shp = list(dram[srckey].shape)
                    dram[dname] = nc.dram_tensor(dname, shp, ddt, kind="ExternalOutput").ap()
                    def dumpf(g, dname=dname, srckey=srckey):
                        g.dma_start(out=dram[dname], in_=dram[srckey]).then_inc(c.sem["d8"], 16)
                else:
                    dram[dname] = nc.dram_tensor(dname, [128, cols], ddt, kind="ExternalOutput").ap()
                    def dumpf(g, dname=dname, skey=skey, cols=cols):
                        g.dma_start(out=dram[dname], in_=sb[skey][:, 0:cols]).then_inc(c.sem["d8"], 16)
                c.block.gpsimd(dumpf)
                c.inc("d8", 16)
            if DEBUG_DUMPS:
                c.block.gpsimd(lambda g: g.wait_ge(c.sem["d8"], c.now("d8")))

            def yout(g):
                g.dma_start(out=dram["x_out"].rearrange("(c p) t -> p c t", p=128),
                            in_=sb["RES"][:, :].rearrange("p (c t) -> p c t", c=DC)).then_inc(c.sem["d9"], 16)
                g.wait_ge(c.sem["d9"], c.now("d9") + 16)
            c.block.gpsimd(yout)
            c.inc("d9", 16)
    return nc


def kernel(**inputs):
    nc1 = build_program(SUBM1)
    nc2 = build_program(SUBM2)
    ncs = build_l2a_seg()
    base = [prep_inputs(inputs, core) for core in range(8)]
    cons = seg_consts(inputs)
    res1 = run_bass_kernel_spmd(nc1, base, core_ids=list(range(8)))
    S = [np.zeros((64, 1024), np.float32) for _ in range(8)]
    segs_o = [[] for _ in range(8)]
    for i in range(16):
        in_maps = []
        for core in range(8):
            r = res1.results[core]
            m = dict(cons)
            for nm, key in (("sg_q", "l2a_q"), ("sg_k", "l2a_k"),
                            ("sg_v", "l2a_v"), ("sg_sp", "l2a_sp")):
                m[nm] = np.ascontiguousarray(
                    r[key].reshape(64, 16, 2048)[:, :, i * 128:(i + 1) * 128]
                    .reshape(64, 2048))
            m["sg_beta"] = np.ascontiguousarray(
                r["l2a_beta"][:, i * 128:(i + 1) * 128].T)
            m["sg_sin"] = S[core]
            in_maps.append(m)
        rs = run_bass_kernel_spmd(ncs, in_maps, core_ids=list(range(8)))
        for core in range(8):
            S[core] = np.ascontiguousarray(rs.results[core]["sg_sout"])
            segs_o[core].append(rs.results[core]["sg_o"])
    in2 = []
    for core in range(8):
        O = np.stack(segs_o[core], 0).reshape(16, 128, 8, 128)
        O = np.ascontiguousarray(O.transpose(1, 2, 0, 3).reshape(128, 16384))
        d2 = dict(base[core])
        d2["x_in"] = np.ascontiguousarray(res1.results[core]["x_out"])
        d2["l2a_o"] = O
        in2.append(d2)
    res2 = run_bass_kernel_spmd(nc2, in2, core_ids=list(range(8)))
    outs = [np.ascontiguousarray(r["x_out"].T) for r in res2.results]
    return np.stack(outs, 0).astype(np.float32)
